# version 01
# baseline (speedup 1.0000x reference)
import numpy as np
import ml_dtypes
import concourse.bass as bass
import concourse.mybir as mybir
from concourse.bass_utils import run_bass_kernel_spmd
from contextlib import ExitStack

F32 = mybir.dt.float32
BF16 = mybir.dt.bfloat16
AF = mybir.ActivationFunctionType
ALU = mybir.AluOpType
AX = mybir.AxisListType

DEBUG = 0
N_CORES = 8
T = 4096
SEQ = 2048
D = 1024
FC = 8
NTT = 8
NTC = 32
E = 8
NQ8 = 8
HC4 = 4
RANK = 16
PAIRS = 8
NKC = 16
QG = 256
NQG = 8
EPS = 1e-5

C_PT = 0
C_PB = 4096
C_OT = 8192
C_VST = 10240
C_USB = 10752
CB_COLS = 11264
C_XLNTM = 4096
C_XTM = 5120
C_B2SB = 8192
C_B1COLS = 9216
C_WTSTM = 9728
C_PRB = 10240

CONST_ID = 0
CONST_SEL = 128
CONST_ONES = 1152

D_LAND = 0
D_STATS = 1024


_UID = [0]


def _u(name):
    _UID[0] += 1
    return f"{name}_{_UID[0]}"


def _hbm(nc, name, shape, dtype, out=False):
    return nc.dram_tensor(name, list(shape), dtype,
                          kind="ExternalOutput" if out else "ExternalInput").ap()


class Prog:
    ENGS = ("pe", "act", "dve", "pool", "syncq")

    def __init__(self, nc, stack):
        self.nc = nc
        self.stack = stack
        self.ops = {e: [] for e in self.ENGS}
        self.sems = {}
        self.cnt = {}

    def new_sems(self, names):
        for n in names:
            self.sems[n] = self.stack.enter_context(self.nc.semaphore(n))
            self.cnt[n] = 0

    def op(self, eng, fn, inc=None, incn=1):
        if inc is None:
            self.ops[eng].append(lambda e, fn=fn: fn(e))
        else:
            self.cnt[inc] += incn
            s = self.sems[inc]
            self.ops[eng].append(
                lambda e, fn=fn, s=s, incn=incn: fn(e).then_inc(s, incn))

    def wait(self, eng, sem, val=None):
        if val is None:
            val = self.cnt[sem]
        if val <= 0:
            return
        s = self.sems[sem]
        self.ops[eng].append(lambda e, s=s, v=val: e.wait_ge(s, v))

    def entry_guard(self, eng="pe"):
        for s in ("act", "dve", "pool"):
            self.wait(eng, s)

    def flush_block(self, name):
        nc = self.nc
        with nc.Block(name) as blk:
            for eng, deco in (("pe", blk.tensor), ("act", blk.scalar),
                              ("dve", blk.vector), ("pool", blk.gpsimd),
                              ("syncq", blk.sync)):
                ops = self.ops[eng]
                if not ops:
                    continue

                def body(e, ops=ops):
                    for f in ops:
                        f(e)
                deco(body)
        self.ops = {e: [] for e in self.ENGS}


class G:
    def __init__(self, nc, p, h, **bufs):
        self.nc, self.p, self.h = nc, p, h
        self.__dict__.update(bufs)
        p.new_sems(["dmaw", "dmaio", "pe", "act", "dve", "pool", "st"])

    def ident(self):
        return self.cb[:, CONST_ID:CONST_ID + 128]

    def ident_f(self):
        return self.cf[0:8, 0:8]

    def sel(self, e):
        return self.cb[0:8, CONST_SEL + e * 128:CONST_SEL + (e + 1) * 128]

    def ones_row(self, n):
        return self.cb[0:1, CONST_ONES:CONST_ONES + n]

    def stats(self, i, w=1):
        return self.fpD[:, D_STATS + i:D_STATS + i + w]

    def land(self, slot):
        return self.fpD[:, D_LAND + slot * 512:D_LAND + (slot + 1) * 512]

    def land2(self):
        return self.fpD[:, D_LAND:D_LAND + 1024]

    def b1col(self, e, hc_abs):
        v = self.bufC[:, C_B1COLS:C_B1COLS + 512].bitcast(F32)
        return v[:, hc_abs * 8 + e:hc_abs * 8 + e + 1]

    def wts_tm_f(self):
        return self.bufC[:, C_WTSTM:C_WTSTM + 512].bitcast(F32)

    def prb(self):
        return self.bufC[:, C_PRB:C_PRB + 256].bitcast(F32)


def emit_load_biases(g):
    nc, p, h = g.nc, g.p, g.h
    p.entry_guard("syncq")
    p.op("syncq", lambda e: e.dma_start(g.fpD[0:8, 0:1024], h["b2"]),
         inc="dmaio", incn=16)
    p.wait("pool", "dmaio")
    p.op("pool", lambda e: e.tensor_copy(
        g.bufC[0:8, C_B2SB:C_B2SB + 1024], g.fpD[0:8, 0:1024]), inc="pool")
    bA32 = g.bufA[:, 0:8192].bitcast(F32)
    p.op("syncq", lambda e: e.dma_start(bA32[0:8, 0:4096], h["b1"]),
         inc="dmaio", incn=16)
    p.wait("pe", "dmaio")
    p.entry_guard("pe")
    with nc.psum_tensor(_u("pbias"), [128, 512], F32) as pi0:
        for grp in range(2):
            if grp:
                p.wait("pe", "dve")
            for k in range(16):
                hc = grp * 16 + k
                src = bA32[0:8, hc * 128:(hc + 1) * 128]
                p.op("pe", lambda e, s=src, d=pi0[:, k * 8:(k + 1) * 8]:
                     e.transpose(d, s, g.ident_f()), inc="pe")
            p.wait("dve", "pe")
            dstc = g.bufC[:, C_B1COLS + grp * 256:
                          C_B1COLS + (grp + 1) * 256].bitcast(F32)
            p.op("dve", lambda e, d=dstc: e.tensor_copy(d, pi0[:, 0:128]),
                 inc="dve")
    p.wait("syncq", "dve")
    p.wait("syncq", "pool")


def emit_init(g):
    nc, p, h = g.nc, g.p, g.h
    p.op("syncq", lambda e: e.dma_start(g.cb[:, :], h["cbf"]), inc="dmaio", incn=16)
    p.op("syncq", lambda e: e.dma_start(g.cf[:, :], h["cf32"]), inc="dmaio", incn=16)
    land3 = g.land(0).rearrange("p (c r) -> p c r", c=8)[:, :, 0:8]
    p.op("syncq", lambda e: e.dma_start(
        land3, h["wr"].rearrange("(c p) r -> p c r", c=8)), inc="dmaio", incn=16)
    p.wait("pool", "dmaio")
    p.op("pool", lambda e: e.tensor_copy(
        g.rW[:, :].rearrange("p (c r) -> p c r", r=8), land3), inc="pool")
    p.wait("syncq", "pool")
    p.op("syncq", lambda e: e.dma_start(g.fpD[0:1, 0:1024], h["bo"]),
         inc="dmaio", incn=16)
    p.wait("pe", "dmaio")
    with nc.psum_tensor(_u("pbo"), [128, 512], F32) as pbo:
        for oc in range(8):
            src = g.fpD[0:1, oc * 128:(oc + 1) * 128]
            p.op("pe", lambda e, s2=src, d=pbo[:, oc:oc + 1]: e.transpose(
                d, s2, g.ident_f()[0:1, 0:1]), inc="pe")
        p.wait("dve", "pe")
        p.op("dve", lambda e: e.tensor_copy(g.stats(32, 8), pbo[:, 0:8]),
             inc="dve")
    p.wait("syncq", "dve")
    emit_load_biases(g)


def emit_ln(g, seqs, from_input):
    nc, p, h = g.nc, g.p, g.h
    tcs = [(s, tc) for s in seqs for tc in range(16)]
    p.entry_guard("pe")
    with nc.psum_tensor(_u("lnpkA"), [128, 512], BF16) as pkA, \
         nc.psum_tensor(_u("lnpkB"), [128, 512], BF16) as pkB, \
         nc.psum_tensor(_u("lnsq"), [128, 1024], F32) as psq:
        pks = (pkA, pkB)
        hist = {0: ("dve", 0), 1: ("dve", 0)}
        sm = g.stats(0)
        sq = g.stats(1)
        rstd = g.stats(2)
        nmrs = g.stats(3)
        for s, tc in tcs:
            t0 = s * SEQ + tc * 128
            xtm = g.bufC[:, C_XTM:C_XTM + 1024]
            xlntm = g.bufC[:, C_XLNTM:C_XLNTM + 1024]
            if from_input:
                p.wait("syncq", "act")
                p.wait("syncq", "pool")
                p.wait("syncq", "dve")
                p.op("syncq", lambda e, t0=t0: e.dma_start(
                    g.land2(), h["x"][t0:t0 + 128, :]), inc="dmaio", incn=16)
                p.wait("dve", "dmaio")
                p.op("dve", lambda e: e.reduce_sum(sm, g.land2(), AX.X), inc="dve")
                p.wait("act", "dmaio")
                p.wait("act", "dve", p.cnt["dve"] - 1)
                p.op("act", lambda e: e.activation(psq[:, :], g.land2(), AF.Square),
                     inc="act")
                p.wait("dve", "act")
                p.op("dve", lambda e: e.reduce_sum(sq, psq[:, :], AX.X), inc="dve")
                p.wait("pool", "dmaio")
                p.wait("pool", "pe")
                p.op("pool", lambda e: e.tensor_copy(xtm, g.land2()), inc="pool")
                p.wait("pe", "pool")
                for k in range(2):
                    pk = pks[k]
                    heng, hcnt = hist[k]
                    p.wait("pe", heng, hcnt)
                    for j in range(4):
                        fc = k * 4 + j
                        p.op("pe", lambda e, pk=pk, j=j, fc=fc: e.transpose(
                            pk[:, j * 128:(j + 1) * 128],
                            xtm[:, fc * 128:(fc + 1) * 128], g.ident()), inc="pe")
                    p.wait("dve", "pe")
                    dst = g.xres.rearrange("p (f t) -> p f t", f=8)[
                        :, k * 4:(k + 1) * 4, t0:t0 + 128]
                    p.op("dve", lambda e, d=dst, pk=pk: e.tensor_copy(
                        d, pk[:, :].rearrange("p (j t) -> p j t", j=4)), inc="dve")
                    hist[k] = ("dve", p.cnt["dve"])
            else:
                for k in range(2):
                    pk = pks[k]
                    heng, hcnt = hist[k]
                    p.wait("pe", heng, hcnt)
                    if k == 0:
                        p.wait("pe", "dve")
                        p.wait("pe", "act")
                    for j in range(4):
                        fc = k * 4 + j
                        src = g.xres[:, fc * 4096 + t0:fc * 4096 + t0 + 128]
                        p.op("pe", lambda e, pk=pk, j=j, s=src: e.transpose(
                            pk[:, j * 128:(j + 1) * 128], s, g.ident()), inc="pe")
                    p.wait("act", "pe")
                    dst = xtm.rearrange("p (f t) -> p f t", f=8)[:, k * 4:(k + 1) * 4, :]
                    p.op("act", lambda e, d=dst, pk=pk: e.copy(
                        d, pk[:, :].rearrange("p (j t) -> p j t", j=4)), inc="act")
                    hist[k] = ("act", p.cnt["act"])
                p.wait("dve", "act")
                p.op("dve", lambda e: e.reduce_sum(sm, xtm, AX.X), inc="dve")
                p.wait("act", "act")
                p.wait("act", "dve", p.cnt["dve"] - 1)
                p.op("act", lambda e: e.activation(psq[:, :], xtm, AF.Square),
                     inc="act")
                p.wait("dve", "act")
                p.op("dve", lambda e: e.reduce_sum(sq, psq[:, :], AX.X), inc="dve")
            n = float(D)
            p.wait("dve", "dve")
            p.op("dve", lambda e: e.tensor_scalar(sm, sm, 1.0 / n, None,
                                                  ALU.mult, ALU.bypass), inc="dve")
            p.wait("dve", "dve")
            p.op("dve", lambda e: e.tensor_tensor(rstd, sm, sm, ALU.mult), inc="dve")
            p.op("dve", lambda e: e.tensor_scalar(sq, sq, 1.0 / n, None,
                                                  ALU.mult, ALU.bypass), inc="dve")
            p.wait("dve", "dve")
            p.op("dve", lambda e: e.tensor_scalar(rstd, rstd, EPS, None,
                                                  ALU.subtract, ALU.bypass),
                 inc="dve")
            p.wait("dve", "dve")
            p.op("dve", lambda e: e.tensor_tensor(sq, sq, rstd, ALU.subtract),
                 inc="dve")
            p.wait("act", "dve")
            p.op("act", lambda e: e.activation(rstd, sq, AF.Sqrt), inc="act")
            p.wait("dve", "act")
            p.op("dve", lambda e: e.reciprocal(rstd, rstd), inc="dve")
            p.wait("dve", "dve")
            p.op("dve", lambda e: e.tensor_tensor(nmrs, sm, rstd, ALU.mult),
                 inc="dve")
            p.wait("dve", "dve")
            p.op("dve", lambda e: e.tensor_scalar(nmrs, nmrs, -1.0, None,
                                                  ALU.mult, ALU.bypass), inc="dve")
            p.wait("dve", "dve")
            src_aff = g.land2() if from_input else xtm
            p.op("dve", lambda e, s=src_aff: e.tensor_scalar(
                xlntm, s, rstd, nmrs, ALU.mult, ALU.add), inc="dve")
            aff_cnt = p.cnt["dve"]
            for k in range(2):
                pk = pks[k]
                heng, hcnt = hist[k]
                p.wait("pe", heng, hcnt)
                p.wait("pe", "dve", aff_cnt)
                for j in range(4):
                    fc = k * 4 + j
                    p.op("pe", lambda e, pk=pk, j=j, fc=fc: e.transpose(
                        pk[:, j * 128:(j + 1) * 128],
                        xlntm[:, fc * 128:(fc + 1) * 128], g.ident()), inc="pe")
                p.wait("dve", "pe")
                dst = g.xln.rearrange("p (f t) -> p f t", f=8)[
                    :, k * 4:(k + 1) * 4, t0:t0 + 128]
                p.op("dve", lambda e, d=dst, pk=pk: e.tensor_copy(
                    d, pk[:, :].rearrange("p (j t) -> p j t", j=4)), inc="dve")
                hist[k] = ("dve", p.cnt["dve"])
        p.wait("syncq", "dve")


def emit_router(g):
    nc, p = g.nc, g.p
    p.entry_guard("pe")
    logT = g.fpD[0:8, 0:512]
    with nc.psum_tensor(_u("rlog"), [128, 512], F32) as plog, \
         nc.psum_tensor(_u("rtm"), [128, 512], F32) as ptm, \
         nc.psum_tensor(_u("rpk"), [128, 512], BF16) as ppk:
        for tt in range(NTT):
            if tt:
                p.wait("pe", "dve")
            for fc in range(FC):
                rhs = g.xln[:, fc * 4096 + tt * 512:fc * 4096 + (tt + 1) * 512]
                p.op("pe", lambda e, fc=fc, rhs=rhs: e.matmul(
                    plog[0:8, :], g.rW[:, fc * 8:(fc + 1) * 8], rhs,
                    start=(fc == 0), stop=(fc == 7)),
                     inc="pe" if fc == 7 else None)
            p.wait("dve", "pe")
            p.op("dve", lambda e: e.tensor_copy(logT, plog[0:8, :]), inc="dve")
            p.wait("pe", "dve")
            for k in range(4):
                p.op("pe", lambda e, k=k: e.transpose(
                    ptm[:, k * 8:(k + 1) * 8], logT[:, k * 128:(k + 1) * 128],
                    g.ident_f()), inc="pe")
            p.wait("dve", "pe")
            for k in range(4):
                tc = tt * 4 + k
                lg = ptm[:, k * 8:(k + 1) * 8]
                prb = g.prb()
                ex = prb[:, 0:8]
                ismax = prb[:, 8:16]
                msk = prb[:, 16:24]
                nm = g.stats(8)
                smx = g.stats(9)
                rs = g.stats(10)
                m2 = g.stats(11)
                p.wait("dve", "dve")
                p.op("dve", lambda e, lg=lg: e.tensor_reduce(
                    nm, lg, AX.X, ALU.max, negate=True), inc="dve")
                p.wait("act", "dve")
                p.op("act", lambda e, lg=lg, ex=ex: e.activation(
                    ex, lg, AF.Exp, bias=nm, scale=1.0, accum_out=smx), inc="act")
                p.wait("dve", "act")
                p.op("dve", lambda e: e.reciprocal(rs, smx), inc="dve")
                p.wait("dve", "dve")
                p.op("dve", lambda e, ex=ex: e.tensor_scalar(
                    ex, ex, rs, 0.5, ALU.mult, ALU.mult), inc="dve")
                p.wait("dve", "dve")
                p.op("dve", lambda e, ex=ex: e.tensor_reduce(
                    nm, ex, AX.X, ALU.max), inc="dve")
                p.wait("dve", "dve")
                p.op("dve", lambda e, ex=ex, ismax=ismax: e.tensor_scalar(
                    ismax, ex, nm, 1e30, ALU.is_ge, ALU.mult), inc="dve")
                p.wait("dve", "dve")
                p.op("dve", lambda e, ex=ex, ismax=ismax, msk=msk: e.tensor_tensor(
                    msk, ex, ismax, ALU.subtract), inc="dve")
                p.wait("dve", "dve")
                p.op("dve", lambda e, msk=msk: e.tensor_reduce(
                    m2, msk, AX.X, ALU.max), inc="dve")
                p.wait("dve", "dve")
                p.op("dve", lambda e, ex=ex, msk=msk: e.tensor_scalar(
                    msk, ex, m2, None, ALU.is_ge, ALU.bypass), inc="dve")
                p.wait("dve", "dve")
                wdst = g.wts_tm_f()[:, tc * 8:(tc + 1) * 8]
                p.op("dve", lambda e, wdst=wdst, ex=ex, msk=msk: e.tensor_tensor(
                    wdst, ex, msk, ALU.mult), inc="dve")
        wbf = g.bufC[:, C_VST:C_VST + 256]
        p.wait("pool", "dve")
        p.op("pool", lambda e: e.tensor_copy(wbf, g.wts_tm_f()), inc="pool")
        p.wait("pe", "pool")
        for grp in range(8):
            if grp:
                p.wait("pe", "dve")
            for k in range(4):
                tc = grp * 4 + k
                p.op("pe", lambda e, tc=tc, k=k: e.transpose(
                    ppk[0:8, k * 128:(k + 1) * 128],
                    wbf[:, tc * 8:(tc + 1) * 8], g.ident()), inc="pe")
            p.wait("dve", "pe")
            dst = g.bufC[0:8, C_PT + grp * 512:C_PT + (grp + 1) * 512]
            p.op("dve", lambda e, d=dst: e.tensor_copy(d, ppk[0:8, 0:512]),
                 inc="dve")
        p.wait("syncq", "dve")
        if DEBUG:
            p.op("syncq", lambda e: e.dma_start(
                g.h["dbg_wts"], g.wts_tm_f()), inc="dmaio", incn=16)
            p.wait("syncq", "dmaio")


def emit_ffn(g):
    nc, p, h = g.nc, g.p, g.h
    p.entry_guard("pe")
    p.entry_guard("syncq")
    with ExitStack() as ps:
        pt = lambda nm: ps.enter_context(nc.psum_tensor(_u(nm), [128, 512], F32))[:, :]
        pA = (pt("fA0"), pt("fA1"))
        pB = (pt("fB0"), pt("fB1"))
        pW = (pt("fW0"), pt("fW1"))

        conv_done = {}
        pe_done_unit = {}

        def emit_weight_load(u):
            e_, q8 = divmod(u, NQ8)
            half = (u % 2) * 8192
            first_conv = True
            for ch in range(16):
                slot = ch % 2
                if ch < 8:
                    fc = ch
                    src = h["w1"][e_, fc * 128:(fc + 1) * 128,
                                  q8 * 512:(q8 + 1) * 512]
                    dst = g.bufA[:, half + fc * 512:half + (fc + 1) * 512]
                else:
                    kk = ch - 8
                    hc, hlf = divmod(kk, 2)
                    base = half + 4096 + hc * 1024 + hlf * 512
                    src = h["w2"][e_, q8 * 512 + hc * 128:q8 * 512 + (hc + 1) * 128,
                                  hlf * 512:(hlf + 1) * 512]
                    dst = g.bufA[:, base:base + 512]
                p.wait("syncq", "pool", max(0, p.cnt["pool"] - 1))
                p.op("syncq", lambda e, s=src, sl=slot: e.dma_start(g.land(sl), s),
                     inc="dmaw", incn=16)
                if first_conv and u - 2 in pe_done_unit:
                    p.wait("pool", "pe", pe_done_unit[u - 2])
                first_conv = False
                p.wait("pool", "dmaw")
                p.op("pool", lambda e, d=dst, sl=slot: e.tensor_copy(d, g.land(sl)),
                     inc="pool")
            conv_done[u] = p.cnt["pool"]

        emit_weight_load(0)
        emit_weight_load(1)

        NU = E * NQ8
        hw_cols = lambda par, hc: g.bufB[:, par * 2048 + hc * 512:
                                         par * 2048 + (hc + 1) * 512]
        gtmp = lambda sl: g.uni[:, sl * 512:(sl + 1) * 512]

        stepc = 0
        hist_gelu = {}
        hist_add = {}
        hist_hw = {}
        hist_wb = {}
        pe_b_stop = {}

        for u in range(NU):
            e_, q8 = divmod(u, NQ8)
            half = (u % 2) * 8192
            W1 = lambda fc, hc, half=half: g.bufA[
                :, half + fc * 512 + hc * 128:half + fc * 512 + (hc + 1) * 128]
            W2 = lambda hc, oc, half=half: g.bufA[
                :, half + 4096 + hc * 1024 + oc * 128:
                half + 4096 + hc * 1024 + (oc + 1) * 128]
            p.wait("pe", "pool", conv_done[u])
            for tt in range(NTT):
                par = stepc % 2
                wbank = pW[par]
                if stepc - 2 in hist_wb:
                    p.wait("pe", "dve", hist_wb[stepc - 2])
                rhs_w = g.bufC[0:8, C_PT + tt * 512:C_PT + (tt + 1) * 512]
                p.op("pe", lambda e, wb=wbank, s=g.sel(e_), r=rhs_w: e.matmul(
                    wb, s, r, start=True, stop=True), inc="pe")
                for hc in range(HC4):
                    bank = pA[hc % 2]
                    p.wait("pe", "act", hist_gelu.get(hc % 2, 0))
                    for fc in range(FC):
                        rhs = g.xln[:, fc * 4096 + tt * 512:
                                    fc * 4096 + (tt + 1) * 512]
                        p.op("pe", lambda e, b=bank, w=W1(fc, hc), r=rhs, fc=fc:
                             e.matmul(b, w, r, start=(fc == 0), stop=(fc == 7)),
                             inc="pe" if fc == 7 else None)
                    pe_a = p.cnt["pe"]
                    p.wait("act", "pe", pe_a)
                    p.wait("act", "dve", hist_hw.get(hc % 2, 0))
                    b1c = g.b1col(e_, q8 * 4 + hc)
                    p.op("act", lambda e, b=bank, gt=gtmp(hc % 2), b1c=b1c:
                         e.activation(gt, b, AF.Gelu_apprx_tanh, bias=b1c,
                                      scale=1.0), inc="act")
                    hist_gelu[hc % 2] = p.cnt["act"]
                    p.wait("dve", "act")
                    if stepc - 2 in pe_b_stop:
                        p.wait("dve", "pe", pe_b_stop[stepc - 2])
                    dst = hw_cols(par, hc)
                    p.op("dve", lambda e, d=dst, gt=gtmp(hc % 2), wb=wbank:
                         e.tensor_tensor(d, gt, wb, ALU.mult), inc="dve")
                    hist_hw[hc % 2] = p.cnt["dve"]
                hist_wb[stepc] = p.cnt["dve"]
                p.wait("pe", "dve", p.cnt["dve"])
                for oc in range(FC):
                    bank = pB[oc % 2]
                    p.wait("pe", "dve", hist_add.get(oc % 2, 0))
                    for hc in range(HC4):
                        p.op("pe", lambda e, b=bank, w=W2(hc, oc),
                             r=hw_cols(par, hc), hc=hc: e.matmul(
                                 b, w, r, start=(hc == 0), stop=(hc == 3)),
                             inc="pe" if hc == 3 else None)
                    p.wait("dve", "pe", p.cnt["pe"])
                    xs = g.xres[:, oc * 4096 + tt * 512:oc * 4096 + (tt + 1) * 512]
                    p.op("dve", lambda e, x=xs, b=bank: e.tensor_tensor(
                        x, x, b, ALU.add), inc="dve")
                    hist_add[oc % 2] = p.cnt["dve"]
                pe_b_stop[stepc] = p.cnt["pe"]
                stepc += 1
            pe_done_unit[u] = p.cnt["pe"]
            if u + 2 < NU:
                emit_weight_load(u + 2)
        for tt in range(NTT):
            for fc in range(FC):
                bank = pW[(tt * FC + fc) % 2]
                p.wait("pe", "dve")
                b2l = g.bufC[0:8, C_B2SB + fc * 128:C_B2SB + (fc + 1) * 128]
                rhs_w = g.bufC[0:8, C_PT + tt * 512:C_PT + (tt + 1) * 512]
                p.op("pe", lambda e, b=bank, l=b2l, r=rhs_w: e.matmul(
                    b, l, r, start=True, stop=True), inc="pe")
                p.wait("dve", "pe")
                xs = g.xres[:, fc * 4096 + tt * 512:fc * 4096 + (tt + 1) * 512]
                p.op("dve", lambda e, x=xs, b=bank: e.tensor_tensor(
                    x, x, b, ALU.add), inc="dve")
        p.wait("syncq", "dve")


def emit_attn(g, seq):
    nc, p, h = g.nc, g.p, g.h
    s = seq
    o = 1 - seq
    unib = g.uni[:, :]
    lorB = lambda oc: unib[0:16, oc * 128:(oc + 1) * 128]
    lorA = lambda fc: unib[:, 1024 + fc * 16:1024 + (fc + 1) * 16]
    xcol = lambda fc, t0, w=512: g.xln[:, fc * 4096 + s * SEQ + t0:
                                       fc * 4096 + s * SEQ + t0 + w]
    kcol = lambda fc, t0, w=512: g.xln[:, fc * 4096 + o * SEQ + t0:
                                       fc * 4096 + o * SEQ + t0 + w]
    wstg = g.bufC[:, C_PB:C_PB + 1024]
    WT = lambda kc, oc: g.bufB[:, (kc * 8 + oc) * 128:(kc * 8 + oc + 1) * 128]
    u_sb = g.bufC[0:16, C_USB:C_USB + 512]
    vst = g.bufC[:, C_VST:C_VST + 512]
    PT = lambda kc: g.bufC[:, C_PT + kc * 256:C_PT + (kc + 1) * 256]
    OT = lambda c: g.bufC[:, C_OT + c * 256:C_OT + (c + 1) * 256]
    Pb = lambda sl: g.bufC[:, C_PB + sl * 2048:C_PB + (sl + 1) * 2048]
    Vtm = lambda c, kc: g.bufA[:, (c * NKC + kc) * 128:(c * NKC + kc + 1) * 128]

    p.entry_guard("pe")
    p.entry_guard("syncq")
    with ExitStack() as ps:
        pP = [ps.enter_context(nc.psum_tensor(_u(f"aP{i}"), [128, 512], F32))[:, :]
              for i in range(8)]

        def load_W(name):
            for ocg in range(8):
                for hlf in range(2):
                    p.wait("syncq", "pool")
                    src = h[name][ocg * 128:(ocg + 1) * 128,
                                  hlf * 512:(hlf + 1) * 512]
                    p.op("syncq", lambda e, sr=src, sl=hlf: e.dma_start(
                        g.land(sl), sr), inc="dmaw", incn=16)
                    p.wait("pool", "dmaw")
                    p.wait("pool", "pe")
                    dst = wstg[:, hlf * 512:(hlf + 1) * 512]
                    p.op("pool", lambda e, d=dst, sl=hlf: e.tensor_copy(
                        d, g.land(sl)), inc="pool")
                p.wait("pe", "pool")
                for fcg in range(2):
                    bank = pP[6 + fcg]
                    p.wait("pe", "dve")
                    for j in range(4):
                        fc = fcg * 4 + j
                        p.op("pe", lambda e, b=bank, j=j, fc=fc: e.transpose(
                            b.bitcast(BF16)[:, j * 128:(j + 1) * 128],
                            wstg[:, fc * 128:(fc + 1) * 128], g.ident()),
                             inc="pe")
                    p.wait("dve", "pe")
                    for j in range(4):
                        fc = fcg * 4 + j
                        p.op("dve", lambda e, b=bank, j=j, d=WT(fc, ocg):
                             e.tensor_copy(d, b.bitcast(BF16)[:, j * 128:(j + 1) * 128]),
                             inc="dve")

        def load_lora(wn_a, wn_b):
            p.wait("syncq", "pool")
            p.wait("syncq", "pe")
            p.op("syncq", lambda e: e.dma_start(g.fpD[0:16, 0:1024], h[wn_a]),
                 inc="dmaw", incn=16)
            p.wait("pool", "dmaw")
            p.wait("pool", "pe")
            astg = wstg[0:16, 0:1024]
            p.op("pool", lambda e: e.tensor_copy(astg, g.fpD[0:16, 0:1024]),
                 inc="pool")
            p.wait("pe", "pool")
            bank = pP[6]
            p.wait("pe", "dve")
            for fc in range(8):
                p.op("pe", lambda e, b=bank, fc=fc: e.transpose(
                    b.bitcast(BF16)[:, fc * 16:(fc + 1) * 16],
                    astg[:, fc * 128:(fc + 1) * 128], g.ident()[0:16, 0:16]),
                     inc="pe")
            p.wait("dve", "pe")
            p.op("dve", lambda e, b=bank: e.tensor_copy(
                unib[:, 1024:1152], b.bitcast(BF16)[:, 0:128]), inc="dve")
            p.wait("syncq", "pool")
            land3 = g.land(0).rearrange("p (c r) -> p c r", c=8)[:, :, 0:16]
            p.op("syncq", lambda e, l3=land3: e.dma_start(
                l3, h[wn_b].rearrange("(c p) r -> p c r", c=8)),
                 inc="dmaw", incn=16)
            p.wait("pool", "dmaw")
            p.wait("pool", "pe")
            bstg = wstg[:, 0:128]
            p.op("pool", lambda e, l3=land3: e.tensor_copy(
                bstg.rearrange("p (c r) -> p c r", c=8), l3), inc="pool")
            p.wait("pe", "pool")
            bank2 = pP[7]
            p.wait("pe", "dve")
            for oc in range(8):
                p.op("pe", lambda e, b=bank2, oc=oc: e.transpose(
                    b.bitcast(BF16)[0:16, oc * 128:(oc + 1) * 128],
                    bstg[:, oc * 16:(oc + 1) * 16], g.ident()), inc="pe")
            p.wait("dve", "pe")
            p.op("dve", lambda e, b=bank2: e.tensor_copy(
                unib[0:16, 0:1024], b.bitcast(BF16)[0:16, 0:1024]), inc="dve")

        def proj(kind):
            wn = {"k": "Wk", "v": "Wv", "q": "Wq"}[kind]
            load_W(wn)
            load_lora({"k": "Ak", "v": "Av", "q": "Aq"}[kind],
                      {"k": "Bk", "v": "Bv", "q": "Bq"}[kind])
            for tt in range(4):
                t0 = tt * 512
                bank_u = pP[0]
                p.wait("pe", "act")
                for fc in range(FC):
                    p.op("pe", lambda e, b=bank_u, w=lorA(fc), r=xcol(fc, t0),
                         fc=fc: e.matmul(b[0:16, :], w, r, start=(fc == 0),
                                         stop=(fc == 7)),
                         inc="pe" if fc == 7 else None)
                p.wait("act", "pe")
                p.op("act", lambda e, b=bank_u: e.activation(
                    u_sb, b[0:16, :], AF.Copy, scale=1.0 / RANK), inc="act")
                uc = p.cnt["act"]
                if kind == "q":
                    for oc in range(FC):
                        bank = pP[oc]
                        if oc == 0:
                            p.wait("pe", "act", uc)
                        for fc in range(FC):
                            p.op("pe", lambda e, b=bank, w=WT(fc, oc),
                                 r=xcol(fc, t0), fc=fc: e.matmul(
                                     b, w, r, start=(fc == 0), stop=False))
                        p.op("pe", lambda e, b=bank, w=lorB(oc): e.matmul(
                            b, w, u_sb, start=False, stop=True), inc="pe")
                    p.wait("act", "pe")
                    for oc in range(FC):
                        dst = xcol(oc, t0)
                        p.op("act", lambda e, b=pP[oc], d=dst: e.activation(
                            d, b, AF.Copy, scale=0.125), inc="act")
                else:
                    for oc in range(FC):
                        bank = pP[1 + (oc % 2)]
                        p.wait("pe", "act")
                        if kind == "v":
                            p.wait("pe", "dve")
                        for fc in range(FC):
                            p.op("pe", lambda e, b=bank, w=WT(fc, oc),
                                 r=xcol(fc, t0), fc=fc: e.matmul(
                                     b, w, r, start=(fc == 0), stop=False))
                        p.wait("pe", "act", uc)
                        p.op("pe", lambda e, b=bank, w=lorB(oc): e.matmul(
                            b, w, u_sb, start=False, stop=True), inc="pe")
                        p.wait("act", "pe")
                        if kind == "k":
                            dst = kcol(oc, t0)
                            p.op("act", lambda e, b=bank, d=dst: e.activation(
                                d, b, AF.Copy, scale=1.0), inc="act")
                        else:
                            p.wait("act", "pe")
                            p.op("act", lambda e, b=bank: e.activation(
                                vst, b, AF.Copy, scale=1.0), inc="act")
                            bankT = pP[3 + (oc % 2)]
                            p.wait("pe", "act")
                            p.wait("pe", "dve")
                            for j in range(4):
                                p.op("pe", lambda e, b=bankT, j=j: e.transpose(
                                    b.bitcast(BF16)[:, j * 128:(j + 1) * 128],
                                    vst[:, j * 128:(j + 1) * 128], g.ident()),
                                     inc="pe")
                            p.wait("dve", "pe")
                            for j in range(4):
                                kc = tt * 4 + j
                                p.op("dve", lambda e, b=bankT, j=j,
                                     d=Vtm(oc, kc): e.tensor_copy(
                                         d, b.bitcast(BF16)[:, j * 128:(j + 1) * 128]),
                                     inc="dve")

        proj("k")
        proj("v")
        proj("q")
        load_lora("Ao", "Bo")
        load_W("Wo")
        p.wait("syncq", "dve")
        p.wait("syncq", "act")

    p.entry_guard("pe")
    with nc.psum_tensor(_u("aSA"), [128, 2048], F32) as SAh, \
         nc.psum_tensor(_u("aSB"), [128, 2048], F32) as SBh:
        SA = SAh[:, :]
        SB = SBh[:, :]
        Ssets = (SA, SB)
        nsum = lambda i: g.stats(16 + i)
        negm = g.stats(20)
        ssum = g.stats(21)
        rs = g.stats(22)
        set_free_exp = [0, 0]
        set_pack_evac = [0, 0]
        pv_evac = 0
        op_add = [0, 0]

        for qg in range(NQG):
            q0g = qg * QG
            for c in range(PAIRS):
                for j in (0, 1):
                    prow = j * 64
                    for qt in (0, 1):
                        si = qt
                        St = Ssets[si]
                        q0 = q0g + qt * 128
                        p.wait("pe", "act", set_free_exp[si])
                        p.wait("pe", "dve", max(set_pack_evac[si], pv_evac,
                                                op_add[0], op_add[1]))
                        lhs = g.xln[prow:prow + 64,
                                    c * 4096 + s * SEQ + q0:
                                    c * 4096 + s * SEQ + q0 + 128]
                        for kb in range(4):
                            rhs = g.xln[prow:prow + 64,
                                        c * 4096 + o * SEQ + kb * 512:
                                        c * 4096 + o * SEQ + (kb + 1) * 512]
                            p.op("pe", lambda e, St=St, kb=kb, l=lhs, r=rhs:
                                 e.matmul(St[:, kb * 512:(kb + 1) * 512], l, r,
                                          start=True, stop=True),
                                 inc="pe" if kb == 3 else None)
                        p.wait("dve", "pe")
                        p.op("dve", lambda e, St=St: e.tensor_reduce(
                            negm, St[:, :], AX.X, ALU.max, negate=True),
                             inc="dve")
                        p.wait("act", "dve")
                        for kb in range(4):
                            p.op("act", lambda e, St=St, kb=kb, si=si:
                                 e.activation(
                                     Pb(si)[:, kb * 512:(kb + 1) * 512],
                                     St[:, kb * 512:(kb + 1) * 512], AF.Exp,
                                     bias=negm, scale=1.0,
                                     accum_out=nsum(kb)), inc="act")
                        set_free_exp[si] = p.cnt["act"]
                        p.wait("dve", "act")
                        p.op("dve", lambda e: e.tensor_tensor(
                            nsum(0), nsum(0), nsum(1), ALU.add), inc="dve")
                        p.op("dve", lambda e: e.tensor_tensor(
                            nsum(2), nsum(2), nsum(3), ALU.add), inc="dve")
                        p.wait("dve", "dve")
                        p.op("dve", lambda e: e.tensor_tensor(
                            ssum, nsum(0), nsum(2), ALU.add), inc="dve")
                        p.wait("dve", "dve")
                        p.op("dve", lambda e: e.reciprocal(rs, ssum), inc="dve")
                        p.wait("pool", "dve")
                        p.op("pool", lambda e, si=si: e.tensor_scalar(
                            Pb(si), Pb(si), rs, None, ALU.mult, ALU.bypass),
                             inc="pool")
                        p.wait("pe", "pool")
                        for pk in range(4):
                            for jj in range(4):
                                kc = pk * 4 + jj
                                p.op("pe", lambda e, St=St, pk=pk, jj=jj,
                                     kc=kc, si=si: e.transpose(
                                         St.bitcast(BF16)[
                                             :, pk * 1024 + jj * 128:
                                             pk * 1024 + (jj + 1) * 128],
                                         Pb(si)[:, kc * 128:(kc + 1) * 128],
                                         g.ident()),
                                     inc="pe" if jj == 3 else None)
                            p.wait("dve", "pe")
                            base = C_PT + pk * 4 * 256
                            dst = g.bufC[:, base:base + 1024].rearrange(
                                "p (k v) -> p k v", v=256)[:, :, qt * 128:
                                                           qt * 128 + 128]
                            srcv = St.bitcast(BF16)[:, pk * 1024:pk * 1024 + 512]
                            p.op("dve", lambda e, d=dst, sv=srcv: e.tensor_copy(
                                d, sv.rearrange("p (k v) -> p k v", v=128)),
                                 inc="dve")
                        set_pack_evac[si] = p.cnt["dve"]
                    p.wait("pe", "dve", max(set_pack_evac[0], set_pack_evac[1],
                                            pv_evac))
                    for kc in range(NKC):
                        p.op("pe", lambda e, kc=kc, c=c, j=j: e.matmul(
                            SA[j * 64:(j + 1) * 64, 0:256],
                            Vtm(c, kc)[:, j * 64:(j + 1) * 64],
                            PT(kc), start=(kc == 0), stop=(kc == 15)),
                             inc="pe" if kc == 15 else None)
                    p.wait("dve", "pe")
                    dsto = OT(c)[j * 64:(j + 1) * 64, :]
                    p.op("dve", lambda e, d=dsto, j=j: e.tensor_copy(
                        d, SA[j * 64:(j + 1) * 64, 0:256]), inc="dve")
                    pv_evac = p.cnt["dve"]
            bank_u = SB
            p.wait("pe", "act", set_free_exp[1])
            p.wait("pe", "dve", max(pv_evac, set_pack_evac[1]))
            for pc in range(PAIRS):
                p.op("pe", lambda e, pc=pc: e.matmul(
                    bank_u[0:16, 0:256], lorA(pc), OT(pc),
                    start=(pc == 0), stop=(pc == 7)),
                     inc="pe" if pc == 7 else None)
            p.wait("act", "pe")
            p.op("act", lambda e: e.activation(
                u_sb[:, 0:256], bank_u[0:16, 0:256], AF.Copy, scale=1.0 / RANK),
                 inc="act")
            uc2 = p.cnt["act"]
            for oc in range(FC):
                off = (oc % 2) * 512
                if oc < 2:
                    p.wait("pe", "act", set_free_exp[0] if oc == 0 else 0)
                    p.wait("pe", "dve", max(set_pack_evac[0], pv_evac))
                    if oc == 0:
                        p.wait("pe", "act", uc2)
                else:
                    p.wait("pe", "dve", op_add[oc % 2])
                for pc in range(PAIRS):
                    p.op("pe", lambda e, oc=oc, pc=pc, off=off: e.matmul(
                        SA[:, off:off + 256], WT(pc, oc), OT(pc),
                        start=(pc == 0), stop=False))
                p.op("pe", lambda e, oc=oc, off=off: e.matmul(
                    SA[:, off:off + 256], lorB(oc), u_sb[:, 0:256],
                    start=False, stop=True), inc="pe")
                p.wait("dve", "pe")
                p.wait("dve", "dve")
                bocol = g.stats(32 + oc)
                p.op("dve", lambda e, oc=oc, off=off, bc=bocol: e.tensor_scalar(
                    vst[:, 0:256], SA[:, off:off + 256], bc, None,
                    ALU.add, ALU.bypass), inc="dve")
                p.wait("dve", "dve")
                xs = g.xres[:, oc * 4096 + s * SEQ + q0g:
                            oc * 4096 + s * SEQ + q0g + 256]
                p.op("dve", lambda e, x=xs: e.tensor_tensor(
                    x, x, vst[:, 0:256], ALU.add), inc="dve")
                op_add[oc % 2] = p.cnt["dve"]
        p.wait("syncq", "dve")


def emit_debug_dump(g, sbuf, name):
    p = g.p
    p.entry_guard("syncq")
    p.op("syncq", lambda e: e.dma_start(g.h[name], sbuf[:, :]),
         inc="dmaio", incn=16)
    p.wait("syncq", "dmaio")


def emit_store(g):
    nc, p, h = g.nc, g.p, g.h
    p.entry_guard("pe")
    with nc.psum_tensor(_u("spk0"), [128, 512], BF16) as pk0, \
         nc.psum_tensor(_u("spk1"), [128, 512], BF16) as pk1:
        pks = (pk0, pk1)
        for tc in range(NTC):
            t0 = tc * 128
            for k in range(2):
                pk = pks[k]
                p.wait("pe", "act")
                for j in range(4):
                    fc = k * 4 + j
                    src = g.xres[:, fc * 4096 + t0:fc * 4096 + t0 + 128]
                    p.op("pe", lambda e, pk=pk, j=j, srt=src: e.transpose(
                        pk[:, j * 128:(j + 1) * 128], srt, g.ident()), inc="pe")
                p.wait("act", "pe")
                p.wait("act", "st")
                dst = g.fpD[:, k * 512:(k + 1) * 512]
                p.op("act", lambda e, d=dst, pk=pk: e.copy(d, pk[:, :]),
                     inc="act")
            p.wait("syncq", "act")
            p.op("syncq", lambda e, t0=t0: e.dma_start(
                h["out"][t0:t0 + 128, :], g.fpD[:, 0:1024]), inc="st", incn=16)
        p.wait("syncq", "st")


def build_nc():
    nc = bass.Bass("TRN2", target_bir_lowering=False)
    h = {}
    h["x"] = _hbm(nc, "x", (T, D), F32)
    for w in ("Wq", "Wk", "Wv", "Wo"):
        h[w] = _hbm(nc, w, (D, D), F32)
    h["bo"] = _hbm(nc, "bo", (1, D), F32)
    for a in ("Aq", "Ak", "Av", "Ao"):
        h[a] = _hbm(nc, a, (RANK, D), F32)
    for b in ("Bq", "Bk", "Bv", "Bo"):
        h[b] = _hbm(nc, b, (D, RANK), F32)
    h["wr"] = _hbm(nc, "wr", (D, E), F32)
    h["w1"] = _hbm(nc, "w1", (E, D, 4 * D), F32)
    h["b1"] = _hbm(nc, "b1", (E, 4 * D), F32)
    h["w2"] = _hbm(nc, "w2", (E, 4 * D, D), F32)
    h["b2"] = _hbm(nc, "b2", (E, D), F32)
    h["cbf"] = _hbm(nc, "cbf", (128, 1408), BF16)
    h["cf32"] = _hbm(nc, "cf32", (8, 8), F32)
    h["out"] = _hbm(nc, "out", (T, D), F32, out=True)
    if DEBUG:
        h["dbg_xln"] = _hbm(nc, "dbg_xln", (128, 32768), BF16, out=True)
        h["dbg_x1"] = _hbm(nc, "dbg_x1", (128, 32768), BF16, out=True)
        h["dbg_x2"] = _hbm(nc, "dbg_x2", (128, 32768), BF16, out=True)
        h["dbg_wts"] = _hbm(nc, "dbg_wts", (128, 256), F32, out=True)
        h["dbg_q"] = _hbm(nc, "dbg_q", (128, 32768), BF16, out=True)
        h["dbg_vtm"] = _hbm(nc, "dbg_vtm", (128, 16384), BF16, out=True)
        h["dbg_bufc"] = _hbm(nc, "dbg_bufc", (128, CB_COLS), BF16, out=True)

    with ExitStack() as stack:
        sb = lambda name, shape, dt: stack.enter_context(
            nc.sbuf_tensor(name, shape, dt))
        bufs = dict(
            xres=sb("xres", [128, 32768], BF16),
            xln=sb("xln", [128, 32768], BF16),
            bufA=sb("bufA", [128, 16384], BF16),
            bufB=sb("bufB", [128, 8192], BF16),
            bufC=sb("bufC", [128, CB_COLS], BF16),
            fpD=sb("fpD", [128, 1088], F32),
            uni=sb("uni", [128, 1152], BF16),
            cb=sb("cb", [128, 1408], BF16),
            cf=sb("cf", [8, 8], F32),
            rW=sb("rW", [128, 64], BF16),
        )
        p = Prog(nc, stack)
        g = G(nc, p, h, **bufs)

        emit_init(g)
        p.flush_block("init")
        emit_ln(g, seqs=(0, 1), from_input=True)
        p.flush_block("ln1")
        if DEBUG:
            emit_debug_dump(g, g.xln, "dbg_xln")
            p.flush_block("dbgln")
        emit_router(g)
        p.flush_block("rt1")
        emit_ffn(g)
        p.flush_block("ffn1")
        if DEBUG:
            emit_debug_dump(g, g.xres, "dbg_x1")
            p.flush_block("dbg1")
        for s in (0, 1):
            emit_ln(g, seqs=(s,), from_input=False)
            p.flush_block(f"ln2s{s}")
            emit_attn(g, seq=s)
            p.flush_block(f"attns{s}")
            if DEBUG and s == 0:
                emit_debug_dump(g, g.xln, "dbg_q")
                emit_debug_dump(g, g.bufA, "dbg_vtm")
                emit_debug_dump(g, g.bufC, "dbg_bufc")
                p.flush_block("dbgattn")
        if DEBUG:
            emit_debug_dump(g, g.xres, "dbg_x2")
            p.flush_block("dbg2")
        emit_ln(g, seqs=(0, 1), from_input=False)
        p.flush_block("ln3")
        emit_load_biases(g)
        p.flush_block("bias2")
        emit_router(g)
        p.flush_block("rt3")
        emit_ffn(g)
        p.flush_block("ffn2")
        emit_store(g)
        p.flush_block("store")
    return nc


def _make_consts():
    cbf = np.zeros((128, 1408), dtype=ml_dtypes.bfloat16)
    cbf[:, CONST_ID:CONST_ID + 128] = np.eye(128, dtype=ml_dtypes.bfloat16)
    for e in range(8):
        cbf[e, CONST_SEL + e * 128:CONST_SEL + (e + 1) * 128] = 1.0
    cbf[0, CONST_ONES:CONST_ONES + 256] = 1.0
    cf32 = np.eye(8, dtype=np.float32)
    return cbf, cf32


_NC_CACHE = [None]


def kernel(L0_attn_x, L0_attn_Wq, L0_attn_Wk, L0_attn_Wv, L0_attn_Wo, L0_attn_bo,
           L0_attn_Aq, L0_attn_Bq, L0_attn_Ak, L0_attn_Bk, L0_attn_Av, L0_attn_Bv,
           L0_attn_Ao, L0_attn_Bo, L0_ffn_w_router, L0_ffn_w1, L0_ffn_b1,
           L0_ffn_w2, L0_ffn_b2):
    x = np.ascontiguousarray(np.asarray(L0_attn_x, dtype=np.float32))
    cbf, cf32 = _make_consts()
    f32c = lambda a: np.ascontiguousarray(np.asarray(a, dtype=np.float32))
    shared = {
        "Wq": f32c(L0_attn_Wq), "Wk": f32c(L0_attn_Wk),
        "Wv": f32c(L0_attn_Wv), "Wo": f32c(L0_attn_Wo),
        "bo": f32c(L0_attn_bo).reshape(1, D),
        "Aq": f32c(L0_attn_Aq), "Ak": f32c(L0_attn_Ak),
        "Av": f32c(L0_attn_Av), "Ao": f32c(L0_attn_Ao),
        "Bq": f32c(L0_attn_Bq), "Bk": f32c(L0_attn_Bk),
        "Bv": f32c(L0_attn_Bv), "Bo": f32c(L0_attn_Bo),
        "wr": f32c(L0_ffn_w_router),
        "w1": f32c(L0_ffn_w1), "b1": f32c(L0_ffn_b1),
        "w2": f32c(L0_ffn_w2), "b2": f32c(L0_ffn_b2),
        "cbf": cbf, "cf32": cf32,
    }
    if _NC_CACHE[0] is None:
        _NC_CACHE[0] = build_nc()
    nc = _NC_CACHE[0]
    in_maps = []
    for c in range(N_CORES):
        m = dict(shared)
        m["x"] = np.ascontiguousarray(x[2 * c:2 * c + 2].reshape(T, D))
        in_maps.append(m)
    res = run_bass_kernel_spmd(nc, in_maps, core_ids=list(range(N_CORES)))
    outs = [np.asarray(r["out"]).reshape(2, SEQ, D) for r in res.results]
    return np.concatenate(outs, axis=0).astype(np.float32)


# version 07
# speedup vs baseline: 1.3686x; 1.3686x over previous
import numpy as np
import ml_dtypes
import concourse.bass as bass
import concourse.mybir as mybir
from concourse.bass_utils import run_bass_kernel_spmd
from contextlib import ExitStack

F32 = mybir.dt.float32
BF16 = mybir.dt.bfloat16
AF = mybir.ActivationFunctionType
ALU = mybir.AluOpType
AX = mybir.AxisListType

DEBUG = 0
N_CORES = 8
T = 4096
SEQ = 2048
D = 1024
FC = 8
NTT = 8
NTC = 32
E = 8
NQ8 = 8
HC4 = 4
RANK = 16
PAIRS = 8
NKC = 16
QG = 256
NQG = 8
EPS = 1e-5

C_PT = 0
C_PB = 4096
C_OT = 8192
C_VST = 10240
C_USB = 10752
CB_COLS = 11264
C_XLNTM = 4096
C_XTM = 5120
C_B2SB = 8192
C_B1COLS = 9216
C_WTSTM = 9728
C_PRB = 10240

CONST_ID = 0
CONST_SEL = 128
CONST_ONES = 1152

D_LAND = 0
D_STATS = 1024


_UID = [0]


def _u(name):
    _UID[0] += 1
    return f"{name}_{_UID[0]}"


def _hbm(nc, name, shape, dtype, out=False):
    return nc.dram_tensor(name, list(shape), dtype,
                          kind="ExternalOutput" if out else "ExternalInput").ap()


class Prog:
    ENGS = ("pe", "act", "dve", "pool", "syncq")

    def __init__(self, nc, stack):
        self.nc = nc
        self.stack = stack
        self.ops = {e: [] for e in self.ENGS}
        self.sems = {}
        self.cnt = {}

    def new_sems(self, names):
        for n in names:
            self.sems[n] = self.stack.enter_context(self.nc.semaphore(n))
            self.cnt[n] = 0

    def op(self, eng, fn, inc=None, incn=1):
        if inc is None:
            self.ops[eng].append(lambda e, fn=fn: fn(e))
        else:
            self.cnt[inc] += incn
            s = self.sems[inc]
            self.ops[eng].append(
                lambda e, fn=fn, s=s, incn=incn: fn(e).then_inc(s, incn))

    def wait(self, eng, sem, val=None):
        if val is None:
            val = self.cnt[sem]
        if val <= 0:
            return
        s = self.sems[sem]
        self.ops[eng].append(lambda e, s=s, v=val: e.wait_ge(s, v))

    def entry_guard(self, eng="pe"):
        for s in ("act", "dve", "pool"):
            self.wait(eng, s)

    def flush_block(self, name):
        nc = self.nc
        with nc.Block(name) as blk:
            for eng, deco in (("pe", blk.tensor), ("act", blk.scalar),
                              ("dve", blk.vector), ("pool", blk.gpsimd),
                              ("syncq", blk.sync)):
                ops = self.ops[eng]
                if not ops:
                    continue

                def body(e, ops=ops):
                    for f in ops:
                        f(e)
                deco(body)
        self.ops = {e: [] for e in self.ENGS}


class G:
    def __init__(self, nc, p, h, **bufs):
        self.nc, self.p, self.h = nc, p, h
        self.__dict__.update(bufs)
        p.new_sems(["dmaw", "dmaio", "pe", "act", "dve", "pool", "st"])

    def ident(self):
        return self.cb[:, CONST_ID:CONST_ID + 128]

    def ident_f(self):
        return self.cf[0:8, 0:8]

    def sel(self, e):
        return self.cb[0:8, CONST_SEL + e * 128:CONST_SEL + (e + 1) * 128]

    def ones_row(self, n):
        return self.cb[0:1, CONST_ONES:CONST_ONES + n]

    def stats(self, i, w=1):
        return self.fpD[:, D_STATS + i:D_STATS + i + w]

    def land(self, slot):
        return self.fpD[:, D_LAND + slot * 512:D_LAND + (slot + 1) * 512]

    def land2(self):
        return self.fpD[:, D_LAND:D_LAND + 1024]

    def b1col(self, e, hc_abs):
        v = self.bufC[:, C_B1COLS:C_B1COLS + 512].bitcast(F32)
        return v[:, hc_abs * 8 + e:hc_abs * 8 + e + 1]

    def wts_tm_f(self):
        return self.bufC[:, C_WTSTM:C_WTSTM + 512].bitcast(F32)

    def prb(self):
        return self.bufC[:, C_PRB:C_PRB + 256].bitcast(F32)


def emit_load_biases(g):
    nc, p, h = g.nc, g.p, g.h
    p.entry_guard("syncq")
    p.op("syncq", lambda e: e.dma_start(g.fpD[0:8, 0:1024], h["b2"]),
         inc="dmaio", incn=16)
    p.wait("pool", "dmaio")
    p.op("pool", lambda e: e.tensor_copy(
        g.bufC[0:8, C_B2SB:C_B2SB + 1024], g.fpD[0:8, 0:1024]), inc="pool")
    bA32 = g.bufA[:, 0:8192].bitcast(F32)
    p.op("syncq", lambda e: e.dma_start(bA32[0:8, 0:4096], h["b1"]),
         inc="dmaio", incn=16)
    p.wait("pe", "dmaio")
    p.entry_guard("pe")
    with nc.psum_tensor(_u("pbias"), [128, 512], F32) as pi0:
        for grp in range(2):
            if grp:
                p.wait("pe", "dve")
            for k in range(16):
                hc = grp * 16 + k
                src = bA32[0:8, hc * 128:(hc + 1) * 128]
                p.op("pe", lambda e, s=src, d=pi0[:, k * 8:(k + 1) * 8]:
                     e.transpose(d, s, g.ident_f()), inc="pe")
            p.wait("dve", "pe")
            dstc = g.bufC[:, C_B1COLS + grp * 256:
                          C_B1COLS + (grp + 1) * 256].bitcast(F32)
            p.op("dve", lambda e, d=dstc: e.tensor_copy(d, pi0[:, 0:128]),
                 inc="dve")
    p.wait("syncq", "dve")
    p.wait("syncq", "pool")


def emit_init(g):
    nc, p, h = g.nc, g.p, g.h
    p.op("syncq", lambda e: e.dma_start(g.cb[:, :], h["cbf"]), inc="dmaio", incn=16)
    p.op("syncq", lambda e: e.dma_start(g.cf[:, :], h["cf32"]), inc="dmaio", incn=16)
    land3 = g.land(0).rearrange("p (c r) -> p c r", c=8)[:, :, 0:8]
    p.op("syncq", lambda e: e.dma_start(
        land3, h["wr"].rearrange("(c p) r -> p c r", c=8)), inc="dmaio", incn=16)
    p.wait("pool", "dmaio")
    p.op("pool", lambda e: e.tensor_copy(
        g.rW[:, :].rearrange("p (c r) -> p c r", r=8), land3), inc="pool")
    p.wait("syncq", "pool")
    p.op("syncq", lambda e: e.dma_start(g.fpD[0:1, 0:1024], h["bo"]),
         inc="dmaio", incn=16)
    p.wait("pe", "dmaio")
    with nc.psum_tensor(_u("pbo"), [128, 512], F32) as pbo:
        for oc in range(8):
            src = g.fpD[0:1, oc * 128:(oc + 1) * 128]
            p.op("pe", lambda e, s2=src, d=pbo[:, oc:oc + 1]: e.transpose(
                d, s2, g.ident_f()[0:1, 0:1]), inc="pe")
        p.wait("dve", "pe")
        p.op("dve", lambda e: e.tensor_copy(g.stats(32, 8), pbo[:, 0:8]),
             inc="dve")
    p.wait("syncq", "dve")
    emit_load_biases(g)


def emit_ln(g, seqs, from_input):
    nc, p, h = g.nc, g.p, g.h
    tcs = [(s, tc) for s in seqs for tc in range(16)]
    p.entry_guard("pe")
    with nc.psum_tensor(_u("lnpkA"), [128, 512], BF16) as pkA, \
         nc.psum_tensor(_u("lnpkB"), [128, 512], BF16) as pkB, \
         nc.psum_tensor(_u("lnsq"), [128, 1024], F32) as psq:
        pks = (pkA, pkB)
        hist = {0: ("dve", 0), 1: ("dve", 0)}
        sm = g.stats(0)
        sq = g.stats(1)
        rstd = g.stats(2)
        nmrs = g.stats(3)
        for s, tc in tcs:
            t0 = s * SEQ + tc * 128
            xtm = g.bufC[:, C_XTM:C_XTM + 1024]
            xlntm = g.bufC[:, C_XLNTM:C_XLNTM + 1024]
            if from_input:
                p.wait("syncq", "act")
                p.wait("syncq", "pool")
                p.wait("syncq", "dve")
                p.op("syncq", lambda e, t0=t0: e.dma_start(
                    g.land2(), h["x"][t0:t0 + 128, :]), inc="dmaio", incn=16)
                p.wait("dve", "dmaio")
                p.op("dve", lambda e: e.reduce_sum(sm, g.land2(), AX.X), inc="dve")
                p.wait("act", "dmaio")
                p.wait("act", "dve", p.cnt["dve"] - 1)
                p.op("act", lambda e: e.activation(psq[:, :], g.land2(), AF.Square),
                     inc="act")
                p.wait("dve", "act")
                p.op("dve", lambda e: e.reduce_sum(sq, psq[:, :], AX.X), inc="dve")
                p.wait("pool", "dmaio")
                p.wait("pool", "pe")
                p.op("pool", lambda e: e.tensor_copy(xtm, g.land2()), inc="pool")
                p.wait("pe", "pool")
                for k in range(2):
                    pk = pks[k]
                    heng, hcnt = hist[k]
                    p.wait("pe", heng, hcnt)
                    for j in range(4):
                        fc = k * 4 + j
                        p.op("pe", lambda e, pk=pk, j=j, fc=fc: e.transpose(
                            pk[:, j * 128:(j + 1) * 128],
                            xtm[:, fc * 128:(fc + 1) * 128], g.ident()), inc="pe")
                    p.wait("dve", "pe")
                    dst = g.xres.rearrange("p (f t) -> p f t", f=8)[
                        :, k * 4:(k + 1) * 4, t0:t0 + 128]
                    p.op("dve", lambda e, d=dst, pk=pk: e.tensor_copy(
                        d, pk[:, :].rearrange("p (j t) -> p j t", j=4)), inc="dve")
                    hist[k] = ("dve", p.cnt["dve"])
            else:
                for k in range(2):
                    pk = pks[k]
                    heng, hcnt = hist[k]
                    p.wait("pe", heng, hcnt)
                    if k == 0:
                        p.wait("pe", "dve")
                        p.wait("pe", "act")
                    for j in range(4):
                        fc = k * 4 + j
                        src = g.xres[:, fc * 4096 + t0:fc * 4096 + t0 + 128]
                        p.op("pe", lambda e, pk=pk, j=j, s=src: e.transpose(
                            pk[:, j * 128:(j + 1) * 128], s, g.ident()), inc="pe")
                    p.wait("act", "pe")
                    dst = xtm.rearrange("p (f t) -> p f t", f=8)[:, k * 4:(k + 1) * 4, :]
                    p.op("act", lambda e, d=dst, pk=pk: e.copy(
                        d, pk[:, :].rearrange("p (j t) -> p j t", j=4)), inc="act")
                    hist[k] = ("act", p.cnt["act"])
                p.wait("dve", "act")
                p.op("dve", lambda e: e.reduce_sum(sm, xtm, AX.X), inc="dve")
                p.wait("act", "act")
                p.wait("act", "dve", p.cnt["dve"] - 1)
                p.op("act", lambda e: e.activation(psq[:, :], xtm, AF.Square),
                     inc="act")
                p.wait("dve", "act")
                p.op("dve", lambda e: e.reduce_sum(sq, psq[:, :], AX.X), inc="dve")
            n = float(D)
            p.wait("dve", "dve")
            p.op("dve", lambda e: e.tensor_scalar(sm, sm, 1.0 / n, None,
                                                  ALU.mult, ALU.bypass), inc="dve")
            p.wait("dve", "dve")
            p.op("dve", lambda e: e.tensor_tensor(rstd, sm, sm, ALU.mult), inc="dve")
            p.op("dve", lambda e: e.tensor_scalar(sq, sq, 1.0 / n, None,
                                                  ALU.mult, ALU.bypass), inc="dve")
            p.wait("dve", "dve")
            p.op("dve", lambda e: e.tensor_scalar(rstd, rstd, EPS, None,
                                                  ALU.subtract, ALU.bypass),
                 inc="dve")
            p.wait("dve", "dve")
            p.op("dve", lambda e: e.tensor_tensor(sq, sq, rstd, ALU.subtract),
                 inc="dve")
            p.wait("act", "dve")
            p.op("act", lambda e: e.activation(rstd, sq, AF.Sqrt), inc="act")
            p.wait("dve", "act")
            p.op("dve", lambda e: e.reciprocal(rstd, rstd), inc="dve")
            p.wait("dve", "dve")
            p.op("dve", lambda e: e.tensor_tensor(nmrs, sm, rstd, ALU.mult),
                 inc="dve")
            p.wait("dve", "dve")
            p.op("dve", lambda e: e.tensor_scalar(nmrs, nmrs, -1.0, None,
                                                  ALU.mult, ALU.bypass), inc="dve")
            p.wait("dve", "dve")
            src_aff = g.land2() if from_input else xtm
            p.op("dve", lambda e, s=src_aff: e.tensor_scalar(
                xlntm, s, rstd, nmrs, ALU.mult, ALU.add), inc="dve")
            aff_cnt = p.cnt["dve"]
            for k in range(2):
                pk = pks[k]
                heng, hcnt = hist[k]
                p.wait("pe", heng, hcnt)
                p.wait("pe", "dve", aff_cnt)
                for j in range(4):
                    fc = k * 4 + j
                    p.op("pe", lambda e, pk=pk, j=j, fc=fc: e.transpose(
                        pk[:, j * 128:(j + 1) * 128],
                        xlntm[:, fc * 128:(fc + 1) * 128], g.ident()), inc="pe")
                p.wait("dve", "pe")
                dst = g.xln.rearrange("p (f t) -> p f t", f=8)[
                    :, k * 4:(k + 1) * 4, t0:t0 + 128]
                p.op("dve", lambda e, d=dst, pk=pk: e.tensor_copy(
                    d, pk[:, :].rearrange("p (j t) -> p j t", j=4)), inc="dve")
                hist[k] = ("dve", p.cnt["dve"])
        p.wait("syncq", "dve")


def emit_router(g):
    nc, p = g.nc, g.p
    p.entry_guard("pe")
    logT = g.fpD[0:8, 0:512]
    with nc.psum_tensor(_u("rlog"), [128, 512], F32) as plog, \
         nc.psum_tensor(_u("rtm"), [128, 512], F32) as ptm, \
         nc.psum_tensor(_u("rpk"), [128, 512], BF16) as ppk:
        for tt in range(NTT):
            if tt:
                p.wait("pe", "dve")
            for fc in range(FC):
                rhs = g.xln[:, fc * 4096 + tt * 512:fc * 4096 + (tt + 1) * 512]
                p.op("pe", lambda e, fc=fc, rhs=rhs: e.matmul(
                    plog[0:8, :], g.rW[:, fc * 8:(fc + 1) * 8], rhs,
                    start=(fc == 0), stop=(fc == 7)),
                     inc="pe" if fc == 7 else None)
            p.wait("dve", "pe")
            p.op("dve", lambda e: e.tensor_copy(logT, plog[0:8, :]), inc="dve")
            p.wait("pe", "dve")
            for k in range(4):
                p.op("pe", lambda e, k=k: e.transpose(
                    ptm[:, k * 8:(k + 1) * 8], logT[:, k * 128:(k + 1) * 128],
                    g.ident_f()), inc="pe")
            p.wait("dve", "pe")
            for k in range(4):
                tc = tt * 4 + k
                lg = ptm[:, k * 8:(k + 1) * 8]
                prb = g.prb()
                ex = prb[:, 0:8]
                ismax = prb[:, 8:16]
                msk = prb[:, 16:24]
                nm = g.stats(8)
                smx = g.stats(9)
                rs = g.stats(10)
                m2 = g.stats(11)
                p.wait("dve", "dve")
                p.op("dve", lambda e, lg=lg: e.tensor_reduce(
                    nm, lg, AX.X, ALU.max, negate=True), inc="dve")
                p.wait("act", "dve")
                p.op("act", lambda e, lg=lg, ex=ex: e.activation(
                    ex, lg, AF.Exp, bias=nm, scale=1.0, accum_out=smx), inc="act")
                p.wait("dve", "act")
                p.op("dve", lambda e: e.reciprocal(rs, smx), inc="dve")
                p.wait("dve", "dve")
                p.op("dve", lambda e, ex=ex: e.tensor_scalar(
                    ex, ex, rs, 0.5, ALU.mult, ALU.mult), inc="dve")
                p.wait("dve", "dve")
                p.op("dve", lambda e, ex=ex: e.tensor_reduce(
                    nm, ex, AX.X, ALU.max), inc="dve")
                p.wait("dve", "dve")
                p.op("dve", lambda e, ex=ex, ismax=ismax: e.tensor_scalar(
                    ismax, ex, nm, 1e30, ALU.is_ge, ALU.mult), inc="dve")
                p.wait("dve", "dve")
                p.op("dve", lambda e, ex=ex, ismax=ismax, msk=msk: e.tensor_tensor(
                    msk, ex, ismax, ALU.subtract), inc="dve")
                p.wait("dve", "dve")
                p.op("dve", lambda e, msk=msk: e.tensor_reduce(
                    m2, msk, AX.X, ALU.max), inc="dve")
                p.wait("dve", "dve")
                p.op("dve", lambda e, ex=ex, msk=msk: e.tensor_scalar(
                    msk, ex, m2, None, ALU.is_ge, ALU.bypass), inc="dve")
                p.wait("dve", "dve")
                wdst = g.wts_tm_f()[:, tc * 8:(tc + 1) * 8]
                p.op("dve", lambda e, wdst=wdst, ex=ex, msk=msk: e.tensor_tensor(
                    wdst, ex, msk, ALU.mult), inc="dve")
        wbf = g.bufC[:, C_VST:C_VST + 256]
        p.wait("pool", "dve")
        p.op("pool", lambda e: e.tensor_copy(wbf, g.wts_tm_f()), inc="pool")
        p.wait("pe", "pool")
        for grp in range(8):
            if grp:
                p.wait("pe", "dve")
            for k in range(4):
                tc = grp * 4 + k
                p.op("pe", lambda e, tc=tc, k=k: e.transpose(
                    ppk[0:8, k * 128:(k + 1) * 128],
                    wbf[:, tc * 8:(tc + 1) * 8], g.ident()), inc="pe")
            p.wait("dve", "pe")
            dst = g.bufC[0:8, C_PT + grp * 512:C_PT + (grp + 1) * 512]
            p.op("dve", lambda e, d=dst: e.tensor_copy(d, ppk[0:8, 0:512]),
                 inc="dve")
        p.wait("syncq", "dve")
        if DEBUG:
            p.op("syncq", lambda e: e.dma_start(
                g.h["dbg_wts"], g.wts_tm_f()), inc="dmaio", incn=16)
            p.wait("syncq", "dmaio")


def emit_ffn(g):
    nc, p, h = g.nc, g.p, g.h
    p.entry_guard("pe")
    p.entry_guard("syncq")
    with ExitStack() as ps:
        pt = lambda nm: ps.enter_context(nc.psum_tensor(_u(nm), [128, 512], F32))[:, :]
        pA = (pt("fA0"), pt("fA1"))
        pB = (pt("fB0"), pt("fB1"))
        pW = (pt("fW0"), pt("fW1"))

        conv_done = {}
        pe_done_unit = {}

        def emit_weight_load(u):
            e_, q8 = divmod(u, NQ8)
            half = (u % 2) * 8192
            first_conv = True
            for ch in range(16):
                slot = ch % 2
                if ch < 8:
                    fc = ch
                    src = h["w1"][e_, fc * 128:(fc + 1) * 128,
                                  q8 * 512:(q8 + 1) * 512]
                    dst = g.bufA[:, half + fc * 512:half + (fc + 1) * 512]
                else:
                    kk = ch - 8
                    hc, hlf = divmod(kk, 2)
                    base = half + 4096 + hc * 1024 + hlf * 512
                    src = h["w2"][e_, q8 * 512 + hc * 128:q8 * 512 + (hc + 1) * 128,
                                  hlf * 512:(hlf + 1) * 512]
                    dst = g.bufA[:, base:base + 512]
                p.wait("syncq", "pool", max(0, p.cnt["pool"] - 1))
                p.op("syncq", lambda e, s=src, sl=slot: e.dma_start(g.land(sl), s),
                     inc="dmaw", incn=16)
                if first_conv and u - 2 in pe_done_unit:
                    p.wait("pool", "pe", pe_done_unit[u - 2])
                first_conv = False
                p.wait("pool", "dmaw")
                p.op("pool", lambda e, d=dst, sl=slot: e.tensor_copy(d, g.land(sl)),
                     inc="pool")
            conv_done[u] = p.cnt["pool"]

        emit_weight_load(0)
        emit_weight_load(1)

        NU = E * NQ8
        hw_cols = lambda par, hc: g.bufB[:, par * 2048 + hc * 512:
                                         par * 2048 + (hc + 1) * 512]
        gtmp = lambda sl: g.uni[:, sl * 512:(sl + 1) * 512]

        stepc = 0
        hist_gelu = {}
        hist_add = {}
        hist_hw = {}
        hist_wb = {}
        pe_b_stop = {}

        for u in range(NU):
            e_, q8 = divmod(u, NQ8)
            half = (u % 2) * 8192
            W1 = lambda fc, hc, half=half: g.bufA[
                :, half + fc * 512 + hc * 128:half + fc * 512 + (hc + 1) * 128]
            W2 = lambda hc, oc, half=half: g.bufA[
                :, half + 4096 + hc * 1024 + oc * 128:
                half + 4096 + hc * 1024 + (oc + 1) * 128]
            p.wait("pe", "pool", conv_done[u])
            for tt in range(NTT):
                par = stepc % 2
                wbank = pW[par]
                if stepc - 2 in hist_wb:
                    p.wait("pe", "dve", hist_wb[stepc - 2])
                rhs_w = g.bufC[0:8, C_PT + tt * 512:C_PT + (tt + 1) * 512]
                p.op("pe", lambda e, wb=wbank, s=g.sel(e_), r=rhs_w: e.matmul(
                    wb, s, r, start=True, stop=True), inc="pe")
                for hc in range(HC4):
                    bank = pA[hc % 2]
                    p.wait("pe", "act", hist_gelu.get(hc % 2, 0))
                    for fc in range(FC):
                        rhs = g.xln[:, fc * 4096 + tt * 512:
                                    fc * 4096 + (tt + 1) * 512]
                        p.op("pe", lambda e, b=bank, w=W1(fc, hc), r=rhs, fc=fc:
                             e.matmul(b, w, r, start=(fc == 0), stop=(fc == 7)),
                             inc="pe" if fc == 7 else None)
                    pe_a = p.cnt["pe"]
                    p.wait("act", "pe", pe_a)
                    p.wait("act", "dve", hist_hw.get(hc % 2, 0))
                    b1c = g.b1col(e_, q8 * 4 + hc)
                    p.op("act", lambda e, b=bank, gt=gtmp(hc % 2), b1c=b1c:
                         e.activation(gt, b, AF.Gelu_apprx_tanh, bias=b1c,
                                      scale=1.0), inc="act")
                    hist_gelu[hc % 2] = p.cnt["act"]
                    p.wait("dve", "act")
                    if stepc - 2 in pe_b_stop:
                        p.wait("dve", "pe", pe_b_stop[stepc - 2])
                    dst = hw_cols(par, hc)
                    p.op("dve", lambda e, d=dst, gt=gtmp(hc % 2), wb=wbank:
                         e.tensor_tensor(d, gt, wb, ALU.mult), inc="dve")
                    hist_hw[hc % 2] = p.cnt["dve"]
                hist_wb[stepc] = p.cnt["dve"]
                p.wait("pe", "dve", p.cnt["dve"])
                for oc in range(FC):
                    bank = pB[oc % 2]
                    p.wait("pe", "dve", hist_add.get(oc % 2, 0))
                    for hc in range(HC4):
                        p.op("pe", lambda e, b=bank, w=W2(hc, oc),
                             r=hw_cols(par, hc), hc=hc: e.matmul(
                                 b, w, r, start=(hc == 0), stop=(hc == 3)),
                             inc="pe" if hc == 3 else None)
                    p.wait("dve", "pe", p.cnt["pe"])
                    xs = g.xres[:, oc * 4096 + tt * 512:oc * 4096 + (tt + 1) * 512]
                    p.op("dve", lambda e, x=xs, b=bank: e.tensor_tensor(
                        x, x, b, ALU.add), inc="dve")
                    hist_add[oc % 2] = p.cnt["dve"]
                pe_b_stop[stepc] = p.cnt["pe"]
                stepc += 1
            pe_done_unit[u] = p.cnt["pe"]
            if u + 2 < NU:
                emit_weight_load(u + 2)
        for tt in range(NTT):
            for fc in range(FC):
                bank = pW[(tt * FC + fc) % 2]
                p.wait("pe", "dve")
                b2l = g.bufC[0:8, C_B2SB + fc * 128:C_B2SB + (fc + 1) * 128]
                rhs_w = g.bufC[0:8, C_PT + tt * 512:C_PT + (tt + 1) * 512]
                p.op("pe", lambda e, b=bank, l=b2l, r=rhs_w: e.matmul(
                    b, l, r, start=True, stop=True), inc="pe")
                p.wait("dve", "pe")
                xs = g.xres[:, fc * 4096 + tt * 512:fc * 4096 + (tt + 1) * 512]
                p.op("dve", lambda e, x=xs, b=bank: e.tensor_tensor(
                    x, x, b, ALU.add), inc="dve")
        p.wait("syncq", "dve")


def emit_attn(g, seq):
    nc, p, h = g.nc, g.p, g.h
    s = seq
    o = 1 - seq
    unib = g.uni[:, :]
    lorB = lambda oc: unib[0:16, oc * 128:(oc + 1) * 128]
    lorA = lambda fc: unib[:, 1024 + fc * 16:1024 + (fc + 1) * 16]
    xcol = lambda fc, t0, w=512: g.xln[:, fc * 4096 + s * SEQ + t0:
                                       fc * 4096 + s * SEQ + t0 + w]
    kcol = lambda fc, t0, w=512: g.xln[:, fc * 4096 + o * SEQ + t0:
                                       fc * 4096 + o * SEQ + t0 + w]
    wstg = g.bufC[:, C_PB:C_PB + 1024]
    WT = lambda kc, oc: g.bufB[:, (kc * 8 + oc) * 128:(kc * 8 + oc + 1) * 128]
    u_sb = g.bufC[0:16, C_USB:C_USB + 512]
    vst = g.bufC[:, C_VST:C_VST + 512]
    PT = lambda kc: g.bufC[:, C_PT + kc * 256:C_PT + (kc + 1) * 256]
    OT = lambda c: g.bufC[:, C_OT + c * 256:C_OT + (c + 1) * 256]
    Pb = lambda sl: g.bufC[:, C_PB + sl * 2048:C_PB + (sl + 1) * 2048]
    Vtm = lambda c, kc: g.bufA[:, (c * NKC + kc) * 128:(c * NKC + kc + 1) * 128]

    p.entry_guard("pe")
    p.entry_guard("syncq")
    with ExitStack() as ps:
        pP = [ps.enter_context(nc.psum_tensor(_u(f"aP{i}"), [128, 512], F32))[:, :]
              for i in range(8)]

        def load_W(name):
            for ocg in range(8):
                for hlf in range(2):
                    p.wait("syncq", "pool")
                    src = h[name][ocg * 128:(ocg + 1) * 128,
                                  hlf * 512:(hlf + 1) * 512]
                    p.op("syncq", lambda e, sr=src, sl=hlf: e.dma_start(
                        g.land(sl), sr), inc="dmaw", incn=16)
                    p.wait("pool", "dmaw")
                    p.wait("pool", "pe")
                    dst = wstg[:, hlf * 512:(hlf + 1) * 512]
                    p.op("pool", lambda e, d=dst, sl=hlf: e.tensor_copy(
                        d, g.land(sl)), inc="pool")
                p.wait("pe", "pool")
                for fcg in range(2):
                    bank = pP[6 + fcg]
                    p.wait("pe", "dve")
                    for j in range(4):
                        fc = fcg * 4 + j
                        p.op("pe", lambda e, b=bank, j=j, fc=fc: e.transpose(
                            b.bitcast(BF16)[:, j * 128:(j + 1) * 128],
                            wstg[:, fc * 128:(fc + 1) * 128], g.ident()),
                             inc="pe")
                    p.wait("dve", "pe")
                    for j in range(4):
                        fc = fcg * 4 + j
                        p.op("dve", lambda e, b=bank, j=j, d=WT(fc, ocg):
                             e.tensor_copy(d, b.bitcast(BF16)[:, j * 128:(j + 1) * 128]),
                             inc="dve")

        def load_lora(wn_a, wn_b):
            p.wait("syncq", "pool")
            p.wait("syncq", "pe")
            p.op("syncq", lambda e: e.dma_start(g.fpD[0:16, 0:1024], h[wn_a]),
                 inc="dmaw", incn=16)
            p.wait("pool", "dmaw")
            p.wait("pool", "pe")
            astg = wstg[0:16, 0:1024]
            p.op("pool", lambda e: e.tensor_copy(astg, g.fpD[0:16, 0:1024]),
                 inc="pool")
            p.wait("pe", "pool")
            bank = pP[6]
            p.wait("pe", "dve")
            for fc in range(8):
                p.op("pe", lambda e, b=bank, fc=fc: e.transpose(
                    b.bitcast(BF16)[:, fc * 16:(fc + 1) * 16],
                    astg[:, fc * 128:(fc + 1) * 128], g.ident()[0:16, 0:16]),
                     inc="pe")
            p.wait("dve", "pe")
            p.op("dve", lambda e, b=bank: e.tensor_copy(
                unib[:, 1024:1152], b.bitcast(BF16)[:, 0:128]), inc="dve")
            p.wait("syncq", "pool")
            land3 = g.land(0).rearrange("p (c r) -> p c r", c=8)[:, :, 0:16]
            p.op("syncq", lambda e, l3=land3: e.dma_start(
                l3, h[wn_b].rearrange("(c p) r -> p c r", c=8)),
                 inc="dmaw", incn=16)
            p.wait("pool", "dmaw")
            p.wait("pool", "pe")
            bstg = wstg[:, 0:128]
            p.op("pool", lambda e, l3=land3: e.tensor_copy(
                bstg.rearrange("p (c r) -> p c r", c=8), l3), inc="pool")
            p.wait("pe", "pool")
            bank2 = pP[7]
            p.wait("pe", "dve")
            for oc in range(8):
                p.op("pe", lambda e, b=bank2, oc=oc: e.transpose(
                    b.bitcast(BF16)[0:16, oc * 128:(oc + 1) * 128],
                    bstg[:, oc * 16:(oc + 1) * 16], g.ident()), inc="pe")
            p.wait("dve", "pe")
            p.op("dve", lambda e, b=bank2: e.tensor_copy(
                unib[0:16, 0:1024], b.bitcast(BF16)[0:16, 0:1024]), inc="dve")

        def proj(kind):
            wn = {"k": "Wk", "v": "Wv", "q": "Wq"}[kind]
            load_W(wn)
            load_lora({"k": "Ak", "v": "Av", "q": "Aq"}[kind],
                      {"k": "Bk", "v": "Bv", "q": "Bq"}[kind])
            for tt in range(4):
                t0 = tt * 512
                bank_u = pP[0]
                p.wait("pe", "act")
                for fc in range(FC):
                    p.op("pe", lambda e, b=bank_u, w=lorA(fc), r=xcol(fc, t0),
                         fc=fc: e.matmul(b[0:16, :], w, r, start=(fc == 0),
                                         stop=(fc == 7)),
                         inc="pe" if fc == 7 else None)
                p.wait("act", "pe")
                p.op("act", lambda e, b=bank_u: e.activation(
                    u_sb, b[0:16, :], AF.Copy, scale=1.0 / RANK), inc="act")
                uc = p.cnt["act"]
                if kind == "q":
                    for oc in range(FC):
                        bank = pP[oc]
                        if oc == 0:
                            p.wait("pe", "act", uc)
                        for fc in range(FC):
                            p.op("pe", lambda e, b=bank, w=WT(fc, oc),
                                 r=xcol(fc, t0), fc=fc: e.matmul(
                                     b, w, r, start=(fc == 0), stop=False))
                        p.op("pe", lambda e, b=bank, w=lorB(oc): e.matmul(
                            b, w, u_sb, start=False, stop=True), inc="pe")
                    p.wait("act", "pe")
                    for oc in range(FC):
                        dst = xcol(oc, t0)
                        p.op("act", lambda e, b=pP[oc], d=dst: e.activation(
                            d, b, AF.Copy, scale=0.125), inc="act")
                else:
                    for oc in range(FC):
                        bank = pP[1 + (oc % 2)]
                        p.wait("pe", "act")
                        if kind == "v":
                            p.wait("pe", "dve")
                        for fc in range(FC):
                            p.op("pe", lambda e, b=bank, w=WT(fc, oc),
                                 r=xcol(fc, t0), fc=fc: e.matmul(
                                     b, w, r, start=(fc == 0), stop=False))
                        p.wait("pe", "act", uc)
                        p.op("pe", lambda e, b=bank, w=lorB(oc): e.matmul(
                            b, w, u_sb, start=False, stop=True), inc="pe")
                        p.wait("act", "pe")
                        if kind == "k":
                            dst = kcol(oc, t0)
                            p.op("act", lambda e, b=bank, d=dst: e.activation(
                                d, b, AF.Copy, scale=1.0), inc="act")
                        else:
                            p.wait("act", "pe")
                            p.op("act", lambda e, b=bank: e.activation(
                                vst, b, AF.Copy, scale=1.0), inc="act")
                            bankT = pP[3 + (oc % 2)]
                            p.wait("pe", "act")
                            p.wait("pe", "dve")
                            for j in range(4):
                                p.op("pe", lambda e, b=bankT, j=j: e.transpose(
                                    b.bitcast(BF16)[:, j * 128:(j + 1) * 128],
                                    vst[:, j * 128:(j + 1) * 128], g.ident()),
                                     inc="pe")
                            p.wait("dve", "pe")
                            for j in range(4):
                                kc = tt * 4 + j
                                p.op("dve", lambda e, b=bankT, j=j,
                                     d=Vtm(oc, kc): e.tensor_copy(
                                         d, b.bitcast(BF16)[:, j * 128:(j + 1) * 128]),
                                     inc="dve")

        proj("k")
        proj("v")
        proj("q")
        load_lora("Ao", "Bo")
        load_W("Wo")
        p.wait("syncq", "dve")
        p.wait("syncq", "act")

    p.entry_guard("pe")
    with nc.psum_tensor(_u("aSA"), [128, 2048], F32) as SAh, \
         nc.psum_tensor(_u("aSB"), [128, 2048], F32) as SBh:
        SA = SAh[:, :]
        SB = SBh[:, :]
        Ssets = (SA, SB)
        nsum = lambda i: g.stats(16 + i)
        negm = g.stats(20)
        ssum = g.stats(21)
        rs = g.stats(22)
        set_free_exp = [0, 0]
        set_pack_evac = [0, 0]
        pv_evac = 0
        op_add = [0, 0]

        for qg in range(NQG):
            q0g = qg * QG
            for c in range(PAIRS):
                for j in (0, 1):
                    prow = j * 64
                    for qt in (0, 1):
                        si = qt
                        St = Ssets[si]
                        q0 = q0g + qt * 128
                        p.wait("pe", "act", set_free_exp[si])
                        p.wait("pe", "dve", max(set_pack_evac[si], pv_evac,
                                                op_add[0], op_add[1]))
                        lhs = g.xln[prow:prow + 64,
                                    c * 4096 + s * SEQ + q0:
                                    c * 4096 + s * SEQ + q0 + 128]
                        for kb in range(4):
                            rhs = g.xln[prow:prow + 64,
                                        c * 4096 + o * SEQ + kb * 512:
                                        c * 4096 + o * SEQ + (kb + 1) * 512]
                            p.op("pe", lambda e, St=St, kb=kb, l=lhs, r=rhs:
                                 e.matmul(St[:, kb * 512:(kb + 1) * 512], l, r,
                                          start=True, stop=True),
                                 inc="pe" if kb == 3 else None)
                        p.wait("dve", "pe")
                        p.op("dve", lambda e, St=St: e.tensor_reduce(
                            negm, St[:, :], AX.X, ALU.max, negate=True),
                             inc="dve")
                        p.wait("act", "dve")
                        for kb in range(4):
                            p.op("act", lambda e, St=St, kb=kb:
                                 e.activation(
                                     St[:, kb * 512:(kb + 1) * 512],
                                     St[:, kb * 512:(kb + 1) * 512], AF.Exp,
                                     bias=negm, scale=1.0,
                                     accum_out=nsum(kb)), inc="act")
                        set_free_exp[si] = p.cnt["act"]
                        p.wait("dve", "act")
                        p.op("dve", lambda e: e.tensor_tensor(
                            nsum(0), nsum(0), nsum(1), ALU.add), inc="dve")
                        p.op("dve", lambda e: e.tensor_tensor(
                            nsum(2), nsum(2), nsum(3), ALU.add), inc="dve")
                        p.wait("dve", "dve")
                        p.op("dve", lambda e: e.tensor_tensor(
                            ssum, nsum(0), nsum(2), ALU.add), inc="dve")
                        p.wait("dve", "dve")
                        p.op("dve", lambda e: e.reciprocal(rs, ssum), inc="dve")
                        p.wait("dve", "dve")
                        p.op("dve", lambda e, si=si, St=St: e.tensor_scalar(
                            Pb(si), St[:, :], rs, None, ALU.mult, ALU.bypass),
                             inc="dve")
                        p.wait("pe", "dve")
                        for pk in range(4):
                            for jj in range(4):
                                kc = pk * 4 + jj
                                p.op("pe", lambda e, St=St, pk=pk, jj=jj,
                                     kc=kc, si=si: e.transpose(
                                         St.bitcast(BF16)[
                                             :, pk * 1024 + jj * 128:
                                             pk * 1024 + (jj + 1) * 128],
                                         Pb(si)[:, kc * 128:(kc + 1) * 128],
                                         g.ident()),
                                     inc="pe" if jj == 3 else None)
                            p.wait("dve", "pe")
                            base = C_PT + pk * 4 * 256
                            dst = g.bufC[:, base:base + 1024].rearrange(
                                "p (k v) -> p k v", v=256)[:, :, qt * 128:
                                                           qt * 128 + 128]
                            srcv = St.bitcast(BF16)[:, pk * 1024:pk * 1024 + 512]
                            p.op("dve", lambda e, d=dst, sv=srcv: e.tensor_copy(
                                d, sv.rearrange("p (k v) -> p k v", v=128)),
                                 inc="dve")
                        set_pack_evac[si] = p.cnt["dve"]
                    p.wait("pe", "dve", max(set_pack_evac[0], set_pack_evac[1],
                                            pv_evac))
                    for kc in range(NKC):
                        p.op("pe", lambda e, kc=kc, c=c, j=j: e.matmul(
                            SA[j * 64:(j + 1) * 64, 0:256],
                            Vtm(c, kc)[:, j * 64:(j + 1) * 64],
                            PT(kc), start=(kc == 0), stop=(kc == 15)),
                             inc="pe" if kc == 15 else None)
                    p.wait("dve", "pe")
                    dsto = OT(c)[j * 64:(j + 1) * 64, :]
                    p.op("dve", lambda e, d=dsto, j=j: e.tensor_copy(
                        d, SA[j * 64:(j + 1) * 64, 0:256]), inc="dve")
                    pv_evac = p.cnt["dve"]
            bank_u = SB
            p.wait("pe", "act", set_free_exp[1])
            p.wait("pe", "dve", max(pv_evac, set_pack_evac[1]))
            for pc in range(PAIRS):
                p.op("pe", lambda e, pc=pc: e.matmul(
                    bank_u[0:16, 0:256], lorA(pc), OT(pc),
                    start=(pc == 0), stop=(pc == 7)),
                     inc="pe" if pc == 7 else None)
            p.wait("act", "pe")
            p.op("act", lambda e: e.activation(
                u_sb[:, 0:256], bank_u[0:16, 0:256], AF.Copy, scale=1.0 / RANK),
                 inc="act")
            uc2 = p.cnt["act"]
            for oc in range(FC):
                off = (oc % 2) * 512
                if oc < 2:
                    p.wait("pe", "act", set_free_exp[0] if oc == 0 else 0)
                    p.wait("pe", "dve", max(set_pack_evac[0], pv_evac))
                    if oc == 0:
                        p.wait("pe", "act", uc2)
                else:
                    p.wait("pe", "dve", op_add[oc % 2])
                for pc in range(PAIRS):
                    p.op("pe", lambda e, oc=oc, pc=pc, off=off: e.matmul(
                        SA[:, off:off + 256], WT(pc, oc), OT(pc),
                        start=(pc == 0), stop=False))
                p.op("pe", lambda e, oc=oc, off=off: e.matmul(
                    SA[:, off:off + 256], lorB(oc), u_sb[:, 0:256],
                    start=False, stop=True), inc="pe")
                p.wait("dve", "pe")
                p.wait("dve", "dve")
                bocol = g.stats(32 + oc)
                p.op("dve", lambda e, oc=oc, off=off, bc=bocol: e.tensor_scalar(
                    vst[:, 0:256], SA[:, off:off + 256], bc, None,
                    ALU.add, ALU.bypass), inc="dve")
                p.wait("dve", "dve")
                xs = g.xres[:, oc * 4096 + s * SEQ + q0g:
                            oc * 4096 + s * SEQ + q0g + 256]
                p.op("dve", lambda e, x=xs: e.tensor_tensor(
                    x, x, vst[:, 0:256], ALU.add), inc="dve")
                op_add[oc % 2] = p.cnt["dve"]
        p.wait("syncq", "dve")


def emit_debug_dump(g, sbuf, name):
    p = g.p
    p.entry_guard("syncq")
    p.op("syncq", lambda e: e.dma_start(g.h[name], sbuf[:, :]),
         inc="dmaio", incn=16)
    p.wait("syncq", "dmaio")


def emit_store(g):
    nc, p, h = g.nc, g.p, g.h
    p.entry_guard("pe")
    with nc.psum_tensor(_u("spk0"), [128, 512], BF16) as pk0, \
         nc.psum_tensor(_u("spk1"), [128, 512], BF16) as pk1:
        pks = (pk0, pk1)
        for tc in range(NTC):
            t0 = tc * 128
            for k in range(2):
                pk = pks[k]
                p.wait("pe", "act")
                for j in range(4):
                    fc = k * 4 + j
                    src = g.xres[:, fc * 4096 + t0:fc * 4096 + t0 + 128]
                    p.op("pe", lambda e, pk=pk, j=j, srt=src: e.transpose(
                        pk[:, j * 128:(j + 1) * 128], srt, g.ident()), inc="pe")
                p.wait("act", "pe")
                p.wait("act", "st")
                dst = g.fpD[:, k * 512:(k + 1) * 512]
                p.op("act", lambda e, d=dst, pk=pk: e.copy(d, pk[:, :]),
                     inc="act")
            p.wait("syncq", "act")
            p.op("syncq", lambda e, t0=t0: e.dma_start(
                h["out"][t0:t0 + 128, :], g.fpD[:, 0:1024]), inc="st", incn=16)
        p.wait("syncq", "st")


def build_nc():
    nc = bass.Bass("TRN2", target_bir_lowering=False)
    h = {}
    h["x"] = _hbm(nc, "x", (T, D), F32)
    for w in ("Wq", "Wk", "Wv", "Wo"):
        h[w] = _hbm(nc, w, (D, D), F32)
    h["bo"] = _hbm(nc, "bo", (1, D), F32)
    for a in ("Aq", "Ak", "Av", "Ao"):
        h[a] = _hbm(nc, a, (RANK, D), F32)
    for b in ("Bq", "Bk", "Bv", "Bo"):
        h[b] = _hbm(nc, b, (D, RANK), F32)
    h["wr"] = _hbm(nc, "wr", (D, E), F32)
    h["w1"] = _hbm(nc, "w1", (E, D, 4 * D), F32)
    h["b1"] = _hbm(nc, "b1", (E, 4 * D), F32)
    h["w2"] = _hbm(nc, "w2", (E, 4 * D, D), F32)
    h["b2"] = _hbm(nc, "b2", (E, D), F32)
    h["cbf"] = _hbm(nc, "cbf", (128, 1408), BF16)
    h["cf32"] = _hbm(nc, "cf32", (8, 8), F32)
    h["out"] = _hbm(nc, "out", (T, D), F32, out=True)
    if DEBUG:
        h["dbg_xln"] = _hbm(nc, "dbg_xln", (128, 32768), BF16, out=True)
        h["dbg_x1"] = _hbm(nc, "dbg_x1", (128, 32768), BF16, out=True)
        h["dbg_x2"] = _hbm(nc, "dbg_x2", (128, 32768), BF16, out=True)
        h["dbg_wts"] = _hbm(nc, "dbg_wts", (128, 256), F32, out=True)
        h["dbg_q"] = _hbm(nc, "dbg_q", (128, 32768), BF16, out=True)
        h["dbg_vtm"] = _hbm(nc, "dbg_vtm", (128, 16384), BF16, out=True)
        h["dbg_bufc"] = _hbm(nc, "dbg_bufc", (128, CB_COLS), BF16, out=True)

    with ExitStack() as stack:
        sb = lambda name, shape, dt: stack.enter_context(
            nc.sbuf_tensor(name, shape, dt))
        bufs = dict(
            xres=sb("xres", [128, 32768], BF16),
            xln=sb("xln", [128, 32768], BF16),
            bufA=sb("bufA", [128, 16384], BF16),
            bufB=sb("bufB", [128, 8192], BF16),
            bufC=sb("bufC", [128, CB_COLS], BF16),
            fpD=sb("fpD", [128, 1088], F32),
            uni=sb("uni", [128, 1152], BF16),
            cb=sb("cb", [128, 1408], BF16),
            cf=sb("cf", [8, 8], F32),
            rW=sb("rW", [128, 64], BF16),
        )
        p = Prog(nc, stack)
        g = G(nc, p, h, **bufs)

        emit_init(g)
        p.flush_block("init")
        emit_ln(g, seqs=(0, 1), from_input=True)
        p.flush_block("ln1")
        if DEBUG:
            emit_debug_dump(g, g.xln, "dbg_xln")
            p.flush_block("dbgln")
        emit_router(g)
        p.flush_block("rt1")
        emit_ffn(g)
        p.flush_block("ffn1")
        if DEBUG:
            emit_debug_dump(g, g.xres, "dbg_x1")
            p.flush_block("dbg1")
        for s in (0, 1):
            emit_ln(g, seqs=(s,), from_input=False)
            p.flush_block(f"ln2s{s}")
            emit_attn(g, seq=s)
            p.flush_block(f"attns{s}")
            if DEBUG and s == 0:
                emit_debug_dump(g, g.xln, "dbg_q")
                emit_debug_dump(g, g.bufA, "dbg_vtm")
                emit_debug_dump(g, g.bufC, "dbg_bufc")
                p.flush_block("dbgattn")
        if DEBUG:
            emit_debug_dump(g, g.xres, "dbg_x2")
            p.flush_block("dbg2")
        emit_ln(g, seqs=(0, 1), from_input=False)
        p.flush_block("ln3")
        emit_load_biases(g)
        p.flush_block("bias2")
        emit_router(g)
        p.flush_block("rt3")
        emit_ffn(g)
        p.flush_block("ffn2")
        emit_store(g)
        p.flush_block("store")
    return nc


def _make_consts():
    cbf = np.zeros((128, 1408), dtype=ml_dtypes.bfloat16)
    cbf[:, CONST_ID:CONST_ID + 128] = np.eye(128, dtype=ml_dtypes.bfloat16)
    for e in range(8):
        cbf[e, CONST_SEL + e * 128:CONST_SEL + (e + 1) * 128] = 1.0
    cbf[0, CONST_ONES:CONST_ONES + 256] = 1.0
    cf32 = np.eye(8, dtype=np.float32)
    return cbf, cf32


_NC_CACHE = [None]


def kernel(L0_attn_x, L0_attn_Wq, L0_attn_Wk, L0_attn_Wv, L0_attn_Wo, L0_attn_bo,
           L0_attn_Aq, L0_attn_Bq, L0_attn_Ak, L0_attn_Bk, L0_attn_Av, L0_attn_Bv,
           L0_attn_Ao, L0_attn_Bo, L0_ffn_w_router, L0_ffn_w1, L0_ffn_b1,
           L0_ffn_w2, L0_ffn_b2):
    x = np.ascontiguousarray(np.asarray(L0_attn_x, dtype=np.float32))
    cbf, cf32 = _make_consts()
    f32c = lambda a: np.ascontiguousarray(np.asarray(a, dtype=np.float32))
    shared = {
        "Wq": f32c(L0_attn_Wq), "Wk": f32c(L0_attn_Wk),
        "Wv": f32c(L0_attn_Wv), "Wo": f32c(L0_attn_Wo),
        "bo": f32c(L0_attn_bo).reshape(1, D),
        "Aq": f32c(L0_attn_Aq), "Ak": f32c(L0_attn_Ak),
        "Av": f32c(L0_attn_Av), "Ao": f32c(L0_attn_Ao),
        "Bq": f32c(L0_attn_Bq), "Bk": f32c(L0_attn_Bk),
        "Bv": f32c(L0_attn_Bv), "Bo": f32c(L0_attn_Bo),
        "wr": f32c(L0_ffn_w_router),
        "w1": f32c(L0_ffn_w1), "b1": f32c(L0_ffn_b1),
        "w2": f32c(L0_ffn_w2), "b2": f32c(L0_ffn_b2),
        "cbf": cbf, "cf32": cf32,
    }
    if _NC_CACHE[0] is None:
        _NC_CACHE[0] = build_nc()
    nc = _NC_CACHE[0]
    in_maps = []
    for c in range(N_CORES):
        m = dict(shared)
        m["x"] = np.ascontiguousarray(x[2 * c:2 * c + 2].reshape(T, D))
        in_maps.append(m)
    res = run_bass_kernel_spmd(nc, in_maps, core_ids=list(range(N_CORES)))
    outs = [np.asarray(r["out"]).reshape(2, SEQ, D) for r in res.results]
    return np.concatenate(outs, axis=0).astype(np.float32)


# version 09
# speedup vs baseline: 1.6563x; 1.2103x over previous
import numpy as np
import ml_dtypes
import concourse.bass as bass
import concourse.mybir as mybir
from concourse.bass_utils import run_bass_kernel_spmd
from contextlib import ExitStack

F32 = mybir.dt.float32
BF16 = mybir.dt.bfloat16
AF = mybir.ActivationFunctionType
ALU = mybir.AluOpType
AX = mybir.AxisListType

DEBUG = 0
N_CORES = 8
T = 4096
SEQ = 2048
D = 1024
FC = 8
NTT = 8
NTC = 32
E = 8
NQ8 = 8
HC4 = 4
RANK = 16
PAIRS = 8
NKC = 16
QG = 256
NQG = 8
EPS = 1e-5

C_PT = 0
C_PB = 4096
C_OT = 8192
C_VST = 10240
C_USB = 10752
CB_COLS = 11264
C_XLNTM = 4096
C_XTM = 5120
C_B2SB = 8192
C_B1COLS = 9216
C_WTSTM = 9728
C_PRB = 10240

CONST_ID = 0
CONST_SEL = 128
CONST_ONES = 1152

D_LAND = 0
D_STATS = 1024


_UID = [0]


def _u(name):
    _UID[0] += 1
    return f"{name}_{_UID[0]}"


def _hbm(nc, name, shape, dtype, out=False):
    return nc.dram_tensor(name, list(shape), dtype,
                          kind="ExternalOutput" if out else "ExternalInput").ap()


class Prog:
    ENGS = ("pe", "act", "dve", "pool", "syncq")

    def __init__(self, nc, stack):
        self.nc = nc
        self.stack = stack
        self.ops = {e: [] for e in self.ENGS}
        self.sems = {}
        self.cnt = {}

    def new_sems(self, names):
        for n in names:
            self.sems[n] = self.stack.enter_context(self.nc.semaphore(n))
            self.cnt[n] = 0

    def op(self, eng, fn, inc=None, incn=1):
        if inc is None:
            self.ops[eng].append(lambda e, fn=fn: fn(e))
        else:
            self.cnt[inc] += incn
            s = self.sems[inc]
            self.ops[eng].append(
                lambda e, fn=fn, s=s, incn=incn: fn(e).then_inc(s, incn))

    def wait(self, eng, sem, val=None):
        if val is None:
            val = self.cnt[sem]
        if val <= 0:
            return
        s = self.sems[sem]
        self.ops[eng].append(lambda e, s=s, v=val: e.wait_ge(s, v))

    def entry_guard(self, eng="pe"):
        for s in ("act", "dve", "pool"):
            self.wait(eng, s)

    def flush_block(self, name):
        nc = self.nc
        with nc.Block(name) as blk:
            for eng, deco in (("pe", blk.tensor), ("act", blk.scalar),
                              ("dve", blk.vector), ("pool", blk.gpsimd),
                              ("syncq", blk.sync)):
                ops = self.ops[eng]
                if not ops:
                    continue

                def body(e, ops=ops):
                    for f in ops:
                        f(e)
                deco(body)
        self.ops = {e: [] for e in self.ENGS}


class G:
    def __init__(self, nc, p, h, **bufs):
        self.nc, self.p, self.h = nc, p, h
        self.__dict__.update(bufs)
        p.new_sems(["dmaw", "dmaio", "pe", "act", "dve", "pool", "st"])

    def ident(self):
        return self.cb[:, CONST_ID:CONST_ID + 128]

    def ident_f(self):
        return self.cf[0:8, 0:8]

    def sel(self, e):
        return self.cb[0:8, CONST_SEL + e * 128:CONST_SEL + (e + 1) * 128]

    def ones_row(self, n):
        return self.cb[0:1, CONST_ONES:CONST_ONES + n]

    def stats(self, i, w=1):
        return self.fpD[:, D_STATS + i:D_STATS + i + w]

    def land(self, slot):
        return self.fpD[:, D_LAND + slot * 512:D_LAND + (slot + 1) * 512]

    def land2(self):
        return self.fpD[:, D_LAND:D_LAND + 1024]

    def b1col(self, e, hc_abs):
        v = self.bufC[:, C_B1COLS:C_B1COLS + 512].bitcast(F32)
        return v[:, hc_abs * 8 + e:hc_abs * 8 + e + 1]

    def wts_tm_f(self):
        return self.bufC[:, C_WTSTM:C_WTSTM + 512].bitcast(F32)

    def prb(self):
        return self.bufC[:, C_PRB:C_PRB + 256].bitcast(F32)


def emit_load_biases(g):
    nc, p, h = g.nc, g.p, g.h
    p.entry_guard("syncq")
    p.op("syncq", lambda e: e.dma_start(g.fpD[0:8, 0:1024], h["b2"]),
         inc="dmaio", incn=16)
    p.wait("pool", "dmaio")
    p.op("pool", lambda e: e.tensor_copy(
        g.bufC[0:8, C_B2SB:C_B2SB + 1024], g.fpD[0:8, 0:1024]), inc="pool")
    bA32 = g.bufA[:, 0:8192].bitcast(F32)
    p.op("syncq", lambda e: e.dma_start(bA32[0:8, 0:4096], h["b1"]),
         inc="dmaio", incn=16)
    p.wait("pe", "dmaio")
    p.entry_guard("pe")
    with nc.psum_tensor(_u("pbias"), [128, 512], F32) as pi0:
        for grp in range(2):
            if grp:
                p.wait("pe", "dve")
            for k in range(16):
                hc = grp * 16 + k
                src = bA32[0:8, hc * 128:(hc + 1) * 128]
                p.op("pe", lambda e, s=src, d=pi0[:, k * 8:(k + 1) * 8]:
                     e.transpose(d, s, g.ident_f()), inc="pe")
            p.wait("dve", "pe")
            dstc = g.bufC[:, C_B1COLS + grp * 256:
                          C_B1COLS + (grp + 1) * 256].bitcast(F32)
            p.op("dve", lambda e, d=dstc: e.tensor_copy(d, pi0[:, 0:128]),
                 inc="dve")
    p.wait("syncq", "dve")
    p.wait("syncq", "pool")


def emit_init(g):
    nc, p, h = g.nc, g.p, g.h
    p.op("syncq", lambda e: e.dma_start(g.cb[:, :], h["cbf"]), inc="dmaio", incn=16)
    p.op("syncq", lambda e: e.dma_start(g.cf[:, :], h["cf32"]), inc="dmaio", incn=16)
    land3 = g.land(0).rearrange("p (c r) -> p c r", c=8)[:, :, 0:8]
    p.op("syncq", lambda e: e.dma_start(
        land3, h["wr"].rearrange("(c p) r -> p c r", c=8)), inc="dmaio", incn=16)
    p.wait("pool", "dmaio")
    p.op("pool", lambda e: e.tensor_copy(
        g.rW[:, :].rearrange("p (c r) -> p c r", r=8), land3), inc="pool")
    p.wait("syncq", "pool")
    p.op("syncq", lambda e: e.dma_start(g.fpD[0:1, 0:1024], h["bo"]),
         inc="dmaio", incn=16)
    p.wait("pe", "dmaio")
    with nc.psum_tensor(_u("pbo"), [128, 512], F32) as pbo:
        for oc in range(8):
            src = g.fpD[0:1, oc * 128:(oc + 1) * 128]
            p.op("pe", lambda e, s2=src, d=pbo[:, oc:oc + 1]: e.transpose(
                d, s2, g.ident_f()[0:1, 0:1]), inc="pe")
        p.wait("dve", "pe")
        p.op("dve", lambda e: e.tensor_copy(g.stats(32, 8), pbo[:, 0:8]),
             inc="dve")
    p.wait("syncq", "dve")
    emit_load_biases(g)


def emit_ln(g, seqs, from_input):
    nc, p, h = g.nc, g.p, g.h
    tcs = [(s, tc) for s in seqs for tc in range(16)]
    p.entry_guard("pe")
    with nc.psum_tensor(_u("lnpkA"), [128, 512], BF16) as pkA, \
         nc.psum_tensor(_u("lnpkB"), [128, 512], BF16) as pkB, \
         nc.psum_tensor(_u("lnsq"), [128, 1024], F32) as psq:
        pks = (pkA, pkB)
        hist = {0: ("dve", 0), 1: ("dve", 0)}
        sm = g.stats(0)
        sq = g.stats(1)
        rstd = g.stats(2)
        nmrs = g.stats(3)
        for s, tc in tcs:
            t0 = s * SEQ + tc * 128
            xtm = g.bufC[:, C_XTM:C_XTM + 1024]
            xlntm = g.bufC[:, C_XLNTM:C_XLNTM + 1024]
            if from_input:
                p.wait("syncq", "act")
                p.wait("syncq", "pool")
                p.wait("syncq", "dve")
                p.op("syncq", lambda e, t0=t0: e.dma_start(
                    g.land2(), h["x"][t0:t0 + 128, :]), inc="dmaio", incn=16)
                p.wait("dve", "dmaio")
                p.op("dve", lambda e: e.reduce_sum(sm, g.land2(), AX.X), inc="dve")
                p.wait("act", "dmaio")
                p.wait("act", "dve", p.cnt["dve"] - 1)
                p.op("act", lambda e: e.activation(psq[:, :], g.land2(), AF.Square),
                     inc="act")
                p.wait("dve", "act")
                p.op("dve", lambda e: e.reduce_sum(sq, psq[:, :], AX.X), inc="dve")
                p.wait("pool", "dmaio")
                p.wait("pool", "pe")
                p.op("pool", lambda e: e.tensor_copy(xtm, g.land2()), inc="pool")
                p.wait("pe", "pool")
                for k in range(2):
                    pk = pks[k]
                    heng, hcnt = hist[k]
                    p.wait("pe", heng, hcnt)
                    for j in range(4):
                        fc = k * 4 + j
                        p.op("pe", lambda e, pk=pk, j=j, fc=fc: e.transpose(
                            pk[:, j * 128:(j + 1) * 128],
                            xtm[:, fc * 128:(fc + 1) * 128], g.ident()), inc="pe")
                    p.wait("dve", "pe")
                    dst = g.xres.rearrange("p (f t) -> p f t", f=8)[
                        :, k * 4:(k + 1) * 4, t0:t0 + 128]
                    p.op("dve", lambda e, d=dst, pk=pk: e.tensor_copy(
                        d, pk[:, :].rearrange("p (j t) -> p j t", j=4)), inc="dve")
                    hist[k] = ("dve", p.cnt["dve"])
            else:
                for k in range(2):
                    pk = pks[k]
                    heng, hcnt = hist[k]
                    p.wait("pe", heng, hcnt)
                    if k == 0:
                        p.wait("pe", "dve")
                        p.wait("pe", "act")
                    for j in range(4):
                        fc = k * 4 + j
                        src = g.xres[:, fc * 4096 + t0:fc * 4096 + t0 + 128]
                        p.op("pe", lambda e, pk=pk, j=j, s=src: e.transpose(
                            pk[:, j * 128:(j + 1) * 128], s, g.ident()), inc="pe")
                    p.wait("act", "pe")
                    dst = xtm.rearrange("p (f t) -> p f t", f=8)[:, k * 4:(k + 1) * 4, :]
                    p.op("act", lambda e, d=dst, pk=pk: e.copy(
                        d, pk[:, :].rearrange("p (j t) -> p j t", j=4)), inc="act")
                    hist[k] = ("act", p.cnt["act"])
                p.wait("dve", "act")
                p.op("dve", lambda e: e.reduce_sum(sm, xtm, AX.X), inc="dve")
                p.wait("act", "act")
                p.wait("act", "dve", p.cnt["dve"] - 1)
                p.op("act", lambda e: e.activation(psq[:, :], xtm, AF.Square),
                     inc="act")
                p.wait("dve", "act")
                p.op("dve", lambda e: e.reduce_sum(sq, psq[:, :], AX.X), inc="dve")
            n = float(D)
            p.wait("dve", "dve")
            p.op("dve", lambda e: e.tensor_scalar(sm, sm, 1.0 / n, None,
                                                  ALU.mult, ALU.bypass), inc="dve")
            p.wait("dve", "dve")
            p.op("dve", lambda e: e.tensor_tensor(rstd, sm, sm, ALU.mult), inc="dve")
            p.op("dve", lambda e: e.tensor_scalar(sq, sq, 1.0 / n, None,
                                                  ALU.mult, ALU.bypass), inc="dve")
            p.wait("dve", "dve")
            p.op("dve", lambda e: e.tensor_scalar(rstd, rstd, EPS, None,
                                                  ALU.subtract, ALU.bypass),
                 inc="dve")
            p.wait("dve", "dve")
            p.op("dve", lambda e: e.tensor_tensor(sq, sq, rstd, ALU.subtract),
                 inc="dve")
            p.wait("act", "dve")
            p.op("act", lambda e: e.activation(rstd, sq, AF.Sqrt), inc="act")
            p.wait("dve", "act")
            p.op("dve", lambda e: e.reciprocal(rstd, rstd), inc="dve")
            p.wait("dve", "dve")
            p.op("dve", lambda e: e.tensor_tensor(nmrs, sm, rstd, ALU.mult),
                 inc="dve")
            p.wait("dve", "dve")
            p.op("dve", lambda e: e.tensor_scalar(nmrs, nmrs, -1.0, None,
                                                  ALU.mult, ALU.bypass), inc="dve")
            p.wait("dve", "dve")
            src_aff = g.land2() if from_input else xtm
            p.op("dve", lambda e, s=src_aff: e.tensor_scalar(
                xlntm, s, rstd, nmrs, ALU.mult, ALU.add), inc="dve")
            aff_cnt = p.cnt["dve"]
            for k in range(2):
                pk = pks[k]
                heng, hcnt = hist[k]
                p.wait("pe", heng, hcnt)
                p.wait("pe", "dve", aff_cnt)
                for j in range(4):
                    fc = k * 4 + j
                    p.op("pe", lambda e, pk=pk, j=j, fc=fc: e.transpose(
                        pk[:, j * 128:(j + 1) * 128],
                        xlntm[:, fc * 128:(fc + 1) * 128], g.ident()), inc="pe")
                p.wait("dve", "pe")
                dst = g.xln.rearrange("p (f t) -> p f t", f=8)[
                    :, k * 4:(k + 1) * 4, t0:t0 + 128]
                p.op("dve", lambda e, d=dst, pk=pk: e.tensor_copy(
                    d, pk[:, :].rearrange("p (j t) -> p j t", j=4)), inc="dve")
                hist[k] = ("dve", p.cnt["dve"])
        p.wait("syncq", "dve")


def emit_router(g):
    nc, p = g.nc, g.p
    p.entry_guard("pe")
    logT = g.fpD[0:8, 0:512]
    with nc.psum_tensor(_u("rlog"), [128, 512], F32) as plog, \
         nc.psum_tensor(_u("rtm"), [128, 512], F32) as ptm, \
         nc.psum_tensor(_u("rpk"), [128, 512], BF16) as ppk:
        for tt in range(NTT):
            if tt:
                p.wait("pe", "dve")
            for fc in range(FC):
                rhs = g.xln[:, fc * 4096 + tt * 512:fc * 4096 + (tt + 1) * 512]
                p.op("pe", lambda e, fc=fc, rhs=rhs: e.matmul(
                    plog[0:8, :], g.rW[:, fc * 8:(fc + 1) * 8], rhs,
                    start=(fc == 0), stop=(fc == 7)),
                     inc="pe" if fc == 7 else None)
            p.wait("dve", "pe")
            p.op("dve", lambda e: e.tensor_copy(logT, plog[0:8, :]), inc="dve")
            p.wait("pe", "dve")
            for k in range(4):
                p.op("pe", lambda e, k=k: e.transpose(
                    ptm[:, k * 8:(k + 1) * 8], logT[:, k * 128:(k + 1) * 128],
                    g.ident_f()), inc="pe")
            p.wait("dve", "pe")
            for k in range(4):
                tc = tt * 4 + k
                lg = ptm[:, k * 8:(k + 1) * 8]
                prb = g.prb()
                ex = prb[:, 0:8]
                ismax = prb[:, 8:16]
                msk = prb[:, 16:24]
                nm = g.stats(8)
                smx = g.stats(9)
                rs = g.stats(10)
                m2 = g.stats(11)
                p.wait("dve", "dve")
                p.op("dve", lambda e, lg=lg: e.tensor_reduce(
                    nm, lg, AX.X, ALU.max, negate=True), inc="dve")
                p.wait("act", "dve")
                p.op("act", lambda e, lg=lg, ex=ex: e.activation(
                    ex, lg, AF.Exp, bias=nm, scale=1.0, accum_out=smx), inc="act")
                p.wait("dve", "act")
                p.op("dve", lambda e: e.reciprocal(rs, smx), inc="dve")
                p.wait("dve", "dve")
                p.op("dve", lambda e, ex=ex: e.tensor_scalar(
                    ex, ex, rs, 0.5, ALU.mult, ALU.mult), inc="dve")
                p.wait("dve", "dve")
                p.op("dve", lambda e, ex=ex: e.tensor_reduce(
                    nm, ex, AX.X, ALU.max), inc="dve")
                p.wait("dve", "dve")
                p.op("dve", lambda e, ex=ex, ismax=ismax: e.tensor_scalar(
                    ismax, ex, nm, 1e30, ALU.is_ge, ALU.mult), inc="dve")
                p.wait("dve", "dve")
                p.op("dve", lambda e, ex=ex, ismax=ismax, msk=msk: e.tensor_tensor(
                    msk, ex, ismax, ALU.subtract), inc="dve")
                p.wait("dve", "dve")
                p.op("dve", lambda e, msk=msk: e.tensor_reduce(
                    m2, msk, AX.X, ALU.max), inc="dve")
                p.wait("dve", "dve")
                p.op("dve", lambda e, ex=ex, msk=msk: e.tensor_scalar(
                    msk, ex, m2, None, ALU.is_ge, ALU.bypass), inc="dve")
                p.wait("dve", "dve")
                wdst = g.wts_tm_f()[:, tc * 8:(tc + 1) * 8]
                p.op("dve", lambda e, wdst=wdst, ex=ex, msk=msk: e.tensor_tensor(
                    wdst, ex, msk, ALU.mult), inc="dve")
        wbf = g.bufC[:, C_VST:C_VST + 256]
        p.wait("pool", "dve")
        p.op("pool", lambda e: e.tensor_copy(wbf, g.wts_tm_f()), inc="pool")
        p.wait("pe", "pool")
        for grp in range(8):
            if grp:
                p.wait("pe", "dve")
            for k in range(4):
                tc = grp * 4 + k
                p.op("pe", lambda e, tc=tc, k=k: e.transpose(
                    ppk[0:8, k * 128:(k + 1) * 128],
                    wbf[:, tc * 8:(tc + 1) * 8], g.ident()), inc="pe")
            p.wait("dve", "pe")
            dst = g.bufC[0:8, C_PT + grp * 512:C_PT + (grp + 1) * 512]
            p.op("dve", lambda e, d=dst: e.tensor_copy(d, ppk[0:8, 0:512]),
                 inc="dve")
        p.wait("syncq", "dve")
        if DEBUG:
            p.op("syncq", lambda e: e.dma_start(
                g.h["dbg_wts"], g.wts_tm_f()), inc="dmaio", incn=16)
            p.wait("syncq", "dmaio")


def emit_ffn(g):
    nc, p, h = g.nc, g.p, g.h
    p.entry_guard("pe")
    p.entry_guard("syncq")
    with ExitStack() as ps:
        pt = lambda nm: ps.enter_context(nc.psum_tensor(_u(nm), [128, 512], F32))[:, :]
        pA = (pt("fA0"), pt("fA1"))
        pB = (pt("fB0"), pt("fB1"))
        pW = (pt("fW0"), pt("fW1"))

        conv_done = {}
        pe_done_unit = {}

        def emit_weight_load(u):
            e_, q8 = divmod(u, NQ8)
            half = (u % 2) * 8192
            first_conv = True
            for ch in range(16):
                slot = ch % 2
                if ch < 8:
                    fc = ch
                    src = h["w1"][e_, fc * 128:(fc + 1) * 128,
                                  q8 * 512:(q8 + 1) * 512]
                    dst = g.bufA[:, half + fc * 512:half + (fc + 1) * 512]
                else:
                    kk = ch - 8
                    hc, hlf = divmod(kk, 2)
                    base = half + 4096 + hc * 1024 + hlf * 512
                    src = h["w2"][e_, q8 * 512 + hc * 128:q8 * 512 + (hc + 1) * 128,
                                  hlf * 512:(hlf + 1) * 512]
                    dst = g.bufA[:, base:base + 512]
                p.wait("syncq", "pool", max(0, p.cnt["pool"] - 1))
                p.op("syncq", lambda e, s=src, sl=slot: e.dma_start(g.land(sl), s),
                     inc="dmaw", incn=16)
                if first_conv and u - 2 in pe_done_unit:
                    p.wait("pool", "pe", pe_done_unit[u - 2])
                first_conv = False
                p.wait("pool", "dmaw")
                p.op("pool", lambda e, d=dst, sl=slot: e.tensor_copy(d, g.land(sl)),
                     inc="pool")
            conv_done[u] = p.cnt["pool"]

        emit_weight_load(0)
        emit_weight_load(1)

        NU = E * NQ8
        hw_cols = lambda par, hc: g.bufB[:, par * 2048 + hc * 512:
                                         par * 2048 + (hc + 1) * 512]
        gtmp = lambda sl: g.uni[:, sl * 512:(sl + 1) * 512]

        stepc = 0
        hist_gelu = {}
        hist_add = {}
        hist_hw = {}
        hist_wb = {}
        pe_b_stop = {}

        pending = None

        def emit_w2_for(par, W2, hwcnt, ptt, pu, pstep):
            p.wait("pe", "dve", hwcnt)
            for oc in range(FC):
                bank = pB[oc % 2]
                p.wait("pe", "dve", hist_add.get(oc % 2, 0))
                for hc in range(HC4):
                    p.op("pe", lambda e, b=bank, w=W2(hc, oc),
                         r=hw_cols(par, hc), hc=hc: e.matmul(
                             b, w, r, start=(hc == 0), stop=(hc == 3)),
                         inc="pe" if hc == 3 else None)
                p.wait("dve", "pe", p.cnt["pe"])
                xs = g.xres[:, oc * 4096 + ptt * 512:oc * 4096 + (ptt + 1) * 512]
                p.op("dve", lambda e, x=xs, b=bank: e.tensor_tensor(
                    x, x, b, ALU.add), inc="dve")
                hist_add[oc % 2] = p.cnt["dve"]
            pe_b_stop[pstep] = p.cnt["pe"]
            if ptt == NTT - 1:
                pe_done_unit[pu] = p.cnt["pe"]
                if pu + 2 < NU:
                    emit_weight_load(pu + 2)

        for u in range(NU):
            e_, q8 = divmod(u, NQ8)
            half = (u % 2) * 8192
            W1 = lambda fc, hc, half=half: g.bufA[
                :, half + fc * 512 + hc * 128:half + fc * 512 + (hc + 1) * 128]
            W2 = lambda hc, oc, half=half: g.bufA[
                :, half + 4096 + hc * 1024 + oc * 128:
                half + 4096 + hc * 1024 + (oc + 1) * 128]
            p.wait("pe", "pool", conv_done[u])
            for tt in range(NTT):
                par = stepc % 2
                wbank = pW[par]
                if stepc - 2 in hist_wb:
                    p.wait("pe", "dve", hist_wb[stepc - 2])
                rhs_w = g.bufC[0:8, C_PT + tt * 512:C_PT + (tt + 1) * 512]
                p.op("pe", lambda e, wb=wbank, s=g.sel(e_), r=rhs_w: e.matmul(
                    wb, s, r, start=True, stop=True), inc="pe")
                for hc in range(HC4):
                    bank = pA[hc % 2]
                    p.wait("pe", "act", hist_gelu.get(hc % 2, 0))
                    for fc in range(FC):
                        rhs = g.xln[:, fc * 4096 + tt * 512:
                                    fc * 4096 + (tt + 1) * 512]
                        p.op("pe", lambda e, b=bank, w=W1(fc, hc), r=rhs, fc=fc:
                             e.matmul(b, w, r, start=(fc == 0), stop=(fc == 7)),
                             inc="pe" if fc == 7 else None)
                    pe_a = p.cnt["pe"]
                    p.wait("act", "pe", pe_a)
                    p.wait("act", "dve", hist_hw.get(hc % 2, 0))
                    b1c = g.b1col(e_, q8 * 4 + hc)
                    p.op("act", lambda e, b=bank, gt=gtmp(hc % 2), b1c=b1c:
                         e.activation(gt, b, AF.Gelu_apprx_tanh, bias=b1c,
                                      scale=1.0), inc="act")
                    hist_gelu[hc % 2] = p.cnt["act"]
                    p.wait("dve", "act")
                    if stepc - 2 in pe_b_stop:
                        p.wait("dve", "pe", pe_b_stop[stepc - 2])
                    dst = hw_cols(par, hc)
                    p.op("dve", lambda e, d=dst, gt=gtmp(hc % 2), wb=wbank:
                         e.tensor_tensor(d, gt, wb, ALU.mult), inc="dve")
                    hist_hw[hc % 2] = p.cnt["dve"]
                hist_wb[stepc] = p.cnt["dve"]
                if pending is not None:
                    emit_w2_for(*pending)
                pending = (par, W2, hist_wb[stepc], tt, u, stepc)
                stepc += 1
        emit_w2_for(*pending)
        pending = None
        for tt in range(NTT):
            for fc in range(FC):
                bank = pW[(tt * FC + fc) % 2]
                p.wait("pe", "dve")
                b2l = g.bufC[0:8, C_B2SB + fc * 128:C_B2SB + (fc + 1) * 128]
                rhs_w = g.bufC[0:8, C_PT + tt * 512:C_PT + (tt + 1) * 512]
                p.op("pe", lambda e, b=bank, l=b2l, r=rhs_w: e.matmul(
                    b, l, r, start=True, stop=True), inc="pe")
                p.wait("dve", "pe")
                xs = g.xres[:, fc * 4096 + tt * 512:fc * 4096 + (tt + 1) * 512]
                p.op("dve", lambda e, x=xs, b=bank: e.tensor_tensor(
                    x, x, b, ALU.add), inc="dve")
        p.wait("syncq", "dve")


def emit_attn(g, seq):
    nc, p, h = g.nc, g.p, g.h
    s = seq
    o = 1 - seq
    unib = g.uni[:, :]
    lorB = lambda oc: unib[0:16, oc * 128:(oc + 1) * 128]
    lorA = lambda fc: unib[:, 1024 + fc * 16:1024 + (fc + 1) * 16]
    xcol = lambda fc, t0, w=512: g.xln[:, fc * 4096 + s * SEQ + t0:
                                       fc * 4096 + s * SEQ + t0 + w]
    kcol = lambda fc, t0, w=512: g.xln[:, fc * 4096 + o * SEQ + t0:
                                       fc * 4096 + o * SEQ + t0 + w]
    wstg = g.bufC[:, C_PB:C_PB + 1024]
    WT = lambda kc, oc: g.bufB[:, (kc * 8 + oc) * 128:(kc * 8 + oc + 1) * 128]
    u_sb = g.bufC[0:16, C_USB:C_USB + 512]
    vst = g.bufC[:, C_VST:C_VST + 512]
    PT = lambda kc: g.bufC[:, C_PT + kc * 256:C_PT + (kc + 1) * 256]
    OT = lambda c: g.bufC[:, C_OT + c * 256:C_OT + (c + 1) * 256]
    Pb = lambda sl: g.bufC[:, C_PB + sl * 2048:C_PB + (sl + 1) * 2048]
    Vtm = lambda c, kc: g.bufA[:, (c * NKC + kc) * 128:(c * NKC + kc + 1) * 128]

    p.entry_guard("pe")
    p.entry_guard("syncq")
    with ExitStack() as ps:
        pP = [ps.enter_context(nc.psum_tensor(_u(f"aP{i}"), [128, 512], F32))[:, :]
              for i in range(8)]

        def load_W(name):
            for ocg in range(8):
                for hlf in range(2):
                    p.wait("syncq", "pool")
                    src = h[name][ocg * 128:(ocg + 1) * 128,
                                  hlf * 512:(hlf + 1) * 512]
                    p.op("syncq", lambda e, sr=src, sl=hlf: e.dma_start(
                        g.land(sl), sr), inc="dmaw", incn=16)
                    p.wait("pool", "dmaw")
                    p.wait("pool", "pe")
                    dst = wstg[:, hlf * 512:(hlf + 1) * 512]
                    p.op("pool", lambda e, d=dst, sl=hlf: e.tensor_copy(
                        d, g.land(sl)), inc="pool")
                p.wait("pe", "pool")
                for fcg in range(2):
                    bank = pP[6 + fcg]
                    p.wait("pe", "dve")
                    for j in range(4):
                        fc = fcg * 4 + j
                        p.op("pe", lambda e, b=bank, j=j, fc=fc: e.transpose(
                            b.bitcast(BF16)[:, j * 128:(j + 1) * 128],
                            wstg[:, fc * 128:(fc + 1) * 128], g.ident()),
                             inc="pe")
                    p.wait("dve", "pe")
                    for j in range(4):
                        fc = fcg * 4 + j
                        p.op("dve", lambda e, b=bank, j=j, d=WT(fc, ocg):
                             e.tensor_copy(d, b.bitcast(BF16)[:, j * 128:(j + 1) * 128]),
                             inc="dve")

        def load_lora(wn_a, wn_b):
            p.wait("syncq", "pool")
            p.wait("syncq", "pe")
            p.op("syncq", lambda e: e.dma_start(g.fpD[0:16, 0:1024], h[wn_a]),
                 inc="dmaw", incn=16)
            p.wait("pool", "dmaw")
            p.wait("pool", "pe")
            astg = wstg[0:16, 0:1024]
            p.op("pool", lambda e: e.tensor_copy(astg, g.fpD[0:16, 0:1024]),
                 inc="pool")
            p.wait("pe", "pool")
            bank = pP[6]
            p.wait("pe", "dve")
            for fc in range(8):
                p.op("pe", lambda e, b=bank, fc=fc: e.transpose(
                    b.bitcast(BF16)[:, fc * 16:(fc + 1) * 16],
                    astg[:, fc * 128:(fc + 1) * 128], g.ident()[0:16, 0:16]),
                     inc="pe")
            p.wait("dve", "pe")
            p.op("dve", lambda e, b=bank: e.tensor_copy(
                unib[:, 1024:1152], b.bitcast(BF16)[:, 0:128]), inc="dve")
            p.wait("syncq", "pool")
            land3 = g.land(0).rearrange("p (c r) -> p c r", c=8)[:, :, 0:16]
            p.op("syncq", lambda e, l3=land3: e.dma_start(
                l3, h[wn_b].rearrange("(c p) r -> p c r", c=8)),
                 inc="dmaw", incn=16)
            p.wait("pool", "dmaw")
            p.wait("pool", "pe")
            bstg = wstg[:, 0:128]
            p.op("pool", lambda e, l3=land3: e.tensor_copy(
                bstg.rearrange("p (c r) -> p c r", c=8), l3), inc="pool")
            p.wait("pe", "pool")
            bank2 = pP[7]
            p.wait("pe", "dve")
            for oc in range(8):
                p.op("pe", lambda e, b=bank2, oc=oc: e.transpose(
                    b.bitcast(BF16)[0:16, oc * 128:(oc + 1) * 128],
                    bstg[:, oc * 16:(oc + 1) * 16], g.ident()), inc="pe")
            p.wait("dve", "pe")
            p.op("dve", lambda e, b=bank2: e.tensor_copy(
                unib[0:16, 0:1024], b.bitcast(BF16)[0:16, 0:1024]), inc="dve")

        def proj(kind):
            wn = {"k": "Wk", "v": "Wv", "q": "Wq"}[kind]
            load_W(wn)
            load_lora({"k": "Ak", "v": "Av", "q": "Aq"}[kind],
                      {"k": "Bk", "v": "Bv", "q": "Bq"}[kind])
            for tt in range(4):
                t0 = tt * 512
                bank_u = pP[0]
                p.wait("pe", "act")
                for fc in range(FC):
                    p.op("pe", lambda e, b=bank_u, w=lorA(fc), r=xcol(fc, t0),
                         fc=fc: e.matmul(b[0:16, :], w, r, start=(fc == 0),
                                         stop=(fc == 7)),
                         inc="pe" if fc == 7 else None)
                p.wait("act", "pe")
                p.op("act", lambda e, b=bank_u: e.activation(
                    u_sb, b[0:16, :], AF.Copy, scale=1.0 / RANK), inc="act")
                uc = p.cnt["act"]
                if kind == "q":
                    for oc in range(FC):
                        bank = pP[oc]
                        if oc == 0:
                            p.wait("pe", "act", uc)
                        for fc in range(FC):
                            p.op("pe", lambda e, b=bank, w=WT(fc, oc),
                                 r=xcol(fc, t0), fc=fc: e.matmul(
                                     b, w, r, start=(fc == 0), stop=False))
                        p.op("pe", lambda e, b=bank, w=lorB(oc): e.matmul(
                            b, w, u_sb, start=False, stop=True), inc="pe")
                    p.wait("act", "pe")
                    for oc in range(FC):
                        dst = xcol(oc, t0)
                        p.op("act", lambda e, b=pP[oc], d=dst: e.activation(
                            d, b, AF.Copy, scale=0.125), inc="act")
                else:
                    for oc in range(FC):
                        bank = pP[1 + (oc % 2)]
                        p.wait("pe", "act")
                        if kind == "v":
                            p.wait("pe", "dve")
                        for fc in range(FC):
                            p.op("pe", lambda e, b=bank, w=WT(fc, oc),
                                 r=xcol(fc, t0), fc=fc: e.matmul(
                                     b, w, r, start=(fc == 0), stop=False))
                        p.wait("pe", "act", uc)
                        p.op("pe", lambda e, b=bank, w=lorB(oc): e.matmul(
                            b, w, u_sb, start=False, stop=True), inc="pe")
                        p.wait("act", "pe")
                        if kind == "k":
                            dst = kcol(oc, t0)
                            p.op("act", lambda e, b=bank, d=dst: e.activation(
                                d, b, AF.Copy, scale=1.0), inc="act")
                        else:
                            p.wait("act", "pe")
                            p.op("act", lambda e, b=bank: e.activation(
                                vst, b, AF.Copy, scale=1.0), inc="act")
                            bankT = pP[3 + (oc % 2)]
                            p.wait("pe", "act")
                            p.wait("pe", "dve")
                            for j in range(4):
                                p.op("pe", lambda e, b=bankT, j=j: e.transpose(
                                    b.bitcast(BF16)[:, j * 128:(j + 1) * 128],
                                    vst[:, j * 128:(j + 1) * 128], g.ident()),
                                     inc="pe")
                            p.wait("dve", "pe")
                            for j in range(4):
                                kc = tt * 4 + j
                                p.op("dve", lambda e, b=bankT, j=j,
                                     d=Vtm(oc, kc): e.tensor_copy(
                                         d, b.bitcast(BF16)[:, j * 128:(j + 1) * 128]),
                                     inc="dve")

        proj("k")
        proj("v")
        proj("q")
        load_lora("Ao", "Bo")
        load_W("Wo")
        p.wait("syncq", "dve")
        p.wait("syncq", "act")

    p.entry_guard("pe")
    with nc.psum_tensor(_u("aSA"), [128, 2048], F32) as SAh, \
         nc.psum_tensor(_u("aSB"), [128, 2048], F32) as SBh:
        SA = SAh[:, :]
        SB = SBh[:, :]
        Ssets = (SA, SB)
        negm = lambda si: g.stats(16 + si)
        ssum = lambda si: g.stats(18 + si)
        rsv = lambda si: g.stats(20 + si)
        set_free_exp = [0, 0]
        set_pack_evac = [0, 0]
        norm_done = [0, 0]
        maxdve = {}
        pv_evac = 0
        op_add = [0, 0]

        def phaseA1(k, c, j, qt, q0g):
            si = qt
            St = Ssets[si]
            q0 = q0g + qt * 128
            prow = j * 64
            p.wait("pe", "act", max(set_free_exp[si], set_pack_evac[si]))
            p.wait("pe", "dve", max(pv_evac, op_add[0], op_add[1],
                                    norm_done[si]))
            lhs = g.xln[prow:prow + 64,
                        c * 4096 + s * SEQ + q0:
                        c * 4096 + s * SEQ + q0 + 128]
            for kb in range(4):
                rhs = g.xln[prow:prow + 64,
                            c * 4096 + o * SEQ + kb * 512:
                            c * 4096 + o * SEQ + (kb + 1) * 512]
                p.op("pe", lambda e, St=St, kb=kb, l=lhs, r=rhs:
                     e.matmul(St[:, kb * 512:(kb + 1) * 512], l, r,
                              start=True, stop=True),
                     inc="pe" if kb == 3 else None)
            p.wait("dve", "pe")
            p.op("dve", lambda e, St=St, si=si: e.tensor_reduce(
                negm(si), St[:, :], AX.X, ALU.max, negate=True), inc="dve")
            maxdve[k] = p.cnt["dve"]

        def phaseA2(k, qt):
            si = qt
            St = Ssets[si]
            p.wait("act", "dve", maxdve[k])
            p.op("act", lambda e, St=St, si=si: e.activation(
                St[:, :], St[:, :], AF.Exp, bias=negm(si), scale=1.0,
                accum_out=ssum(si)), inc="act")
            set_free_exp[si] = p.cnt["act"]
            p.wait("dve", "act", p.cnt["act"])
            p.op("dve", lambda e, si=si: e.reciprocal(rsv(si), ssum(si)),
                 inc="dve")
            p.wait("dve", "dve")
            p.op("dve", lambda e, si=si, St=St: e.tensor_scalar(
                Pb(si), St[:, :], rsv(si), None, ALU.mult, ALU.bypass),
                 inc="dve")
            norm_done[si] = p.cnt["dve"]

        def phaseB(k, c, j, qt):
            si = qt
            St = Ssets[si]
            p.wait("pe", "dve", norm_done[si])
            for pk in range(4):
                for jj in range(4):
                    kc = pk * 4 + jj
                    p.op("pe", lambda e, St=St, pk=pk, jj=jj,
                         kc=kc, si=si: e.transpose(
                             St.bitcast(BF16)[
                                 :, pk * 1024 + jj * 128:
                                 pk * 1024 + (jj + 1) * 128],
                             Pb(si)[:, kc * 128:(kc + 1) * 128],
                             g.ident()),
                         inc="pe" if jj == 3 else None)
                p.wait("act", "pe")
                base = C_PT + pk * 4 * 256
                dst = g.bufC[:, base:base + 1024].rearrange(
                    "p (k v) -> p k v", v=256)[:, :, qt * 128:
                                               qt * 128 + 128]
                srcv = St.bitcast(BF16)[:, pk * 1024:pk * 1024 + 512]
                p.op("act", lambda e, d=dst, sv=srcv: e.copy(
                    d, sv.rearrange("p (k v) -> p k v", v=128)),
                     inc="act")
            set_pack_evac[si] = p.cnt["act"]
            if qt == 1:
                p.wait("pe", "act", set_pack_evac[1])
                p.wait("pe", "dve", pv_evac)
                for kc in range(NKC):
                    p.op("pe", lambda e, kc=kc, c=c, j=j: e.matmul(
                        SB[j * 64:(j + 1) * 64, 0:256],
                        Vtm(c, kc)[:, j * 64:(j + 1) * 64],
                        PT(kc), start=(kc == 0), stop=(kc == 15)),
                         inc="pe" if kc == 15 else None)
                p.wait("dve", "pe")
                dsto = OT(c)[j * 64:(j + 1) * 64, :]
                p.op("dve", lambda e, d=dsto, j=j: e.tensor_copy(
                    d, SB[j * 64:(j + 1) * 64, 0:256]), inc="dve")
                nonlocal_pv[0] = p.cnt["dve"]

        nonlocal_pv = [0]

        for qg in range(NQG):
            q0g = qg * QG
            steps = [(c, j, qt)
                     for c in range(PAIRS) for j in (0, 1) for qt in (0, 1)]
            for k, (c, j, qt) in enumerate(steps):
                if k >= 2:
                    phaseB(k - 2, *steps[k - 2])
                    pv_evac = nonlocal_pv[0]
                phaseA1(k, c, j, qt, q0g)
                if k >= 1:
                    phaseA2(k - 1, steps[k - 1][2])
            phaseB(30, *steps[30])
            pv_evac = nonlocal_pv[0]
            phaseA2(31, steps[31][2])
            phaseB(31, *steps[31])
            pv_evac = nonlocal_pv[0]
            bank_u = SB
            p.wait("pe", "act", max(set_free_exp[1], set_pack_evac[1]))
            p.wait("pe", "dve", pv_evac)
            for pc in range(PAIRS):
                p.op("pe", lambda e, pc=pc: e.matmul(
                    bank_u[0:16, 0:256], lorA(pc), OT(pc),
                    start=(pc == 0), stop=(pc == 7)),
                     inc="pe" if pc == 7 else None)
            p.wait("act", "pe")
            p.op("act", lambda e: e.activation(
                u_sb[:, 0:256], bank_u[0:16, 0:256], AF.Copy, scale=1.0 / RANK),
                 inc="act")
            uc2 = p.cnt["act"]
            for oc in range(FC):
                off = (oc % 2) * 512
                if oc < 2:
                    if oc == 0:
                        p.wait("pe", "act", max(set_free_exp[0],
                                                set_pack_evac[0], uc2))
                    p.wait("pe", "dve", pv_evac)
                else:
                    p.wait("pe", "dve", op_add[oc % 2])
                for pc in range(PAIRS):
                    p.op("pe", lambda e, oc=oc, pc=pc, off=off: e.matmul(
                        SA[:, off:off + 256], WT(pc, oc), OT(pc),
                        start=(pc == 0), stop=False))
                p.op("pe", lambda e, oc=oc, off=off: e.matmul(
                    SA[:, off:off + 256], lorB(oc), u_sb[:, 0:256],
                    start=False, stop=True), inc="pe")
                p.wait("dve", "pe")
                p.wait("dve", "dve")
                bocol = g.stats(32 + oc)
                p.op("dve", lambda e, oc=oc, off=off, bc=bocol: e.tensor_scalar(
                    vst[:, 0:256], SA[:, off:off + 256], bc, None,
                    ALU.add, ALU.bypass), inc="dve")
                p.wait("dve", "dve")
                xs = g.xres[:, oc * 4096 + s * SEQ + q0g:
                            oc * 4096 + s * SEQ + q0g + 256]
                p.op("dve", lambda e, x=xs: e.tensor_tensor(
                    x, x, vst[:, 0:256], ALU.add), inc="dve")
                op_add[oc % 2] = p.cnt["dve"]
        p.wait("syncq", "dve")


def emit_debug_dump(g, sbuf, name):
    p = g.p
    p.entry_guard("syncq")
    p.op("syncq", lambda e: e.dma_start(g.h[name], sbuf[:, :]),
         inc="dmaio", incn=16)
    p.wait("syncq", "dmaio")


def emit_store(g):
    nc, p, h = g.nc, g.p, g.h
    p.entry_guard("pe")
    with nc.psum_tensor(_u("spk0"), [128, 512], BF16) as pk0, \
         nc.psum_tensor(_u("spk1"), [128, 512], BF16) as pk1:
        pks = (pk0, pk1)
        for tc in range(NTC):
            t0 = tc * 128
            for k in range(2):
                pk = pks[k]
                p.wait("pe", "act")
                for j in range(4):
                    fc = k * 4 + j
                    src = g.xres[:, fc * 4096 + t0:fc * 4096 + t0 + 128]
                    p.op("pe", lambda e, pk=pk, j=j, srt=src: e.transpose(
                        pk[:, j * 128:(j + 1) * 128], srt, g.ident()), inc="pe")
                p.wait("act", "pe")
                p.wait("act", "st")
                dst = g.fpD[:, k * 512:(k + 1) * 512]
                p.op("act", lambda e, d=dst, pk=pk: e.copy(d, pk[:, :]),
                     inc="act")
            p.wait("syncq", "act")
            p.op("syncq", lambda e, t0=t0: e.dma_start(
                h["out"][t0:t0 + 128, :], g.fpD[:, 0:1024]), inc="st", incn=16)
        p.wait("syncq", "st")


def build_nc():
    nc = bass.Bass("TRN2", target_bir_lowering=False)
    h = {}
    h["x"] = _hbm(nc, "x", (T, D), F32)
    for w in ("Wq", "Wk", "Wv", "Wo"):
        h[w] = _hbm(nc, w, (D, D), F32)
    h["bo"] = _hbm(nc, "bo", (1, D), F32)
    for a in ("Aq", "Ak", "Av", "Ao"):
        h[a] = _hbm(nc, a, (RANK, D), F32)
    for b in ("Bq", "Bk", "Bv", "Bo"):
        h[b] = _hbm(nc, b, (D, RANK), F32)
    h["wr"] = _hbm(nc, "wr", (D, E), F32)
    h["w1"] = _hbm(nc, "w1", (E, D, 4 * D), F32)
    h["b1"] = _hbm(nc, "b1", (E, 4 * D), F32)
    h["w2"] = _hbm(nc, "w2", (E, 4 * D, D), F32)
    h["b2"] = _hbm(nc, "b2", (E, D), F32)
    h["cbf"] = _hbm(nc, "cbf", (128, 1408), BF16)
    h["cf32"] = _hbm(nc, "cf32", (8, 8), F32)
    h["out"] = _hbm(nc, "out", (T, D), F32, out=True)
    if DEBUG:
        h["dbg_xln"] = _hbm(nc, "dbg_xln", (128, 32768), BF16, out=True)
        h["dbg_x1"] = _hbm(nc, "dbg_x1", (128, 32768), BF16, out=True)
        h["dbg_x2"] = _hbm(nc, "dbg_x2", (128, 32768), BF16, out=True)
        h["dbg_wts"] = _hbm(nc, "dbg_wts", (128, 256), F32, out=True)
        h["dbg_q"] = _hbm(nc, "dbg_q", (128, 32768), BF16, out=True)
        h["dbg_vtm"] = _hbm(nc, "dbg_vtm", (128, 16384), BF16, out=True)
        h["dbg_bufc"] = _hbm(nc, "dbg_bufc", (128, CB_COLS), BF16, out=True)

    with ExitStack() as stack:
        sb = lambda name, shape, dt: stack.enter_context(
            nc.sbuf_tensor(name, shape, dt))
        bufs = dict(
            xres=sb("xres", [128, 32768], BF16),
            xln=sb("xln", [128, 32768], BF16),
            bufA=sb("bufA", [128, 16384], BF16),
            bufB=sb("bufB", [128, 8192], BF16),
            bufC=sb("bufC", [128, CB_COLS], BF16),
            fpD=sb("fpD", [128, 1088], F32),
            uni=sb("uni", [128, 1152], BF16),
            cb=sb("cb", [128, 1408], BF16),
            cf=sb("cf", [8, 8], F32),
            rW=sb("rW", [128, 64], BF16),
        )
        p = Prog(nc, stack)
        g = G(nc, p, h, **bufs)

        emit_init(g)
        p.flush_block("init")
        emit_ln(g, seqs=(0, 1), from_input=True)
        p.flush_block("ln1")
        if DEBUG:
            emit_debug_dump(g, g.xln, "dbg_xln")
            p.flush_block("dbgln")
        emit_router(g)
        p.flush_block("rt1")
        emit_ffn(g)
        p.flush_block("ffn1")
        if DEBUG:
            emit_debug_dump(g, g.xres, "dbg_x1")
            p.flush_block("dbg1")
        for s in (0, 1):
            emit_ln(g, seqs=(s,), from_input=False)
            p.flush_block(f"ln2s{s}")
            emit_attn(g, seq=s)
            p.flush_block(f"attns{s}")
            if DEBUG and s == 0:
                emit_debug_dump(g, g.xln, "dbg_q")
                emit_debug_dump(g, g.bufA, "dbg_vtm")
                emit_debug_dump(g, g.bufC, "dbg_bufc")
                p.flush_block("dbgattn")
        if DEBUG:
            emit_debug_dump(g, g.xres, "dbg_x2")
            p.flush_block("dbg2")
        emit_ln(g, seqs=(0, 1), from_input=False)
        p.flush_block("ln3")
        emit_load_biases(g)
        p.flush_block("bias2")
        emit_router(g)
        p.flush_block("rt3")
        emit_ffn(g)
        p.flush_block("ffn2")
        emit_store(g)
        p.flush_block("store")
    return nc


def _make_consts():
    cbf = np.zeros((128, 1408), dtype=ml_dtypes.bfloat16)
    cbf[:, CONST_ID:CONST_ID + 128] = np.eye(128, dtype=ml_dtypes.bfloat16)
    for e in range(8):
        cbf[e, CONST_SEL + e * 128:CONST_SEL + (e + 1) * 128] = 1.0
    cbf[0, CONST_ONES:CONST_ONES + 256] = 1.0
    cf32 = np.eye(8, dtype=np.float32)
    return cbf, cf32


_NC_CACHE = [None]


def kernel(L0_attn_x, L0_attn_Wq, L0_attn_Wk, L0_attn_Wv, L0_attn_Wo, L0_attn_bo,
           L0_attn_Aq, L0_attn_Bq, L0_attn_Ak, L0_attn_Bk, L0_attn_Av, L0_attn_Bv,
           L0_attn_Ao, L0_attn_Bo, L0_ffn_w_router, L0_ffn_w1, L0_ffn_b1,
           L0_ffn_w2, L0_ffn_b2):
    x = np.ascontiguousarray(np.asarray(L0_attn_x, dtype=np.float32))
    cbf, cf32 = _make_consts()
    f32c = lambda a: np.ascontiguousarray(np.asarray(a, dtype=np.float32))
    shared = {
        "Wq": f32c(L0_attn_Wq), "Wk": f32c(L0_attn_Wk),
        "Wv": f32c(L0_attn_Wv), "Wo": f32c(L0_attn_Wo),
        "bo": f32c(L0_attn_bo).reshape(1, D),
        "Aq": f32c(L0_attn_Aq), "Ak": f32c(L0_attn_Ak),
        "Av": f32c(L0_attn_Av), "Ao": f32c(L0_attn_Ao),
        "Bq": f32c(L0_attn_Bq), "Bk": f32c(L0_attn_Bk),
        "Bv": f32c(L0_attn_Bv), "Bo": f32c(L0_attn_Bo),
        "wr": f32c(L0_ffn_w_router),
        "w1": f32c(L0_ffn_w1), "b1": f32c(L0_ffn_b1),
        "w2": f32c(L0_ffn_w2), "b2": f32c(L0_ffn_b2),
        "cbf": cbf, "cf32": cf32,
    }
    if _NC_CACHE[0] is None:
        _NC_CACHE[0] = build_nc()
    nc = _NC_CACHE[0]
    in_maps = []
    for c in range(N_CORES):
        m = dict(shared)
        m["x"] = np.ascontiguousarray(x[2 * c:2 * c + 2].reshape(T, D))
        in_maps.append(m)
    res = run_bass_kernel_spmd(nc, in_maps, core_ids=list(range(N_CORES)))
    outs = [np.asarray(r["out"]).reshape(2, SEQ, D) for r in res.results]
    return np.concatenate(outs, axis=0).astype(np.float32)


# version 10
# speedup vs baseline: 1.6792x; 1.0138x over previous
import numpy as np
import ml_dtypes
import concourse.bass as bass
import concourse.mybir as mybir
from concourse.bass_utils import run_bass_kernel_spmd
from contextlib import ExitStack

F32 = mybir.dt.float32
BF16 = mybir.dt.bfloat16
AF = mybir.ActivationFunctionType
ALU = mybir.AluOpType
AX = mybir.AxisListType

DEBUG = 0
N_CORES = 8
T = 4096
SEQ = 2048
D = 1024
FC = 8
NTT = 8
NTC = 32
E = 8
NQ8 = 8
HC4 = 4
RANK = 16
PAIRS = 8
NKC = 16
QG = 256
NQG = 8
EPS = 1e-5

C_PT = 0
C_PB = 4096
C_OT = 8192
C_VST = 10240
C_USB = 10752
CB_COLS = 11264
C_XLNTM = 4096
C_XTM = 5120
C_B2SB = 8192
C_B1COLS = 9216
C_WTSTM = 9728
C_PRB = 10240

CONST_ID = 0
CONST_SEL = 128
CONST_ONES = 1152

D_LAND = 0
D_STATS = 1024


_UID = [0]


def _u(name):
    _UID[0] += 1
    return f"{name}_{_UID[0]}"


def _hbm(nc, name, shape, dtype, out=False):
    return nc.dram_tensor(name, list(shape), dtype,
                          kind="ExternalOutput" if out else "ExternalInput").ap()


class Prog:
    ENGS = ("pe", "act", "dve", "pool", "syncq")

    def __init__(self, nc, stack):
        self.nc = nc
        self.stack = stack
        self.ops = {e: [] for e in self.ENGS}
        self.sems = {}
        self.cnt = {}

    def new_sems(self, names):
        for n in names:
            self.sems[n] = self.stack.enter_context(self.nc.semaphore(n))
            self.cnt[n] = 0

    def op(self, eng, fn, inc=None, incn=1):
        if inc is None:
            self.ops[eng].append(lambda e, fn=fn: fn(e))
        else:
            self.cnt[inc] += incn
            s = self.sems[inc]
            self.ops[eng].append(
                lambda e, fn=fn, s=s, incn=incn: fn(e).then_inc(s, incn))

    def wait(self, eng, sem, val=None):
        if val is None:
            val = self.cnt[sem]
        if val <= 0:
            return
        s = self.sems[sem]
        self.ops[eng].append(lambda e, s=s, v=val: e.wait_ge(s, v))

    def entry_guard(self, eng="pe"):
        for s in ("act", "dve", "pool"):
            self.wait(eng, s)

    def flush_block(self, name):
        nc = self.nc
        with nc.Block(name) as blk:
            for eng, deco in (("pe", blk.tensor), ("act", blk.scalar),
                              ("dve", blk.vector), ("pool", blk.gpsimd),
                              ("syncq", blk.sync)):
                ops = self.ops[eng]
                if not ops:
                    continue

                def body(e, ops=ops):
                    for f in ops:
                        f(e)
                deco(body)
        self.ops = {e: [] for e in self.ENGS}


class G:
    def __init__(self, nc, p, h, **bufs):
        self.nc, self.p, self.h = nc, p, h
        self.__dict__.update(bufs)
        p.new_sems(["dmaw", "dmaio", "pe", "act", "dve", "pool", "st"])

    def ident(self):
        return self.cb[:, CONST_ID:CONST_ID + 128]

    def ident_f(self):
        return self.cf[0:8, 0:8]

    def sel(self, e):
        return self.cb[0:8, CONST_SEL + e * 128:CONST_SEL + (e + 1) * 128]

    def ones_row(self, n):
        return self.cb[0:1, CONST_ONES:CONST_ONES + n]

    def stats(self, i, w=1):
        return self.fpD[:, D_STATS + i:D_STATS + i + w]

    def land(self, slot):
        return self.fpD[:, D_LAND + slot * 512:D_LAND + (slot + 1) * 512]

    def land2(self):
        return self.fpD[:, D_LAND:D_LAND + 1024]

    def b1col(self, e, hc_abs):
        v = self.bufC[:, C_B1COLS:C_B1COLS + 512].bitcast(F32)
        return v[:, hc_abs * 8 + e:hc_abs * 8 + e + 1]

    def wts_tm_f(self):
        return self.bufC[:, C_WTSTM:C_WTSTM + 512].bitcast(F32)

    def prb(self):
        return self.bufC[:, C_PRB:C_PRB + 256].bitcast(F32)


def emit_load_biases(g):
    nc, p, h = g.nc, g.p, g.h
    p.entry_guard("syncq")
    p.op("syncq", lambda e: e.dma_start(g.fpD[0:8, 0:1024], h["b2"]),
         inc="dmaio", incn=16)
    p.wait("pool", "dmaio")
    p.op("pool", lambda e: e.tensor_copy(
        g.bufC[0:8, C_B2SB:C_B2SB + 1024], g.fpD[0:8, 0:1024]), inc="pool")
    bA32 = g.bufA[:, 0:8192].bitcast(F32)
    p.op("syncq", lambda e: e.dma_start(bA32[0:8, 0:4096], h["b1"]),
         inc="dmaio", incn=16)
    p.wait("pe", "dmaio")
    p.entry_guard("pe")
    with nc.psum_tensor(_u("pbias"), [128, 512], F32) as pi0:
        for grp in range(2):
            if grp:
                p.wait("pe", "dve")
            for k in range(16):
                hc = grp * 16 + k
                src = bA32[0:8, hc * 128:(hc + 1) * 128]
                p.op("pe", lambda e, s=src, d=pi0[:, k * 8:(k + 1) * 8]:
                     e.transpose(d, s, g.ident_f()), inc="pe")
            p.wait("dve", "pe")
            dstc = g.bufC[:, C_B1COLS + grp * 256:
                          C_B1COLS + (grp + 1) * 256].bitcast(F32)
            p.op("dve", lambda e, d=dstc: e.tensor_copy(d, pi0[:, 0:128]),
                 inc="dve")
    p.wait("syncq", "dve")
    p.wait("syncq", "pool")


def emit_init(g):
    nc, p, h = g.nc, g.p, g.h
    p.op("syncq", lambda e: e.dma_start(g.cb[:, :], h["cbf"]), inc="dmaio", incn=16)
    p.op("syncq", lambda e: e.dma_start(g.cf[:, :], h["cf32"]), inc="dmaio", incn=16)
    land3 = g.land(0).rearrange("p (c r) -> p c r", c=8)[:, :, 0:8]
    p.op("syncq", lambda e: e.dma_start(
        land3, h["wr"].rearrange("(c p) r -> p c r", c=8)), inc="dmaio", incn=16)
    p.wait("pool", "dmaio")
    p.op("pool", lambda e: e.tensor_copy(
        g.rW[:, :].rearrange("p (c r) -> p c r", r=8), land3), inc="pool")
    p.wait("syncq", "pool")
    p.op("syncq", lambda e: e.dma_start(g.fpD[0:1, 0:1024], h["bo"]),
         inc="dmaio", incn=16)
    p.wait("pe", "dmaio")
    with nc.psum_tensor(_u("pbo"), [128, 512], F32) as pbo:
        for oc in range(8):
            src = g.fpD[0:1, oc * 128:(oc + 1) * 128]
            p.op("pe", lambda e, s2=src, d=pbo[:, oc:oc + 1]: e.transpose(
                d, s2, g.ident_f()[0:1, 0:1]), inc="pe")
        p.wait("dve", "pe")
        p.op("dve", lambda e: e.tensor_copy(g.stats(32, 8), pbo[:, 0:8]),
             inc="dve")
    p.wait("syncq", "dve")
    emit_load_biases(g)


def emit_ln(g, seqs, from_input):
    nc, p, h = g.nc, g.p, g.h
    tcs = [(s, tc) for s in seqs for tc in range(16)]
    p.entry_guard("pe")
    with nc.psum_tensor(_u("lnpkA"), [128, 512], BF16) as pkA, \
         nc.psum_tensor(_u("lnpkB"), [128, 512], BF16) as pkB, \
         nc.psum_tensor(_u("lnsq"), [128, 1024], F32) as psq:
        pks = (pkA, pkB)
        hist = {0: ("dve", 0), 1: ("dve", 0)}
        sm = g.stats(0)
        sq = g.stats(1)
        rstd = g.stats(2)
        nmrs = g.stats(3)
        for s, tc in tcs:
            t0 = s * SEQ + tc * 128
            xtm = g.bufC[:, C_XTM:C_XTM + 1024]
            xlntm = g.bufC[:, C_XLNTM:C_XLNTM + 1024]
            if from_input:
                p.wait("syncq", "act")
                p.wait("syncq", "pool")
                p.wait("syncq", "dve")
                p.op("syncq", lambda e, t0=t0: e.dma_start(
                    g.land2(), h["x"][t0:t0 + 128, :]), inc="dmaio", incn=16)
                p.wait("dve", "dmaio")
                p.op("dve", lambda e: e.reduce_sum(sm, g.land2(), AX.X), inc="dve")
                p.wait("act", "dmaio")
                p.wait("act", "dve", p.cnt["dve"] - 1)
                p.op("act", lambda e: e.activation(psq[:, :], g.land2(), AF.Square),
                     inc="act")
                p.wait("dve", "act")
                p.op("dve", lambda e: e.reduce_sum(sq, psq[:, :], AX.X), inc="dve")
                p.wait("pool", "dmaio")
                p.wait("pool", "pe")
                p.op("pool", lambda e: e.tensor_copy(xtm, g.land2()), inc="pool")
                p.wait("pe", "pool")
                for k in range(2):
                    pk = pks[k]
                    heng, hcnt = hist[k]
                    p.wait("pe", heng, hcnt)
                    for j in range(4):
                        fc = k * 4 + j
                        p.op("pe", lambda e, pk=pk, j=j, fc=fc: e.transpose(
                            pk[:, j * 128:(j + 1) * 128],
                            xtm[:, fc * 128:(fc + 1) * 128], g.ident()), inc="pe")
                    p.wait("dve", "pe")
                    dst = g.xres.rearrange("p (f t) -> p f t", f=8)[
                        :, k * 4:(k + 1) * 4, t0:t0 + 128]
                    p.op("dve", lambda e, d=dst, pk=pk: e.tensor_copy(
                        d, pk[:, :].rearrange("p (j t) -> p j t", j=4)), inc="dve")
                    hist[k] = ("dve", p.cnt["dve"])
            else:
                for k in range(2):
                    pk = pks[k]
                    heng, hcnt = hist[k]
                    p.wait("pe", heng, hcnt)
                    if k == 0:
                        p.wait("pe", "dve")
                        p.wait("pe", "act")
                    for j in range(4):
                        fc = k * 4 + j
                        src = g.xres[:, fc * 4096 + t0:fc * 4096 + t0 + 128]
                        p.op("pe", lambda e, pk=pk, j=j, s=src: e.transpose(
                            pk[:, j * 128:(j + 1) * 128], s, g.ident()), inc="pe")
                    p.wait("act", "pe")
                    dst = xtm.rearrange("p (f t) -> p f t", f=8)[:, k * 4:(k + 1) * 4, :]
                    p.op("act", lambda e, d=dst, pk=pk: e.copy(
                        d, pk[:, :].rearrange("p (j t) -> p j t", j=4)), inc="act")
                    hist[k] = ("act", p.cnt["act"])
                p.wait("dve", "act")
                p.op("dve", lambda e: e.reduce_sum(sm, xtm, AX.X), inc="dve")
                p.wait("act", "act")
                p.wait("act", "dve", p.cnt["dve"] - 1)
                p.op("act", lambda e: e.activation(psq[:, :], xtm, AF.Square),
                     inc="act")
                p.wait("dve", "act")
                p.op("dve", lambda e: e.reduce_sum(sq, psq[:, :], AX.X), inc="dve")
            n = float(D)
            p.wait("dve", "dve")
            p.op("dve", lambda e: e.tensor_scalar(sm, sm, 1.0 / n, None,
                                                  ALU.mult, ALU.bypass), inc="dve")
            p.wait("dve", "dve")
            p.op("dve", lambda e: e.tensor_tensor(rstd, sm, sm, ALU.mult), inc="dve")
            p.op("dve", lambda e: e.tensor_scalar(sq, sq, 1.0 / n, None,
                                                  ALU.mult, ALU.bypass), inc="dve")
            p.wait("dve", "dve")
            p.op("dve", lambda e: e.tensor_scalar(rstd, rstd, EPS, None,
                                                  ALU.subtract, ALU.bypass),
                 inc="dve")
            p.wait("dve", "dve")
            p.op("dve", lambda e: e.tensor_tensor(sq, sq, rstd, ALU.subtract),
                 inc="dve")
            p.wait("act", "dve")
            p.op("act", lambda e: e.activation(rstd, sq, AF.Sqrt), inc="act")
            p.wait("dve", "act")
            p.op("dve", lambda e: e.reciprocal(rstd, rstd), inc="dve")
            p.wait("dve", "dve")
            p.op("dve", lambda e: e.tensor_tensor(nmrs, sm, rstd, ALU.mult),
                 inc="dve")
            p.wait("dve", "dve")
            p.op("dve", lambda e: e.tensor_scalar(nmrs, nmrs, -1.0, None,
                                                  ALU.mult, ALU.bypass), inc="dve")
            p.wait("dve", "dve")
            src_aff = g.land2() if from_input else xtm
            p.op("dve", lambda e, s=src_aff: e.tensor_scalar(
                xlntm, s, rstd, nmrs, ALU.mult, ALU.add), inc="dve")
            aff_cnt = p.cnt["dve"]
            for k in range(2):
                pk = pks[k]
                heng, hcnt = hist[k]
                p.wait("pe", heng, hcnt)
                p.wait("pe", "dve", aff_cnt)
                for j in range(4):
                    fc = k * 4 + j
                    p.op("pe", lambda e, pk=pk, j=j, fc=fc: e.transpose(
                        pk[:, j * 128:(j + 1) * 128],
                        xlntm[:, fc * 128:(fc + 1) * 128], g.ident()), inc="pe")
                p.wait("dve", "pe")
                dst = g.xln.rearrange("p (f t) -> p f t", f=8)[
                    :, k * 4:(k + 1) * 4, t0:t0 + 128]
                p.op("dve", lambda e, d=dst, pk=pk: e.tensor_copy(
                    d, pk[:, :].rearrange("p (j t) -> p j t", j=4)), inc="dve")
                hist[k] = ("dve", p.cnt["dve"])
        p.wait("syncq", "dve")


def emit_router(g):
    nc, p = g.nc, g.p
    p.entry_guard("pe")
    logT = g.fpD[0:8, 0:512]
    with nc.psum_tensor(_u("rlog"), [128, 512], F32) as plog, \
         nc.psum_tensor(_u("rtm"), [128, 512], F32) as ptm, \
         nc.psum_tensor(_u("rpk"), [128, 512], BF16) as ppk:
        for tt in range(NTT):
            if tt:
                p.wait("pe", "dve")
            for fc in range(FC):
                rhs = g.xln[:, fc * 4096 + tt * 512:fc * 4096 + (tt + 1) * 512]
                p.op("pe", lambda e, fc=fc, rhs=rhs: e.matmul(
                    plog[0:8, :], g.rW[:, fc * 8:(fc + 1) * 8], rhs,
                    start=(fc == 0), stop=(fc == 7)),
                     inc="pe" if fc == 7 else None)
            p.wait("dve", "pe")
            p.op("dve", lambda e: e.tensor_copy(logT, plog[0:8, :]), inc="dve")
            p.wait("pe", "dve")
            for k in range(4):
                p.op("pe", lambda e, k=k: e.transpose(
                    ptm[:, k * 8:(k + 1) * 8], logT[:, k * 128:(k + 1) * 128],
                    g.ident_f()), inc="pe")
            p.wait("dve", "pe")
            for k in range(4):
                tc = tt * 4 + k
                lg = ptm[:, k * 8:(k + 1) * 8]
                prb = g.prb()
                ex = prb[:, 0:8]
                ismax = prb[:, 8:16]
                msk = prb[:, 16:24]
                nm = g.stats(8)
                smx = g.stats(9)
                rs = g.stats(10)
                m2 = g.stats(11)
                p.wait("dve", "dve")
                p.op("dve", lambda e, lg=lg: e.tensor_reduce(
                    nm, lg, AX.X, ALU.max, negate=True), inc="dve")
                p.wait("act", "dve")
                p.op("act", lambda e, lg=lg, ex=ex: e.activation(
                    ex, lg, AF.Exp, bias=nm, scale=1.0, accum_out=smx), inc="act")
                p.wait("dve", "act")
                p.op("dve", lambda e: e.reciprocal(rs, smx), inc="dve")
                p.wait("dve", "dve")
                p.op("dve", lambda e, ex=ex: e.tensor_scalar(
                    ex, ex, rs, 0.5, ALU.mult, ALU.mult), inc="dve")
                p.wait("dve", "dve")
                p.op("dve", lambda e, ex=ex: e.tensor_reduce(
                    nm, ex, AX.X, ALU.max), inc="dve")
                p.wait("dve", "dve")
                p.op("dve", lambda e, ex=ex, ismax=ismax: e.tensor_scalar(
                    ismax, ex, nm, 1e30, ALU.is_ge, ALU.mult), inc="dve")
                p.wait("dve", "dve")
                p.op("dve", lambda e, ex=ex, ismax=ismax, msk=msk: e.tensor_tensor(
                    msk, ex, ismax, ALU.subtract), inc="dve")
                p.wait("dve", "dve")
                p.op("dve", lambda e, msk=msk: e.tensor_reduce(
                    m2, msk, AX.X, ALU.max), inc="dve")
                p.wait("dve", "dve")
                p.op("dve", lambda e, ex=ex, msk=msk: e.tensor_scalar(
                    msk, ex, m2, None, ALU.is_ge, ALU.bypass), inc="dve")
                p.wait("dve", "dve")
                wdst = g.wts_tm_f()[:, tc * 8:(tc + 1) * 8]
                p.op("dve", lambda e, wdst=wdst, ex=ex, msk=msk: e.tensor_tensor(
                    wdst, ex, msk, ALU.mult), inc="dve")
        wbf = g.bufC[:, C_VST:C_VST + 256]
        p.wait("pool", "dve")
        p.op("pool", lambda e: e.tensor_copy(wbf, g.wts_tm_f()), inc="pool")
        p.wait("pe", "pool")
        for grp in range(8):
            if grp:
                p.wait("pe", "dve")
            for k in range(4):
                tc = grp * 4 + k
                p.op("pe", lambda e, tc=tc, k=k: e.transpose(
                    ppk[0:8, k * 128:(k + 1) * 128],
                    wbf[:, tc * 8:(tc + 1) * 8], g.ident()), inc="pe")
            p.wait("dve", "pe")
            dst = g.bufC[0:8, C_PT + grp * 512:C_PT + (grp + 1) * 512]
            p.op("dve", lambda e, d=dst: e.tensor_copy(d, ppk[0:8, 0:512]),
                 inc="dve")
        p.wait("syncq", "dve")
        if DEBUG:
            p.op("syncq", lambda e: e.dma_start(
                g.h["dbg_wts"], g.wts_tm_f()), inc="dmaio", incn=16)
            p.wait("syncq", "dmaio")


def emit_ffn(g):
    nc, p, h = g.nc, g.p, g.h
    p.entry_guard("pe")
    p.entry_guard("syncq")
    with ExitStack() as ps:
        pt = lambda nm: ps.enter_context(nc.psum_tensor(_u(nm), [128, 512], F32))[:, :]
        pA = (pt("fA0"), pt("fA1"))
        pB = (pt("fB0"), pt("fB1"))
        pW = (pt("fW0"), pt("fW1"))

        conv_done = {}
        pe_done_unit = {}

        def emit_weight_load(u):
            e_, q8 = divmod(u, NQ8)
            half = (u % 2) * 8192
            first_conv = True
            for ch in range(16):
                slot = ch % 2
                if ch < 8:
                    fc = ch
                    src = h["w1"][e_, fc * 128:(fc + 1) * 128,
                                  q8 * 512:(q8 + 1) * 512]
                    dst = g.bufA[:, half + fc * 512:half + (fc + 1) * 512]
                else:
                    kk = ch - 8
                    hc, hlf = divmod(kk, 2)
                    base = half + 4096 + hc * 1024 + hlf * 512
                    src = h["w2"][e_, q8 * 512 + hc * 128:q8 * 512 + (hc + 1) * 128,
                                  hlf * 512:(hlf + 1) * 512]
                    dst = g.bufA[:, base:base + 512]
                p.wait("syncq", "pool", max(0, p.cnt["pool"] - 1))
                p.op("syncq", lambda e, s=src, sl=slot: e.dma_start(g.land(sl), s),
                     inc="dmaw", incn=16)
                if first_conv and u - 2 in pe_done_unit:
                    p.wait("pool", "pe", pe_done_unit[u - 2])
                first_conv = False
                p.wait("pool", "dmaw")
                p.op("pool", lambda e, d=dst, sl=slot: e.tensor_copy(d, g.land(sl)),
                     inc="pool")
            conv_done[u] = p.cnt["pool"]

        emit_weight_load(0)
        emit_weight_load(1)

        NU = E * NQ8
        hw_cols = lambda par, hc: g.bufB[:, par * 2048 + hc * 512:
                                         par * 2048 + (hc + 1) * 512]
        gtmp = lambda sl: g.uni[:, sl * 512:(sl + 1) * 512]

        stepc = 0
        hist_gelu = {}
        hist_add = {}
        hist_hw = {}
        hist_wb = {}
        pe_b_stop = {}

        pending = None

        def emit_w2_for(par, W2, hwcnt, ptt, pu, pstep):
            p.wait("pe", "dve", hwcnt)
            for oc in range(FC):
                bank = pB[oc % 2]
                p.wait("pe", "dve", hist_add.get(oc % 2, 0))
                for hc in range(HC4):
                    p.op("pe", lambda e, b=bank, w=W2(hc, oc),
                         r=hw_cols(par, hc), hc=hc: e.matmul(
                             b, w, r, start=(hc == 0), stop=(hc == 3)),
                         inc="pe" if hc == 3 else None)
                p.wait("dve", "pe", p.cnt["pe"])
                xs = g.xres[:, oc * 4096 + ptt * 512:oc * 4096 + (ptt + 1) * 512]
                p.op("dve", lambda e, x=xs, b=bank: e.tensor_tensor(
                    x, x, b, ALU.add), inc="dve")
                hist_add[oc % 2] = p.cnt["dve"]
            pe_b_stop[pstep] = p.cnt["pe"]
            if ptt == NTT - 1:
                pe_done_unit[pu] = p.cnt["pe"]
                if pu + 2 < NU:
                    emit_weight_load(pu + 2)

        for u in range(NU):
            e_, q8 = divmod(u, NQ8)
            half = (u % 2) * 8192
            W1 = lambda fc, hc, half=half: g.bufA[
                :, half + fc * 512 + hc * 128:half + fc * 512 + (hc + 1) * 128]
            W2 = lambda hc, oc, half=half: g.bufA[
                :, half + 4096 + hc * 1024 + oc * 128:
                half + 4096 + hc * 1024 + (oc + 1) * 128]
            p.wait("pe", "pool", conv_done[u])
            for tt in range(NTT):
                par = stepc % 2
                wbank = pW[par]
                if stepc - 2 in hist_wb:
                    p.wait("pe", "dve", hist_wb[stepc - 2])
                rhs_w = g.bufC[0:8, C_PT + tt * 512:C_PT + (tt + 1) * 512]
                p.op("pe", lambda e, wb=wbank, s=g.sel(e_), r=rhs_w: e.matmul(
                    wb, s, r, start=True, stop=True), inc="pe")
                for hc in range(HC4):
                    bank = pA[hc % 2]
                    p.wait("pe", "act", hist_gelu.get(hc % 2, 0))
                    for fc in range(FC):
                        rhs = g.xln[:, fc * 4096 + tt * 512:
                                    fc * 4096 + (tt + 1) * 512]
                        p.op("pe", lambda e, b=bank, w=W1(fc, hc), r=rhs, fc=fc:
                             e.matmul(b, w, r, start=(fc == 0), stop=(fc == 7)),
                             inc="pe" if fc == 7 else None)
                    pe_a = p.cnt["pe"]
                    p.wait("act", "pe", pe_a)
                    p.wait("act", "dve", hist_hw.get(hc % 2, 0))
                    b1c = g.b1col(e_, q8 * 4 + hc)
                    p.op("act", lambda e, b=bank, gt=gtmp(hc % 2), b1c=b1c:
                         e.activation(gt, b, AF.Gelu_apprx_tanh, bias=b1c,
                                      scale=1.0), inc="act")
                    hist_gelu[hc % 2] = p.cnt["act"]
                    p.wait("dve", "act")
                    if stepc - 2 in pe_b_stop:
                        p.wait("dve", "pe", pe_b_stop[stepc - 2])
                    dst = hw_cols(par, hc)
                    p.op("dve", lambda e, d=dst, gt=gtmp(hc % 2), wb=wbank:
                         e.tensor_tensor(d, gt, wb, ALU.mult), inc="dve")
                    hist_hw[hc % 2] = p.cnt["dve"]
                hist_wb[stepc] = p.cnt["dve"]
                if pending is not None:
                    emit_w2_for(*pending)
                pending = (par, W2, hist_wb[stepc], tt, u, stepc)
                stepc += 1
        emit_w2_for(*pending)
        pending = None
        for tt in range(NTT):
            for fc in range(FC):
                bank = pW[(tt * FC + fc) % 2]
                p.wait("pe", "dve")
                b2l = g.bufC[0:8, C_B2SB + fc * 128:C_B2SB + (fc + 1) * 128]
                rhs_w = g.bufC[0:8, C_PT + tt * 512:C_PT + (tt + 1) * 512]
                p.op("pe", lambda e, b=bank, l=b2l, r=rhs_w: e.matmul(
                    b, l, r, start=True, stop=True), inc="pe")
                p.wait("dve", "pe")
                xs = g.xres[:, fc * 4096 + tt * 512:fc * 4096 + (tt + 1) * 512]
                p.op("dve", lambda e, x=xs, b=bank: e.tensor_tensor(
                    x, x, b, ALU.add), inc="dve")
        p.wait("syncq", "dve")


def emit_attn(g, seq):
    nc, p, h = g.nc, g.p, g.h
    s = seq
    o = 1 - seq
    unib = g.uni[:, :]
    lorB = lambda oc: unib[0:16, oc * 128:(oc + 1) * 128]
    lorA = lambda fc: unib[:, 1024 + fc * 16:1024 + (fc + 1) * 16]
    xcol = lambda fc, t0, w=512: g.xln[:, fc * 4096 + s * SEQ + t0:
                                       fc * 4096 + s * SEQ + t0 + w]
    kcol = lambda fc, t0, w=512: g.xln[:, fc * 4096 + o * SEQ + t0:
                                       fc * 4096 + o * SEQ + t0 + w]
    wstg = g.bufC[:, C_PB:C_PB + 1024]
    WT = lambda kc, oc: g.bufB[:, (kc * 8 + oc) * 128:(kc * 8 + oc + 1) * 128]
    u_sb = g.bufC[0:16, C_USB:C_USB + 512]
    vst = g.bufC[:, C_VST:C_VST + 512]
    PT = lambda kc: g.bufC[:, C_PT + kc * 256:C_PT + (kc + 1) * 256]
    OT = lambda c: g.bufC[:, C_OT + c * 256:C_OT + (c + 1) * 256]
    Pb = lambda sl: g.bufC[:, C_PB + sl * 2048:C_PB + (sl + 1) * 2048]
    Vtm = lambda c, kc: g.bufA[:, (c * NKC + kc) * 128:(c * NKC + kc + 1) * 128]

    p.entry_guard("pe")
    p.entry_guard("syncq")
    with ExitStack() as ps:
        pP = [ps.enter_context(nc.psum_tensor(_u(f"aP{i}"), [128, 512], F32))[:, :]
              for i in range(8)]

        def load_W(name):
            for ocg in range(8):
                for hlf in range(2):
                    p.wait("syncq", "pool")
                    src = h[name][ocg * 128:(ocg + 1) * 128,
                                  hlf * 512:(hlf + 1) * 512]
                    p.op("syncq", lambda e, sr=src, sl=hlf: e.dma_start(
                        g.land(sl), sr), inc="dmaw", incn=16)
                    p.wait("pool", "dmaw")
                    p.wait("pool", "pe")
                    dst = wstg[:, hlf * 512:(hlf + 1) * 512]
                    p.op("pool", lambda e, d=dst, sl=hlf: e.tensor_copy(
                        d, g.land(sl)), inc="pool")
                p.wait("pe", "pool")
                for fcg in range(2):
                    bank = pP[6 + fcg]
                    p.wait("pe", "dve")
                    for j in range(4):
                        fc = fcg * 4 + j
                        p.op("pe", lambda e, b=bank, j=j, fc=fc: e.transpose(
                            b.bitcast(BF16)[:, j * 128:(j + 1) * 128],
                            wstg[:, fc * 128:(fc + 1) * 128], g.ident()),
                             inc="pe")
                    p.wait("dve", "pe")
                    for j in range(4):
                        fc = fcg * 4 + j
                        p.op("dve", lambda e, b=bank, j=j, d=WT(fc, ocg):
                             e.tensor_copy(d, b.bitcast(BF16)[:, j * 128:(j + 1) * 128]),
                             inc="dve")

        def load_lora(wn_a, wn_b):
            p.wait("syncq", "pool")
            p.wait("syncq", "pe")
            p.op("syncq", lambda e: e.dma_start(g.fpD[0:16, 0:1024], h[wn_a]),
                 inc="dmaw", incn=16)
            p.wait("pool", "dmaw")
            p.wait("pool", "pe")
            astg = wstg[0:16, 0:1024]
            p.op("pool", lambda e: e.tensor_copy(astg, g.fpD[0:16, 0:1024]),
                 inc="pool")
            p.wait("pe", "pool")
            bank = pP[6]
            p.wait("pe", "dve")
            for fc in range(8):
                p.op("pe", lambda e, b=bank, fc=fc: e.transpose(
                    b.bitcast(BF16)[:, fc * 16:(fc + 1) * 16],
                    astg[:, fc * 128:(fc + 1) * 128], g.ident()[0:16, 0:16]),
                     inc="pe")
            p.wait("dve", "pe")
            p.op("dve", lambda e, b=bank: e.tensor_copy(
                unib[:, 1024:1152], b.bitcast(BF16)[:, 0:128]), inc="dve")
            p.wait("syncq", "pool")
            land3 = g.land(0).rearrange("p (c r) -> p c r", c=8)[:, :, 0:16]
            p.op("syncq", lambda e, l3=land3: e.dma_start(
                l3, h[wn_b].rearrange("(c p) r -> p c r", c=8)),
                 inc="dmaw", incn=16)
            p.wait("pool", "dmaw")
            p.wait("pool", "pe")
            bstg = wstg[:, 0:128]
            p.op("pool", lambda e, l3=land3: e.tensor_copy(
                bstg.rearrange("p (c r) -> p c r", c=8), l3), inc="pool")
            p.wait("pe", "pool")
            bank2 = pP[7]
            p.wait("pe", "dve")
            for oc in range(8):
                p.op("pe", lambda e, b=bank2, oc=oc: e.transpose(
                    b.bitcast(BF16)[0:16, oc * 128:(oc + 1) * 128],
                    bstg[:, oc * 16:(oc + 1) * 16], g.ident()), inc="pe")
            p.wait("dve", "pe")
            p.op("dve", lambda e, b=bank2: e.tensor_copy(
                unib[0:16, 0:1024], b.bitcast(BF16)[0:16, 0:1024]), inc="dve")

        def proj(kind):
            wn = {"k": "Wk", "v": "Wv", "q": "Wq"}[kind]
            load_W(wn)
            load_lora({"k": "Ak", "v": "Av", "q": "Aq"}[kind],
                      {"k": "Bk", "v": "Bv", "q": "Bq"}[kind])
            for tt in range(4):
                t0 = tt * 512
                bank_u = pP[0]
                p.wait("pe", "act")
                for fc in range(FC):
                    p.op("pe", lambda e, b=bank_u, w=lorA(fc), r=xcol(fc, t0),
                         fc=fc: e.matmul(b[0:16, :], w, r, start=(fc == 0),
                                         stop=(fc == 7)),
                         inc="pe" if fc == 7 else None)
                p.wait("act", "pe")
                p.op("act", lambda e, b=bank_u: e.activation(
                    u_sb, b[0:16, :], AF.Copy, scale=1.0 / RANK), inc="act")
                uc = p.cnt["act"]
                if kind == "q":
                    for oc in range(FC):
                        bank = pP[oc]
                        if oc == 0:
                            p.wait("pe", "act", uc)
                        for fc in range(FC):
                            p.op("pe", lambda e, b=bank, w=WT(fc, oc),
                                 r=xcol(fc, t0), fc=fc: e.matmul(
                                     b, w, r, start=(fc == 0), stop=False))
                        p.op("pe", lambda e, b=bank, w=lorB(oc): e.matmul(
                            b, w, u_sb, start=False, stop=True), inc="pe")
                    p.wait("act", "pe")
                    for oc in range(FC):
                        dst = xcol(oc, t0)
                        p.op("act", lambda e, b=pP[oc], d=dst: e.activation(
                            d, b, AF.Copy, scale=0.125), inc="act")
                else:
                    for oc in range(FC):
                        bank = pP[1 + (oc % 2)]
                        p.wait("pe", "act")
                        if kind == "v":
                            p.wait("pe", "dve")
                        for fc in range(FC):
                            p.op("pe", lambda e, b=bank, w=WT(fc, oc),
                                 r=xcol(fc, t0), fc=fc: e.matmul(
                                     b, w, r, start=(fc == 0), stop=False))
                        p.wait("pe", "act", uc)
                        p.op("pe", lambda e, b=bank, w=lorB(oc): e.matmul(
                            b, w, u_sb, start=False, stop=True), inc="pe")
                        p.wait("act", "pe")
                        if kind == "k":
                            dst = kcol(oc, t0)
                            p.op("act", lambda e, b=bank, d=dst: e.activation(
                                d, b, AF.Copy, scale=1.0), inc="act")
                        else:
                            p.wait("act", "pe")
                            p.op("act", lambda e, b=bank: e.activation(
                                vst, b, AF.Copy, scale=1.0), inc="act")
                            bankT = pP[3 + (oc % 2)]
                            p.wait("pe", "act")
                            p.wait("pe", "dve")
                            for j in range(4):
                                p.op("pe", lambda e, b=bankT, j=j: e.transpose(
                                    b.bitcast(BF16)[:, j * 128:(j + 1) * 128],
                                    vst[:, j * 128:(j + 1) * 128], g.ident()),
                                     inc="pe")
                            p.wait("dve", "pe")
                            for j in range(4):
                                kc = tt * 4 + j
                                p.op("dve", lambda e, b=bankT, j=j,
                                     d=Vtm(oc, kc): e.tensor_copy(
                                         d, b.bitcast(BF16)[:, j * 128:(j + 1) * 128]),
                                     inc="dve")

        proj("k")
        proj("v")
        proj("q")
        load_lora("Ao", "Bo")
        load_W("Wo")
        p.wait("syncq", "dve")
        p.wait("syncq", "act")

    p.entry_guard("pe")
    with nc.psum_tensor(_u("aSA"), [128, 2048], F32) as SAh, \
         nc.psum_tensor(_u("aSB"), [128, 2048], F32) as SBh:
        SA = SAh[:, :]
        SB = SBh[:, :]
        Ssets = (SA, SB)
        negm = lambda si: g.stats(16 + si)
        ssum = lambda si: g.stats(18 + si)
        rsv = lambda si: g.stats(20 + si)
        pmax = lambda i: g.stats(24 + i)
        set_free_exp = [0, 0]
        set_pack_evac = [0, 0]
        norm_done = [0, 0]
        maxdve = {}
        pv_evac = 0
        op_add = [0, 0]

        def phaseA1(k, c, j, qt, q0g):
            si = qt
            St = Ssets[si]
            q0 = q0g + qt * 128
            prow = j * 64
            p.wait("pe", "act", max(set_free_exp[si], set_pack_evac[si]))
            p.wait("pe", "dve", max(pv_evac, op_add[0], op_add[1],
                                    norm_done[si]))
            lhs = g.xln[prow:prow + 64,
                        c * 4096 + s * SEQ + q0:
                        c * 4096 + s * SEQ + q0 + 128]
            for kb in range(4):
                rhs = g.xln[prow:prow + 64,
                            c * 4096 + o * SEQ + kb * 512:
                            c * 4096 + o * SEQ + (kb + 1) * 512]
                p.op("pe", lambda e, St=St, kb=kb, l=lhs, r=rhs:
                     e.matmul(St[:, kb * 512:(kb + 1) * 512], l, r,
                              start=True, stop=True), inc="pe")
                p.wait("dve", "pe")
                p.op("dve", lambda e, St=St, kb=kb: e.tensor_reduce(
                    pmax(kb), St[:, kb * 512:(kb + 1) * 512], AX.X, ALU.max),
                     inc="dve")
            p.wait("dve", "dve")
            p.op("dve", lambda e: e.tensor_tensor(
                pmax(0), pmax(0), pmax(1), ALU.max), inc="dve")
            p.op("dve", lambda e: e.tensor_tensor(
                pmax(2), pmax(2), pmax(3), ALU.max), inc="dve")
            p.wait("dve", "dve")
            p.op("dve", lambda e: e.tensor_tensor(
                pmax(0), pmax(0), pmax(2), ALU.max), inc="dve")
            p.wait("dve", "dve")
            p.op("dve", lambda e, si=si: e.tensor_scalar(
                negm(si), pmax(0), -1.0, None, ALU.mult, ALU.bypass),
                 inc="dve")
            maxdve[k] = p.cnt["dve"]

        def phaseA2(k, qt):
            si = qt
            St = Ssets[si]
            p.wait("act", "dve", maxdve[k])
            p.op("act", lambda e, St=St, si=si: e.activation(
                St[:, :], St[:, :], AF.Exp, bias=negm(si), scale=1.0,
                accum_out=ssum(si)), inc="act")
            set_free_exp[si] = p.cnt["act"]
            p.wait("dve", "act", p.cnt["act"])
            p.op("dve", lambda e, si=si: e.reciprocal(rsv(si), ssum(si)),
                 inc="dve")
            p.wait("dve", "dve")
            p.op("dve", lambda e, si=si, St=St: e.tensor_scalar(
                Pb(si), St[:, :], rsv(si), None, ALU.mult, ALU.bypass),
                 inc="dve")
            norm_done[si] = p.cnt["dve"]

        def phaseB(k, c, j, qt):
            si = qt
            St = Ssets[si]
            p.wait("pe", "dve", norm_done[si])
            for pk in range(4):
                for jj in range(4):
                    kc = pk * 4 + jj
                    p.op("pe", lambda e, St=St, pk=pk, jj=jj,
                         kc=kc, si=si: e.transpose(
                             St.bitcast(BF16)[
                                 :, pk * 1024 + jj * 128:
                                 pk * 1024 + (jj + 1) * 128],
                             Pb(si)[:, kc * 128:(kc + 1) * 128],
                             g.ident()),
                         inc="pe" if jj == 3 else None)
                p.wait("act", "pe")
                base = C_PT + pk * 4 * 256
                dst = g.bufC[:, base:base + 1024].rearrange(
                    "p (k v) -> p k v", v=256)[:, :, qt * 128:
                                               qt * 128 + 128]
                srcv = St.bitcast(BF16)[:, pk * 1024:pk * 1024 + 512]
                p.op("act", lambda e, d=dst, sv=srcv: e.copy(
                    d, sv.rearrange("p (k v) -> p k v", v=128)),
                     inc="act")
            set_pack_evac[si] = p.cnt["act"]
            if qt == 1:
                p.wait("pe", "act", set_pack_evac[1])
                p.wait("pe", "dve", pv_evac)
                for kc in range(NKC):
                    p.op("pe", lambda e, kc=kc, c=c, j=j: e.matmul(
                        SB[j * 64:(j + 1) * 64, 0:256],
                        Vtm(c, kc)[:, j * 64:(j + 1) * 64],
                        PT(kc), start=(kc == 0), stop=(kc == 15)),
                         inc="pe" if kc == 15 else None)
                p.wait("dve", "pe")
                dsto = OT(c)[j * 64:(j + 1) * 64, :]
                p.op("dve", lambda e, d=dsto, j=j: e.tensor_copy(
                    d, SB[j * 64:(j + 1) * 64, 0:256]), inc="dve")
                nonlocal_pv[0] = p.cnt["dve"]

        nonlocal_pv = [0]

        for qg in range(NQG):
            q0g = qg * QG
            steps = [(c, j, qt)
                     for c in range(PAIRS) for j in (0, 1) for qt in (0, 1)]
            for k, (c, j, qt) in enumerate(steps):
                if k >= 1:
                    phaseA2(k - 1, steps[k - 1][2])
                if k >= 2:
                    phaseB(k - 2, *steps[k - 2])
                    pv_evac = nonlocal_pv[0]
                phaseA1(k, c, j, qt, q0g)
            phaseA2(31, steps[31][2])
            phaseB(30, *steps[30])
            pv_evac = nonlocal_pv[0]
            phaseB(31, *steps[31])
            pv_evac = nonlocal_pv[0]
            bank_u = SB
            p.wait("pe", "act", max(set_free_exp[1], set_pack_evac[1]))
            p.wait("pe", "dve", pv_evac)
            for pc in range(PAIRS):
                p.op("pe", lambda e, pc=pc: e.matmul(
                    bank_u[0:16, 0:256], lorA(pc), OT(pc),
                    start=(pc == 0), stop=(pc == 7)),
                     inc="pe" if pc == 7 else None)
            p.wait("act", "pe")
            p.op("act", lambda e: e.activation(
                u_sb[:, 0:256], bank_u[0:16, 0:256], AF.Copy, scale=1.0 / RANK),
                 inc="act")
            uc2 = p.cnt["act"]
            for oc in range(FC):
                off = (oc % 2) * 512
                if oc < 2:
                    if oc == 0:
                        p.wait("pe", "act", max(set_free_exp[0],
                                                set_pack_evac[0], uc2))
                    p.wait("pe", "dve", pv_evac)
                else:
                    p.wait("pe", "dve", op_add[oc % 2])
                for pc in range(PAIRS):
                    p.op("pe", lambda e, oc=oc, pc=pc, off=off: e.matmul(
                        SA[:, off:off + 256], WT(pc, oc), OT(pc),
                        start=(pc == 0), stop=False))
                p.op("pe", lambda e, oc=oc, off=off: e.matmul(
                    SA[:, off:off + 256], lorB(oc), u_sb[:, 0:256],
                    start=False, stop=True), inc="pe")
                p.wait("dve", "pe")
                p.wait("dve", "dve")
                bocol = g.stats(32 + oc)
                p.op("dve", lambda e, oc=oc, off=off, bc=bocol: e.tensor_scalar(
                    vst[:, 0:256], SA[:, off:off + 256], bc, None,
                    ALU.add, ALU.bypass), inc="dve")
                p.wait("dve", "dve")
                xs = g.xres[:, oc * 4096 + s * SEQ + q0g:
                            oc * 4096 + s * SEQ + q0g + 256]
                p.op("dve", lambda e, x=xs: e.tensor_tensor(
                    x, x, vst[:, 0:256], ALU.add), inc="dve")
                op_add[oc % 2] = p.cnt["dve"]
        p.wait("syncq", "dve")


def emit_debug_dump(g, sbuf, name):
    p = g.p
    p.entry_guard("syncq")
    p.op("syncq", lambda e: e.dma_start(g.h[name], sbuf[:, :]),
         inc="dmaio", incn=16)
    p.wait("syncq", "dmaio")


def emit_store(g):
    nc, p, h = g.nc, g.p, g.h
    p.entry_guard("pe")
    with nc.psum_tensor(_u("spk0"), [128, 512], BF16) as pk0, \
         nc.psum_tensor(_u("spk1"), [128, 512], BF16) as pk1:
        pks = (pk0, pk1)
        for tc in range(NTC):
            t0 = tc * 128
            for k in range(2):
                pk = pks[k]
                p.wait("pe", "act")
                for j in range(4):
                    fc = k * 4 + j
                    src = g.xres[:, fc * 4096 + t0:fc * 4096 + t0 + 128]
                    p.op("pe", lambda e, pk=pk, j=j, srt=src: e.transpose(
                        pk[:, j * 128:(j + 1) * 128], srt, g.ident()), inc="pe")
                p.wait("act", "pe")
                p.wait("act", "st")
                dst = g.fpD[:, k * 512:(k + 1) * 512]
                p.op("act", lambda e, d=dst, pk=pk: e.copy(d, pk[:, :]),
                     inc="act")
            p.wait("syncq", "act")
            p.op("syncq", lambda e, t0=t0: e.dma_start(
                h["out"][t0:t0 + 128, :], g.fpD[:, 0:1024]), inc="st", incn=16)
        p.wait("syncq", "st")


def build_nc():
    nc = bass.Bass("TRN2", target_bir_lowering=False)
    h = {}
    h["x"] = _hbm(nc, "x", (T, D), F32)
    for w in ("Wq", "Wk", "Wv", "Wo"):
        h[w] = _hbm(nc, w, (D, D), F32)
    h["bo"] = _hbm(nc, "bo", (1, D), F32)
    for a in ("Aq", "Ak", "Av", "Ao"):
        h[a] = _hbm(nc, a, (RANK, D), F32)
    for b in ("Bq", "Bk", "Bv", "Bo"):
        h[b] = _hbm(nc, b, (D, RANK), F32)
    h["wr"] = _hbm(nc, "wr", (D, E), F32)
    h["w1"] = _hbm(nc, "w1", (E, D, 4 * D), F32)
    h["b1"] = _hbm(nc, "b1", (E, 4 * D), F32)
    h["w2"] = _hbm(nc, "w2", (E, 4 * D, D), F32)
    h["b2"] = _hbm(nc, "b2", (E, D), F32)
    h["cbf"] = _hbm(nc, "cbf", (128, 1408), BF16)
    h["cf32"] = _hbm(nc, "cf32", (8, 8), F32)
    h["out"] = _hbm(nc, "out", (T, D), F32, out=True)
    if DEBUG:
        h["dbg_xln"] = _hbm(nc, "dbg_xln", (128, 32768), BF16, out=True)
        h["dbg_x1"] = _hbm(nc, "dbg_x1", (128, 32768), BF16, out=True)
        h["dbg_x2"] = _hbm(nc, "dbg_x2", (128, 32768), BF16, out=True)
        h["dbg_wts"] = _hbm(nc, "dbg_wts", (128, 256), F32, out=True)
        h["dbg_q"] = _hbm(nc, "dbg_q", (128, 32768), BF16, out=True)
        h["dbg_vtm"] = _hbm(nc, "dbg_vtm", (128, 16384), BF16, out=True)
        h["dbg_bufc"] = _hbm(nc, "dbg_bufc", (128, CB_COLS), BF16, out=True)

    with ExitStack() as stack:
        sb = lambda name, shape, dt: stack.enter_context(
            nc.sbuf_tensor(name, shape, dt))
        bufs = dict(
            xres=sb("xres", [128, 32768], BF16),
            xln=sb("xln", [128, 32768], BF16),
            bufA=sb("bufA", [128, 16384], BF16),
            bufB=sb("bufB", [128, 8192], BF16),
            bufC=sb("bufC", [128, CB_COLS], BF16),
            fpD=sb("fpD", [128, 1088], F32),
            uni=sb("uni", [128, 1152], BF16),
            cb=sb("cb", [128, 1408], BF16),
            cf=sb("cf", [8, 8], F32),
            rW=sb("rW", [128, 64], BF16),
        )
        p = Prog(nc, stack)
        g = G(nc, p, h, **bufs)

        emit_init(g)
        p.flush_block("init")
        emit_ln(g, seqs=(0, 1), from_input=True)
        p.flush_block("ln1")
        if DEBUG:
            emit_debug_dump(g, g.xln, "dbg_xln")
            p.flush_block("dbgln")
        emit_router(g)
        p.flush_block("rt1")
        emit_ffn(g)
        p.flush_block("ffn1")
        if DEBUG:
            emit_debug_dump(g, g.xres, "dbg_x1")
            p.flush_block("dbg1")
        for s in (0, 1):
            emit_ln(g, seqs=(s,), from_input=False)
            p.flush_block(f"ln2s{s}")
            emit_attn(g, seq=s)
            p.flush_block(f"attns{s}")
            if DEBUG and s == 0:
                emit_debug_dump(g, g.xln, "dbg_q")
                emit_debug_dump(g, g.bufA, "dbg_vtm")
                emit_debug_dump(g, g.bufC, "dbg_bufc")
                p.flush_block("dbgattn")
        if DEBUG:
            emit_debug_dump(g, g.xres, "dbg_x2")
            p.flush_block("dbg2")
        emit_ln(g, seqs=(0, 1), from_input=False)
        p.flush_block("ln3")
        emit_load_biases(g)
        p.flush_block("bias2")
        emit_router(g)
        p.flush_block("rt3")
        emit_ffn(g)
        p.flush_block("ffn2")
        emit_store(g)
        p.flush_block("store")
    return nc


def _make_consts():
    cbf = np.zeros((128, 1408), dtype=ml_dtypes.bfloat16)
    cbf[:, CONST_ID:CONST_ID + 128] = np.eye(128, dtype=ml_dtypes.bfloat16)
    for e in range(8):
        cbf[e, CONST_SEL + e * 128:CONST_SEL + (e + 1) * 128] = 1.0
    cbf[0, CONST_ONES:CONST_ONES + 256] = 1.0
    cf32 = np.eye(8, dtype=np.float32)
    return cbf, cf32


_NC_CACHE = [None]


def kernel(L0_attn_x, L0_attn_Wq, L0_attn_Wk, L0_attn_Wv, L0_attn_Wo, L0_attn_bo,
           L0_attn_Aq, L0_attn_Bq, L0_attn_Ak, L0_attn_Bk, L0_attn_Av, L0_attn_Bv,
           L0_attn_Ao, L0_attn_Bo, L0_ffn_w_router, L0_ffn_w1, L0_ffn_b1,
           L0_ffn_w2, L0_ffn_b2):
    x = np.ascontiguousarray(np.asarray(L0_attn_x, dtype=np.float32))
    cbf, cf32 = _make_consts()
    f32c = lambda a: np.ascontiguousarray(np.asarray(a, dtype=np.float32))
    shared = {
        "Wq": f32c(L0_attn_Wq), "Wk": f32c(L0_attn_Wk),
        "Wv": f32c(L0_attn_Wv), "Wo": f32c(L0_attn_Wo),
        "bo": f32c(L0_attn_bo).reshape(1, D),
        "Aq": f32c(L0_attn_Aq), "Ak": f32c(L0_attn_Ak),
        "Av": f32c(L0_attn_Av), "Ao": f32c(L0_attn_Ao),
        "Bq": f32c(L0_attn_Bq), "Bk": f32c(L0_attn_Bk),
        "Bv": f32c(L0_attn_Bv), "Bo": f32c(L0_attn_Bo),
        "wr": f32c(L0_ffn_w_router),
        "w1": f32c(L0_ffn_w1), "b1": f32c(L0_ffn_b1),
        "w2": f32c(L0_ffn_w2), "b2": f32c(L0_ffn_b2),
        "cbf": cbf, "cf32": cf32,
    }
    if _NC_CACHE[0] is None:
        _NC_CACHE[0] = build_nc()
    nc = _NC_CACHE[0]
    in_maps = []
    for c in range(N_CORES):
        m = dict(shared)
        m["x"] = np.ascontiguousarray(x[2 * c:2 * c + 2].reshape(T, D))
        in_maps.append(m)
    res = run_bass_kernel_spmd(nc, in_maps, core_ids=list(range(N_CORES)))
    outs = [np.asarray(r["out"]).reshape(2, SEQ, D) for r in res.results]
    return np.concatenate(outs, axis=0).astype(np.float32)


# version 11
# speedup vs baseline: 1.7839x; 1.0623x over previous
import numpy as np
import ml_dtypes
import concourse.bass as bass
import concourse.mybir as mybir
from concourse.bass_utils import run_bass_kernel_spmd
from contextlib import ExitStack

F32 = mybir.dt.float32
BF16 = mybir.dt.bfloat16
AF = mybir.ActivationFunctionType
ALU = mybir.AluOpType
AX = mybir.AxisListType

DEBUG = 0
N_CORES = 8
T = 4096
SEQ = 2048
D = 1024
FC = 8
NTT = 8
NTC = 32
E = 8
NQ8 = 8
HC4 = 4
RANK = 16
PAIRS = 8
NKC = 16
QG = 256
NQG = 8
EPS = 1e-5

C_PT = 0
C_PB = 4096
C_OT = 8192
C_VST = 10240
C_USB = 10752
CB_COLS = 11264
C_XLNTM = 4096
C_XTM = 5120
C_B2SB = 8192
C_B1COLS = 9216
C_WTSTM = 9728
C_PRB = 10240

CONST_ID = 0
CONST_SEL = 128
CONST_ONES = 1152

D_LAND = 0
D_STATS = 1024


_UID = [0]


def _u(name):
    _UID[0] += 1
    return f"{name}_{_UID[0]}"


def _hbm(nc, name, shape, dtype, out=False):
    return nc.dram_tensor(name, list(shape), dtype,
                          kind="ExternalOutput" if out else "ExternalInput").ap()


class Prog:
    ENGS = ("pe", "act", "dve", "pool", "syncq")

    def __init__(self, nc, stack):
        self.nc = nc
        self.stack = stack
        self.ops = {e: [] for e in self.ENGS}
        self.sems = {}
        self.cnt = {}

    def new_sems(self, names):
        for n in names:
            self.sems[n] = self.stack.enter_context(self.nc.semaphore(n))
            self.cnt[n] = 0

    def op(self, eng, fn, inc=None, incn=1):
        if inc is None:
            self.ops[eng].append(lambda e, fn=fn: fn(e))
        else:
            self.cnt[inc] += incn
            s = self.sems[inc]
            self.ops[eng].append(
                lambda e, fn=fn, s=s, incn=incn: fn(e).then_inc(s, incn))

    def wait(self, eng, sem, val=None):
        if val is None:
            val = self.cnt[sem]
        if val <= 0:
            return
        s = self.sems[sem]
        self.ops[eng].append(lambda e, s=s, v=val: e.wait_ge(s, v))

    def entry_guard(self, eng="pe"):
        for s in ("act", "dve", "pool"):
            self.wait(eng, s)

    def flush_block(self, name):
        nc = self.nc
        with nc.Block(name) as blk:
            for eng, deco in (("pe", blk.tensor), ("act", blk.scalar),
                              ("dve", blk.vector), ("pool", blk.gpsimd),
                              ("syncq", blk.sync)):
                ops = self.ops[eng]
                if not ops:
                    continue

                def body(e, ops=ops):
                    for f in ops:
                        f(e)
                deco(body)
        self.ops = {e: [] for e in self.ENGS}


class G:
    def __init__(self, nc, p, h, **bufs):
        self.nc, self.p, self.h = nc, p, h
        self.__dict__.update(bufs)
        p.new_sems(["dmaw", "dmaio", "pe", "act", "dve", "pool", "st"])

    def ident(self):
        return self.cb[:, CONST_ID:CONST_ID + 128]

    def ident_f(self):
        return self.cf[0:8, 0:8]

    def sel(self, e):
        return self.cb[0:8, CONST_SEL + e * 128:CONST_SEL + (e + 1) * 128]

    def ones_row(self, n):
        return self.cb[0:1, CONST_ONES:CONST_ONES + n]

    def stats(self, i, w=1):
        return self.fpD[:, D_STATS + i:D_STATS + i + w]

    def land(self, slot):
        return self.fpD[:, D_LAND + slot * 512:D_LAND + (slot + 1) * 512]

    def land2(self):
        return self.fpD[:, D_LAND:D_LAND + 1024]

    def b1col(self, e, hc_abs):
        v = self.bufC[:, C_B1COLS:C_B1COLS + 512].bitcast(F32)
        return v[:, hc_abs * 8 + e:hc_abs * 8 + e + 1]

    def wts_tm_f(self):
        return self.bufC[:, C_WTSTM:C_WTSTM + 512].bitcast(F32)

    def prb(self):
        return self.bufC[:, C_PRB:C_PRB + 256].bitcast(F32)


def emit_load_biases(g):
    nc, p, h = g.nc, g.p, g.h
    p.entry_guard("syncq")
    p.op("syncq", lambda e: e.dma_start(g.fpD[0:8, 0:1024], h["b2"]),
         inc="dmaio", incn=16)
    p.wait("pool", "dmaio")
    p.op("pool", lambda e: e.tensor_copy(
        g.bufC[0:8, C_B2SB:C_B2SB + 1024], g.fpD[0:8, 0:1024]), inc="pool")
    bA32 = g.bufA[:, 0:8192].bitcast(F32)
    p.op("syncq", lambda e: e.dma_start(bA32[0:8, 0:4096], h["b1"]),
         inc="dmaio", incn=16)
    p.wait("pe", "dmaio")
    p.entry_guard("pe")
    with nc.psum_tensor(_u("pbias"), [128, 512], F32) as pi0:
        for grp in range(2):
            if grp:
                p.wait("pe", "dve")
            for k in range(16):
                hc = grp * 16 + k
                src = bA32[0:8, hc * 128:(hc + 1) * 128]
                p.op("pe", lambda e, s=src, d=pi0[:, k * 8:(k + 1) * 8]:
                     e.transpose(d, s, g.ident_f()), inc="pe")
            p.wait("dve", "pe")
            dstc = g.bufC[:, C_B1COLS + grp * 256:
                          C_B1COLS + (grp + 1) * 256].bitcast(F32)
            p.op("dve", lambda e, d=dstc: e.tensor_copy(d, pi0[:, 0:128]),
                 inc="dve")
    p.wait("syncq", "dve")
    p.wait("syncq", "pool")


def emit_init(g):
    nc, p, h = g.nc, g.p, g.h
    p.op("syncq", lambda e: e.dma_start(g.cb[:, :], h["cbf"]), inc="dmaio", incn=16)
    p.op("syncq", lambda e: e.dma_start(g.cf[:, :], h["cf32"]), inc="dmaio", incn=16)
    land3 = g.land(0).rearrange("p (c r) -> p c r", c=8)[:, :, 0:8]
    p.op("syncq", lambda e: e.dma_start(
        land3, h["wr"].rearrange("(c p) r -> p c r", c=8)), inc="dmaio", incn=16)
    p.wait("pool", "dmaio")
    p.op("pool", lambda e: e.tensor_copy(
        g.rW[:, :].rearrange("p (c r) -> p c r", r=8), land3), inc="pool")
    p.wait("syncq", "pool")
    p.op("syncq", lambda e: e.dma_start(g.fpD[0:1, 0:1024], h["bo"]),
         inc="dmaio", incn=16)
    p.wait("pe", "dmaio")
    with nc.psum_tensor(_u("pbo"), [128, 512], F32) as pbo:
        for oc in range(8):
            src = g.fpD[0:1, oc * 128:(oc + 1) * 128]
            p.op("pe", lambda e, s2=src, d=pbo[:, oc:oc + 1]: e.transpose(
                d, s2, g.ident_f()[0:1, 0:1]), inc="pe")
        p.wait("dve", "pe")
        p.op("dve", lambda e: e.tensor_copy(g.stats(32, 8), pbo[:, 0:8]),
             inc="dve")
    p.wait("syncq", "dve")
    emit_load_biases(g)


def emit_ln(g, seqs, from_input):
    nc, p, h = g.nc, g.p, g.h
    tcs = [(s, tc) for s in seqs for tc in range(16)]
    p.entry_guard("pe")
    with nc.psum_tensor(_u("lnpkA"), [128, 512], BF16) as pkA, \
         nc.psum_tensor(_u("lnpkB"), [128, 512], BF16) as pkB, \
         nc.psum_tensor(_u("lnsq"), [128, 1024], F32) as psq:
        pks = (pkA, pkB)
        hist = {0: ("dve", 0), 1: ("dve", 0)}
        sm = g.stats(0)
        sq = g.stats(1)
        rstd = g.stats(2)
        nmrs = g.stats(3)
        for s, tc in tcs:
            t0 = s * SEQ + tc * 128
            xtm = g.bufC[:, C_XTM:C_XTM + 1024]
            xlntm = g.bufC[:, C_XLNTM:C_XLNTM + 1024]
            if from_input:
                p.wait("syncq", "act")
                p.wait("syncq", "pool")
                p.wait("syncq", "dve")
                p.op("syncq", lambda e, t0=t0: e.dma_start(
                    g.land2(), h["x"][t0:t0 + 128, :]), inc="dmaio", incn=16)
                p.wait("dve", "dmaio")
                p.op("dve", lambda e: e.reduce_sum(sm, g.land2(), AX.X), inc="dve")
                p.wait("act", "dmaio")
                p.wait("act", "dve", p.cnt["dve"] - 1)
                p.op("act", lambda e: e.activation(psq[:, :], g.land2(), AF.Square),
                     inc="act")
                p.wait("dve", "act")
                p.op("dve", lambda e: e.reduce_sum(sq, psq[:, :], AX.X), inc="dve")
                p.wait("pool", "dmaio")
                p.wait("pool", "pe")
                p.op("pool", lambda e: e.tensor_copy(xtm, g.land2()), inc="pool")
                p.wait("pe", "pool")
                for k in range(2):
                    pk = pks[k]
                    heng, hcnt = hist[k]
                    p.wait("pe", heng, hcnt)
                    for j in range(4):
                        fc = k * 4 + j
                        p.op("pe", lambda e, pk=pk, j=j, fc=fc: e.transpose(
                            pk[:, j * 128:(j + 1) * 128],
                            xtm[:, fc * 128:(fc + 1) * 128], g.ident()), inc="pe")
                    p.wait("dve", "pe")
                    dst = g.xres.rearrange("p (f t) -> p f t", f=8)[
                        :, k * 4:(k + 1) * 4, t0:t0 + 128]
                    p.op("dve", lambda e, d=dst, pk=pk: e.tensor_copy(
                        d, pk[:, :].rearrange("p (j t) -> p j t", j=4)), inc="dve")
                    hist[k] = ("dve", p.cnt["dve"])
            else:
                for k in range(2):
                    pk = pks[k]
                    heng, hcnt = hist[k]
                    p.wait("pe", heng, hcnt)
                    if k == 0:
                        p.wait("pe", "dve")
                        p.wait("pe", "act")
                    for j in range(4):
                        fc = k * 4 + j
                        src = g.xres[:, fc * 4096 + t0:fc * 4096 + t0 + 128]
                        p.op("pe", lambda e, pk=pk, j=j, s=src: e.transpose(
                            pk[:, j * 128:(j + 1) * 128], s, g.ident()), inc="pe")
                    p.wait("act", "pe")
                    dst = xtm.rearrange("p (f t) -> p f t", f=8)[:, k * 4:(k + 1) * 4, :]
                    p.op("act", lambda e, d=dst, pk=pk: e.copy(
                        d, pk[:, :].rearrange("p (j t) -> p j t", j=4)), inc="act")
                    hist[k] = ("act", p.cnt["act"])
                p.wait("dve", "act")
                p.op("dve", lambda e: e.reduce_sum(sm, xtm, AX.X), inc="dve")
                p.wait("act", "act")
                p.wait("act", "dve", p.cnt["dve"] - 1)
                p.op("act", lambda e: e.activation(psq[:, :], xtm, AF.Square),
                     inc="act")
                p.wait("dve", "act")
                p.op("dve", lambda e: e.reduce_sum(sq, psq[:, :], AX.X), inc="dve")
            n = float(D)
            p.wait("dve", "dve")
            p.op("dve", lambda e: e.tensor_scalar(sm, sm, 1.0 / n, None,
                                                  ALU.mult, ALU.bypass), inc="dve")
            p.wait("dve", "dve")
            p.op("dve", lambda e: e.tensor_tensor(rstd, sm, sm, ALU.mult), inc="dve")
            p.op("dve", lambda e: e.tensor_scalar(sq, sq, 1.0 / n, None,
                                                  ALU.mult, ALU.bypass), inc="dve")
            p.wait("dve", "dve")
            p.op("dve", lambda e: e.tensor_scalar(rstd, rstd, EPS, None,
                                                  ALU.subtract, ALU.bypass),
                 inc="dve")
            p.wait("dve", "dve")
            p.op("dve", lambda e: e.tensor_tensor(sq, sq, rstd, ALU.subtract),
                 inc="dve")
            p.wait("act", "dve")
            p.op("act", lambda e: e.activation(rstd, sq, AF.Sqrt), inc="act")
            p.wait("dve", "act")
            p.op("dve", lambda e: e.reciprocal(rstd, rstd), inc="dve")
            p.wait("dve", "dve")
            p.op("dve", lambda e: e.tensor_tensor(nmrs, sm, rstd, ALU.mult),
                 inc="dve")
            p.wait("dve", "dve")
            p.op("dve", lambda e: e.tensor_scalar(nmrs, nmrs, -1.0, None,
                                                  ALU.mult, ALU.bypass), inc="dve")
            p.wait("dve", "dve")
            src_aff = g.land2() if from_input else xtm
            p.op("dve", lambda e, s=src_aff: e.tensor_scalar(
                xlntm, s, rstd, nmrs, ALU.mult, ALU.add), inc="dve")
            aff_cnt = p.cnt["dve"]
            for k in range(2):
                pk = pks[k]
                heng, hcnt = hist[k]
                p.wait("pe", heng, hcnt)
                p.wait("pe", "dve", aff_cnt)
                for j in range(4):
                    fc = k * 4 + j
                    p.op("pe", lambda e, pk=pk, j=j, fc=fc: e.transpose(
                        pk[:, j * 128:(j + 1) * 128],
                        xlntm[:, fc * 128:(fc + 1) * 128], g.ident()), inc="pe")
                p.wait("dve", "pe")
                dst = g.xln.rearrange("p (f t) -> p f t", f=8)[
                    :, k * 4:(k + 1) * 4, t0:t0 + 128]
                p.op("dve", lambda e, d=dst, pk=pk: e.tensor_copy(
                    d, pk[:, :].rearrange("p (j t) -> p j t", j=4)), inc="dve")
                hist[k] = ("dve", p.cnt["dve"])
        p.wait("syncq", "dve")


def emit_router(g):
    nc, p = g.nc, g.p
    p.entry_guard("pe")
    logT = g.fpD[0:8, 0:512]
    with nc.psum_tensor(_u("rlog"), [128, 512], F32) as plog, \
         nc.psum_tensor(_u("rtm"), [128, 512], F32) as ptm, \
         nc.psum_tensor(_u("rpk"), [128, 512], BF16) as ppk:
        for tt in range(NTT):
            if tt:
                p.wait("pe", "dve")
            for fc in range(FC):
                rhs = g.xln[:, fc * 4096 + tt * 512:fc * 4096 + (tt + 1) * 512]
                p.op("pe", lambda e, fc=fc, rhs=rhs: e.matmul(
                    plog[0:8, :], g.rW[:, fc * 8:(fc + 1) * 8], rhs,
                    start=(fc == 0), stop=(fc == 7)),
                     inc="pe" if fc == 7 else None)
            p.wait("dve", "pe")
            p.op("dve", lambda e: e.tensor_copy(logT, plog[0:8, :]), inc="dve")
            p.wait("pe", "dve")
            for k in range(4):
                p.op("pe", lambda e, k=k: e.transpose(
                    ptm[:, k * 8:(k + 1) * 8], logT[:, k * 128:(k + 1) * 128],
                    g.ident_f()), inc="pe")
            p.wait("dve", "pe")
            for k in range(4):
                tc = tt * 4 + k
                lg = ptm[:, k * 8:(k + 1) * 8]
                prb = g.prb()
                ex = prb[:, 0:8]
                ismax = prb[:, 8:16]
                msk = prb[:, 16:24]
                nm = g.stats(8)
                smx = g.stats(9)
                rs = g.stats(10)
                m2 = g.stats(11)
                p.wait("dve", "dve")
                p.op("dve", lambda e, lg=lg: e.tensor_reduce(
                    nm, lg, AX.X, ALU.max, negate=True), inc="dve")
                p.wait("act", "dve")
                p.op("act", lambda e, lg=lg, ex=ex: e.activation(
                    ex, lg, AF.Exp, bias=nm, scale=1.0, accum_out=smx), inc="act")
                p.wait("dve", "act")
                p.op("dve", lambda e: e.reciprocal(rs, smx), inc="dve")
                p.wait("dve", "dve")
                p.op("dve", lambda e, ex=ex: e.tensor_scalar(
                    ex, ex, rs, 0.5, ALU.mult, ALU.mult), inc="dve")
                p.wait("dve", "dve")
                p.op("dve", lambda e, ex=ex: e.tensor_reduce(
                    nm, ex, AX.X, ALU.max), inc="dve")
                p.wait("dve", "dve")
                p.op("dve", lambda e, ex=ex, ismax=ismax: e.tensor_scalar(
                    ismax, ex, nm, 1e30, ALU.is_ge, ALU.mult), inc="dve")
                p.wait("dve", "dve")
                p.op("dve", lambda e, ex=ex, ismax=ismax, msk=msk: e.tensor_tensor(
                    msk, ex, ismax, ALU.subtract), inc="dve")
                p.wait("dve", "dve")
                p.op("dve", lambda e, msk=msk: e.tensor_reduce(
                    m2, msk, AX.X, ALU.max), inc="dve")
                p.wait("dve", "dve")
                p.op("dve", lambda e, ex=ex, msk=msk: e.tensor_scalar(
                    msk, ex, m2, None, ALU.is_ge, ALU.bypass), inc="dve")
                p.wait("dve", "dve")
                wdst = g.wts_tm_f()[:, tc * 8:(tc + 1) * 8]
                p.op("dve", lambda e, wdst=wdst, ex=ex, msk=msk: e.tensor_tensor(
                    wdst, ex, msk, ALU.mult), inc="dve")
        wbf = g.bufC[:, C_VST:C_VST + 256]
        p.wait("pool", "dve")
        p.op("pool", lambda e: e.tensor_copy(wbf, g.wts_tm_f()), inc="pool")
        p.wait("pe", "pool")
        for grp in range(8):
            if grp:
                p.wait("pe", "dve")
            for k in range(4):
                tc = grp * 4 + k
                p.op("pe", lambda e, tc=tc, k=k: e.transpose(
                    ppk[0:8, k * 128:(k + 1) * 128],
                    wbf[:, tc * 8:(tc + 1) * 8], g.ident()), inc="pe")
            p.wait("dve", "pe")
            dst = g.bufC[0:8, C_PT + grp * 512:C_PT + (grp + 1) * 512]
            p.op("dve", lambda e, d=dst: e.tensor_copy(d, ppk[0:8, 0:512]),
                 inc="dve")
        p.wait("syncq", "dve")
        if DEBUG:
            p.op("syncq", lambda e: e.dma_start(
                g.h["dbg_wts"], g.wts_tm_f()), inc="dmaio", incn=16)
            p.wait("syncq", "dmaio")


def emit_ffn(g):
    nc, p, h = g.nc, g.p, g.h
    p.entry_guard("pe")
    p.entry_guard("syncq")
    with ExitStack() as ps:
        pt = lambda nm: ps.enter_context(nc.psum_tensor(_u(nm), [128, 512], F32))[:, :]
        pA = (pt("fA0"), pt("fA1"))
        pB = (pt("fB0"), pt("fB1"))
        pW = (pt("fW0"), pt("fW1"))

        conv_done = {}
        pe_done_unit = {}

        def emit_weight_load(u):
            e_, q8 = divmod(u, NQ8)
            half = (u % 2) * 8192
            first_conv = True
            for ch in range(16):
                slot = ch % 2
                if ch < 8:
                    fc = ch
                    src = h["w1"][e_, fc * 128:(fc + 1) * 128,
                                  q8 * 512:(q8 + 1) * 512]
                    dst = g.bufA[:, half + fc * 512:half + (fc + 1) * 512]
                else:
                    kk = ch - 8
                    hc, hlf = divmod(kk, 2)
                    base = half + 4096 + hc * 1024 + hlf * 512
                    src = h["w2"][e_, q8 * 512 + hc * 128:q8 * 512 + (hc + 1) * 128,
                                  hlf * 512:(hlf + 1) * 512]
                    dst = g.bufA[:, base:base + 512]
                p.wait("syncq", "pool", max(0, p.cnt["pool"] - 1))
                p.op("syncq", lambda e, s=src, sl=slot: e.dma_start(g.land(sl), s),
                     inc="dmaw", incn=16)
                if first_conv and u - 2 in pe_done_unit:
                    p.wait("pool", "pe", pe_done_unit[u - 2])
                first_conv = False
                p.wait("pool", "dmaw")
                p.op("pool", lambda e, d=dst, sl=slot: e.tensor_copy(d, g.land(sl)),
                     inc="pool")
            conv_done[u] = p.cnt["pool"]

        emit_weight_load(0)
        emit_weight_load(1)

        NU = E * NQ8
        hw_cols = lambda par, hc: g.bufB[:, par * 2048 + hc * 512:
                                         par * 2048 + (hc + 1) * 512]
        gtmp = lambda sl: g.uni[:, sl * 512:(sl + 1) * 512]

        stepc = 0
        hist_gelu = {}
        hist_add = {}
        hist_hw = {}
        hist_wb = {}
        pe_b_stop = {}

        pending = None

        def emit_w2_for(par, W2, hwcnt, ptt, pu, pstep):
            p.wait("pe", "dve", hwcnt)
            for oc in range(FC):
                bank = pB[oc % 2]
                p.wait("pe", "dve", hist_add.get(oc % 2, 0))
                for hc in range(HC4):
                    p.op("pe", lambda e, b=bank, w=W2(hc, oc),
                         r=hw_cols(par, hc), hc=hc: e.matmul(
                             b, w, r, start=(hc == 0), stop=(hc == 3)),
                         inc="pe" if hc == 3 else None)
                p.wait("dve", "pe", p.cnt["pe"])
                xs = g.xres[:, oc * 4096 + ptt * 512:oc * 4096 + (ptt + 1) * 512]
                p.op("dve", lambda e, x=xs, b=bank: e.tensor_tensor(
                    x, x, b, ALU.add), inc="dve")
                hist_add[oc % 2] = p.cnt["dve"]
            pe_b_stop[pstep] = p.cnt["pe"]
            if ptt == NTT - 1:
                pe_done_unit[pu] = p.cnt["pe"]
                if pu + 2 < NU:
                    emit_weight_load(pu + 2)

        for u in range(NU):
            e_, q8 = divmod(u, NQ8)
            half = (u % 2) * 8192
            W1 = lambda fc, hc, half=half: g.bufA[
                :, half + fc * 512 + hc * 128:half + fc * 512 + (hc + 1) * 128]
            W2 = lambda hc, oc, half=half: g.bufA[
                :, half + 4096 + hc * 1024 + oc * 128:
                half + 4096 + hc * 1024 + (oc + 1) * 128]
            p.wait("pe", "pool", conv_done[u])
            for tt in range(NTT):
                par = stepc % 2
                wbank = pW[par]
                if stepc - 2 in hist_wb:
                    p.wait("pe", "dve", hist_wb[stepc - 2])
                rhs_w = g.bufC[0:8, C_PT + tt * 512:C_PT + (tt + 1) * 512]
                p.op("pe", lambda e, wb=wbank, s=g.sel(e_), r=rhs_w: e.matmul(
                    wb, s, r, start=True, stop=True), inc="pe")
                for hc in range(HC4):
                    bank = pA[hc % 2]
                    p.wait("pe", "act", hist_gelu.get(hc % 2, 0))
                    for fc in range(FC):
                        rhs = g.xln[:, fc * 4096 + tt * 512:
                                    fc * 4096 + (tt + 1) * 512]
                        p.op("pe", lambda e, b=bank, w=W1(fc, hc), r=rhs, fc=fc:
                             e.matmul(b, w, r, start=(fc == 0), stop=(fc == 7)),
                             inc="pe" if fc == 7 else None)
                    pe_a = p.cnt["pe"]
                    p.wait("act", "pe", pe_a)
                    p.wait("act", "dve", hist_hw.get(hc % 2, 0))
                    b1c = g.b1col(e_, q8 * 4 + hc)
                    p.op("act", lambda e, b=bank, gt=gtmp(hc % 2), b1c=b1c:
                         e.activation(gt, b, AF.Gelu_apprx_tanh, bias=b1c,
                                      scale=1.0), inc="act")
                    hist_gelu[hc % 2] = p.cnt["act"]
                    p.wait("dve", "act")
                    if stepc - 2 in pe_b_stop:
                        p.wait("dve", "pe", pe_b_stop[stepc - 2])
                    dst = hw_cols(par, hc)
                    p.op("dve", lambda e, d=dst, gt=gtmp(hc % 2), wb=wbank:
                         e.tensor_tensor(d, gt, wb, ALU.mult), inc="dve")
                    hist_hw[hc % 2] = p.cnt["dve"]
                hist_wb[stepc] = p.cnt["dve"]
                if pending is not None:
                    emit_w2_for(*pending)
                pending = (par, W2, hist_wb[stepc], tt, u, stepc)
                stepc += 1
        emit_w2_for(*pending)
        pending = None
        for tt in range(NTT):
            for fc in range(FC):
                bank = pW[(tt * FC + fc) % 2]
                p.wait("pe", "dve")
                b2l = g.bufC[0:8, C_B2SB + fc * 128:C_B2SB + (fc + 1) * 128]
                rhs_w = g.bufC[0:8, C_PT + tt * 512:C_PT + (tt + 1) * 512]
                p.op("pe", lambda e, b=bank, l=b2l, r=rhs_w: e.matmul(
                    b, l, r, start=True, stop=True), inc="pe")
                p.wait("dve", "pe")
                xs = g.xres[:, fc * 4096 + tt * 512:fc * 4096 + (tt + 1) * 512]
                p.op("dve", lambda e, x=xs, b=bank: e.tensor_tensor(
                    x, x, b, ALU.add), inc="dve")
        p.wait("syncq", "dve")


def emit_attn(g, seq):
    nc, p, h = g.nc, g.p, g.h
    s = seq
    o = 1 - seq
    unib = g.uni[:, :]
    lorB = lambda oc: unib[0:16, oc * 128:(oc + 1) * 128]
    lorA = lambda fc: unib[:, 1024 + fc * 16:1024 + (fc + 1) * 16]
    xcol = lambda fc, t0, w=512: g.xln[:, fc * 4096 + s * SEQ + t0:
                                       fc * 4096 + s * SEQ + t0 + w]
    kcol = lambda fc, t0, w=512: g.xln[:, fc * 4096 + o * SEQ + t0:
                                       fc * 4096 + o * SEQ + t0 + w]
    wstg = g.bufC[:, C_PB:C_PB + 1024]
    WT = lambda kc, oc: g.bufB[:, (kc * 8 + oc) * 128:(kc * 8 + oc + 1) * 128]
    u_sb = g.bufC[0:16, C_USB:C_USB + 512]
    vst = g.bufC[:, C_VST:C_VST + 512]
    PT = lambda kc: g.bufC[:, C_PT + kc * 256:C_PT + (kc + 1) * 256]
    OT = lambda c: g.bufC[:, C_OT + c * 256:C_OT + (c + 1) * 256]
    Pb = lambda sl: g.bufC[:, C_PB + sl * 2048:C_PB + (sl + 1) * 2048]
    Vtm = lambda c, kc: g.bufA[:, (c * NKC + kc) * 128:(c * NKC + kc + 1) * 128]

    p.entry_guard("pe")
    p.entry_guard("syncq")
    with ExitStack() as ps:
        pP = [ps.enter_context(nc.psum_tensor(_u(f"aP{i}"), [128, 512], F32))[:, :]
              for i in range(8)]

        def load_W(name):
            for ocg in range(8):
                for hlf in range(2):
                    p.wait("syncq", "pool")
                    src = h[name][ocg * 128:(ocg + 1) * 128,
                                  hlf * 512:(hlf + 1) * 512]
                    p.op("syncq", lambda e, sr=src, sl=hlf: e.dma_start(
                        g.land(sl), sr), inc="dmaw", incn=16)
                    p.wait("pool", "dmaw")
                    p.wait("pool", "pe")
                    dst = wstg[:, hlf * 512:(hlf + 1) * 512]
                    p.op("pool", lambda e, d=dst, sl=hlf: e.tensor_copy(
                        d, g.land(sl)), inc="pool")
                p.wait("pe", "pool")
                for fcg in range(2):
                    bank = pP[6 + fcg]
                    p.wait("pe", "dve")
                    for j in range(4):
                        fc = fcg * 4 + j
                        p.op("pe", lambda e, b=bank, j=j, fc=fc: e.transpose(
                            b.bitcast(BF16)[:, j * 128:(j + 1) * 128],
                            wstg[:, fc * 128:(fc + 1) * 128], g.ident()),
                             inc="pe")
                    p.wait("dve", "pe")
                    for j in range(4):
                        fc = fcg * 4 + j
                        p.op("dve", lambda e, b=bank, j=j, d=WT(fc, ocg):
                             e.tensor_copy(d, b.bitcast(BF16)[:, j * 128:(j + 1) * 128]),
                             inc="dve")

        def load_lora(wn_a, wn_b):
            p.wait("syncq", "pool")
            p.wait("syncq", "pe")
            p.op("syncq", lambda e: e.dma_start(g.fpD[0:16, 0:1024], h[wn_a]),
                 inc="dmaw", incn=16)
            p.wait("pool", "dmaw")
            p.wait("pool", "pe")
            astg = wstg[0:16, 0:1024]
            p.op("pool", lambda e: e.tensor_copy(astg, g.fpD[0:16, 0:1024]),
                 inc="pool")
            p.wait("pe", "pool")
            bank = pP[6]
            p.wait("pe", "dve")
            for fc in range(8):
                p.op("pe", lambda e, b=bank, fc=fc: e.transpose(
                    b.bitcast(BF16)[:, fc * 16:(fc + 1) * 16],
                    astg[:, fc * 128:(fc + 1) * 128], g.ident()[0:16, 0:16]),
                     inc="pe")
            p.wait("dve", "pe")
            p.op("dve", lambda e, b=bank: e.tensor_copy(
                unib[:, 1024:1152], b.bitcast(BF16)[:, 0:128]), inc="dve")
            p.wait("syncq", "pool")
            land3 = g.land(0).rearrange("p (c r) -> p c r", c=8)[:, :, 0:16]
            p.op("syncq", lambda e, l3=land3: e.dma_start(
                l3, h[wn_b].rearrange("(c p) r -> p c r", c=8)),
                 inc="dmaw", incn=16)
            p.wait("pool", "dmaw")
            p.wait("pool", "pe")
            bstg = wstg[:, 0:128]
            p.op("pool", lambda e, l3=land3: e.tensor_copy(
                bstg.rearrange("p (c r) -> p c r", c=8), l3), inc="pool")
            p.wait("pe", "pool")
            bank2 = pP[7]
            p.wait("pe", "dve")
            for oc in range(8):
                p.op("pe", lambda e, b=bank2, oc=oc: e.transpose(
                    b.bitcast(BF16)[0:16, oc * 128:(oc + 1) * 128],
                    bstg[:, oc * 16:(oc + 1) * 16], g.ident()), inc="pe")
            p.wait("dve", "pe")
            p.op("dve", lambda e, b=bank2: e.tensor_copy(
                unib[0:16, 0:1024], b.bitcast(BF16)[0:16, 0:1024]), inc="dve")

        def proj(kind):
            wn = {"k": "Wk", "v": "Wv", "q": "Wq"}[kind]
            load_W(wn)
            load_lora({"k": "Ak", "v": "Av", "q": "Aq"}[kind],
                      {"k": "Bk", "v": "Bv", "q": "Bq"}[kind])
            for tt in range(4):
                t0 = tt * 512
                bank_u = pP[0]
                p.wait("pe", "act")
                for fc in range(FC):
                    p.op("pe", lambda e, b=bank_u, w=lorA(fc), r=xcol(fc, t0),
                         fc=fc: e.matmul(b[0:16, :], w, r, start=(fc == 0),
                                         stop=(fc == 7)),
                         inc="pe" if fc == 7 else None)
                p.wait("act", "pe")
                p.op("act", lambda e, b=bank_u: e.activation(
                    u_sb, b[0:16, :], AF.Copy, scale=1.0 / RANK), inc="act")
                uc = p.cnt["act"]
                if kind == "q":
                    for oc in range(FC):
                        bank = pP[oc]
                        if oc == 0:
                            p.wait("pe", "act", uc)
                        for fc in range(FC):
                            p.op("pe", lambda e, b=bank, w=WT(fc, oc),
                                 r=xcol(fc, t0), fc=fc: e.matmul(
                                     b, w, r, start=(fc == 0), stop=False))
                        p.op("pe", lambda e, b=bank, w=lorB(oc): e.matmul(
                            b, w, u_sb, start=False, stop=True), inc="pe")
                    p.wait("act", "pe")
                    for oc in range(FC):
                        dst = xcol(oc, t0)
                        p.op("act", lambda e, b=pP[oc], d=dst: e.activation(
                            d, b, AF.Copy, scale=0.125), inc="act")
                else:
                    for oc in range(FC):
                        bank = pP[1 + (oc % 2)]
                        p.wait("pe", "act")
                        if kind == "v":
                            p.wait("pe", "dve")
                        for fc in range(FC):
                            p.op("pe", lambda e, b=bank, w=WT(fc, oc),
                                 r=xcol(fc, t0), fc=fc: e.matmul(
                                     b, w, r, start=(fc == 0), stop=False))
                        p.wait("pe", "act", uc)
                        p.op("pe", lambda e, b=bank, w=lorB(oc): e.matmul(
                            b, w, u_sb, start=False, stop=True), inc="pe")
                        p.wait("act", "pe")
                        if kind == "k":
                            dst = kcol(oc, t0)
                            p.op("act", lambda e, b=bank, d=dst: e.activation(
                                d, b, AF.Copy, scale=1.0), inc="act")
                        else:
                            p.wait("act", "pe")
                            p.op("act", lambda e, b=bank: e.activation(
                                vst, b, AF.Copy, scale=1.0), inc="act")
                            bankT = pP[3 + (oc % 2)]
                            p.wait("pe", "act")
                            p.wait("pe", "dve")
                            for j in range(4):
                                p.op("pe", lambda e, b=bankT, j=j: e.transpose(
                                    b.bitcast(BF16)[:, j * 128:(j + 1) * 128],
                                    vst[:, j * 128:(j + 1) * 128], g.ident()),
                                     inc="pe")
                            p.wait("dve", "pe")
                            for j in range(4):
                                kc = tt * 4 + j
                                p.op("dve", lambda e, b=bankT, j=j,
                                     d=Vtm(oc, kc): e.tensor_copy(
                                         d, b.bitcast(BF16)[:, j * 128:(j + 1) * 128]),
                                     inc="dve")

        proj("k")
        proj("v")
        proj("q")
        load_lora("Ao", "Bo")
        load_W("Wo")
        p.wait("syncq", "dve")
        p.wait("syncq", "act")

    p.entry_guard("pe")
    with nc.psum_tensor(_u("aSA"), [128, 2048], F32) as SAh, \
         nc.psum_tensor(_u("aSB"), [128, 2048], F32) as SBh:
        SA = SAh[:, :]
        SB = SBh[:, :]
        Ssets = (SA, SB)
        ssum = lambda si: g.stats(18 + si)
        rsv = lambda si: g.stats(20 + si)
        set_free_exp = [0, 0]
        set_pack_evac = [0, 0]
        norm_done = [0, 0]
        a1pe = {}
        pv_evac = 0
        op_add = [0, 0]

        def phaseA1(k, c, j, qt, q0g):
            si = qt
            St = Ssets[si]
            q0 = q0g + qt * 128
            prow = j * 64
            p.wait("pe", "act", max(set_free_exp[si], set_pack_evac[si]))
            p.wait("pe", "dve", max(pv_evac, op_add[0], op_add[1],
                                    norm_done[si]))
            lhs = g.xln[prow:prow + 64,
                        c * 4096 + s * SEQ + q0:
                        c * 4096 + s * SEQ + q0 + 128]
            for kb in range(4):
                rhs = g.xln[prow:prow + 64,
                            c * 4096 + o * SEQ + kb * 512:
                            c * 4096 + o * SEQ + (kb + 1) * 512]
                p.op("pe", lambda e, St=St, kb=kb, l=lhs, r=rhs:
                     e.matmul(St[:, kb * 512:(kb + 1) * 512], l, r,
                              start=True, stop=True), inc="pe")
            a1pe[k] = p.cnt["pe"]

        def phaseA2(k, qt):
            si = qt
            St = Ssets[si]
            p.wait("act", "pe", a1pe[k])
            p.op("act", lambda e, St=St, si=si: e.activation(
                St[:, :], St[:, :], AF.Exp, scale=1.0,
                accum_out=ssum(si)), inc="act")
            set_free_exp[si] = p.cnt["act"]
            p.wait("dve", "act", p.cnt["act"])
            p.op("dve", lambda e, si=si: e.reciprocal(rsv(si), ssum(si)),
                 inc="dve")
            p.wait("dve", "dve")
            p.op("dve", lambda e, si=si, St=St: e.tensor_scalar(
                Pb(si), St[:, :], rsv(si), None, ALU.mult, ALU.bypass),
                 inc="dve")
            norm_done[si] = p.cnt["dve"]

        def phaseB(k, c, j, qt):
            si = qt
            St = Ssets[si]
            p.wait("pe", "dve", norm_done[si])
            for pk in range(4):
                for jj in range(4):
                    kc = pk * 4 + jj
                    p.op("pe", lambda e, St=St, pk=pk, jj=jj,
                         kc=kc, si=si: e.transpose(
                             St.bitcast(BF16)[
                                 :, pk * 1024 + jj * 128:
                                 pk * 1024 + (jj + 1) * 128],
                             Pb(si)[:, kc * 128:(kc + 1) * 128],
                             g.ident()),
                         inc="pe" if jj == 3 else None)
                p.wait("act", "pe")
                base = C_PT + pk * 4 * 256
                dst = g.bufC[:, base:base + 1024].rearrange(
                    "p (k v) -> p k v", v=256)[:, :, qt * 128:
                                               qt * 128 + 128]
                srcv = St.bitcast(BF16)[:, pk * 1024:pk * 1024 + 512]
                p.op("act", lambda e, d=dst, sv=srcv: e.copy(
                    d, sv.rearrange("p (k v) -> p k v", v=128)),
                     inc="act")
            set_pack_evac[si] = p.cnt["act"]
            if qt == 1:
                p.wait("pe", "act", set_pack_evac[1])
                p.wait("pe", "dve", pv_evac)
                for kc in range(NKC):
                    p.op("pe", lambda e, kc=kc, c=c, j=j: e.matmul(
                        SB[j * 64:(j + 1) * 64, 0:256],
                        Vtm(c, kc)[:, j * 64:(j + 1) * 64],
                        PT(kc), start=(kc == 0), stop=(kc == 15)),
                         inc="pe" if kc == 15 else None)
                p.wait("dve", "pe")
                dsto = OT(c)[j * 64:(j + 1) * 64, :]
                p.op("dve", lambda e, d=dsto, j=j: e.tensor_copy(
                    d, SB[j * 64:(j + 1) * 64, 0:256]), inc="dve")
                nonlocal_pv[0] = p.cnt["dve"]

        nonlocal_pv = [0]

        for qg in range(NQG):
            q0g = qg * QG
            steps = [(c, j, qt)
                     for c in range(PAIRS) for j in (0, 1) for qt in (0, 1)]
            for k, (c, j, qt) in enumerate(steps):
                if k >= 1:
                    phaseA2(k - 1, steps[k - 1][2])
                if k >= 2:
                    phaseB(k - 2, *steps[k - 2])
                    pv_evac = nonlocal_pv[0]
                phaseA1(k, c, j, qt, q0g)
            phaseA2(31, steps[31][2])
            phaseB(30, *steps[30])
            pv_evac = nonlocal_pv[0]
            phaseB(31, *steps[31])
            pv_evac = nonlocal_pv[0]
            bank_u = SB
            p.wait("pe", "act", max(set_free_exp[1], set_pack_evac[1]))
            p.wait("pe", "dve", pv_evac)
            for pc in range(PAIRS):
                p.op("pe", lambda e, pc=pc: e.matmul(
                    bank_u[0:16, 0:256], lorA(pc), OT(pc),
                    start=(pc == 0), stop=(pc == 7)),
                     inc="pe" if pc == 7 else None)
            p.wait("act", "pe")
            p.op("act", lambda e: e.activation(
                u_sb[:, 0:256], bank_u[0:16, 0:256], AF.Copy, scale=1.0 / RANK),
                 inc="act")
            uc2 = p.cnt["act"]
            for oc in range(FC):
                off = (oc % 2) * 512
                if oc < 2:
                    if oc == 0:
                        p.wait("pe", "act", max(set_free_exp[0],
                                                set_pack_evac[0], uc2))
                    p.wait("pe", "dve", pv_evac)
                else:
                    p.wait("pe", "dve", op_add[oc % 2])
                for pc in range(PAIRS):
                    p.op("pe", lambda e, oc=oc, pc=pc, off=off: e.matmul(
                        SA[:, off:off + 256], WT(pc, oc), OT(pc),
                        start=(pc == 0), stop=False))
                p.op("pe", lambda e, oc=oc, off=off: e.matmul(
                    SA[:, off:off + 256], lorB(oc), u_sb[:, 0:256],
                    start=False, stop=True), inc="pe")
                p.wait("dve", "pe")
                p.wait("dve", "dve")
                bocol = g.stats(32 + oc)
                p.op("dve", lambda e, oc=oc, off=off, bc=bocol: e.tensor_scalar(
                    vst[:, 0:256], SA[:, off:off + 256], bc, None,
                    ALU.add, ALU.bypass), inc="dve")
                p.wait("dve", "dve")
                xs = g.xres[:, oc * 4096 + s * SEQ + q0g:
                            oc * 4096 + s * SEQ + q0g + 256]
                p.op("dve", lambda e, x=xs: e.tensor_tensor(
                    x, x, vst[:, 0:256], ALU.add), inc="dve")
                op_add[oc % 2] = p.cnt["dve"]
        p.wait("syncq", "dve")


def emit_debug_dump(g, sbuf, name):
    p = g.p
    p.entry_guard("syncq")
    p.op("syncq", lambda e: e.dma_start(g.h[name], sbuf[:, :]),
         inc="dmaio", incn=16)
    p.wait("syncq", "dmaio")


def emit_store(g):
    nc, p, h = g.nc, g.p, g.h
    p.entry_guard("pe")
    with nc.psum_tensor(_u("spk0"), [128, 512], BF16) as pk0, \
         nc.psum_tensor(_u("spk1"), [128, 512], BF16) as pk1:
        pks = (pk0, pk1)
        for tc in range(NTC):
            t0 = tc * 128
            for k in range(2):
                pk = pks[k]
                p.wait("pe", "act")
                for j in range(4):
                    fc = k * 4 + j
                    src = g.xres[:, fc * 4096 + t0:fc * 4096 + t0 + 128]
                    p.op("pe", lambda e, pk=pk, j=j, srt=src: e.transpose(
                        pk[:, j * 128:(j + 1) * 128], srt, g.ident()), inc="pe")
                p.wait("act", "pe")
                p.wait("act", "st")
                dst = g.fpD[:, k * 512:(k + 1) * 512]
                p.op("act", lambda e, d=dst, pk=pk: e.copy(d, pk[:, :]),
                     inc="act")
            p.wait("syncq", "act")
            p.op("syncq", lambda e, t0=t0: e.dma_start(
                h["out"][t0:t0 + 128, :], g.fpD[:, 0:1024]), inc="st", incn=16)
        p.wait("syncq", "st")


def build_nc():
    nc = bass.Bass("TRN2", target_bir_lowering=False)
    h = {}
    h["x"] = _hbm(nc, "x", (T, D), F32)
    for w in ("Wq", "Wk", "Wv", "Wo"):
        h[w] = _hbm(nc, w, (D, D), F32)
    h["bo"] = _hbm(nc, "bo", (1, D), F32)
    for a in ("Aq", "Ak", "Av", "Ao"):
        h[a] = _hbm(nc, a, (RANK, D), F32)
    for b in ("Bq", "Bk", "Bv", "Bo"):
        h[b] = _hbm(nc, b, (D, RANK), F32)
    h["wr"] = _hbm(nc, "wr", (D, E), F32)
    h["w1"] = _hbm(nc, "w1", (E, D, 4 * D), F32)
    h["b1"] = _hbm(nc, "b1", (E, 4 * D), F32)
    h["w2"] = _hbm(nc, "w2", (E, 4 * D, D), F32)
    h["b2"] = _hbm(nc, "b2", (E, D), F32)
    h["cbf"] = _hbm(nc, "cbf", (128, 1408), BF16)
    h["cf32"] = _hbm(nc, "cf32", (8, 8), F32)
    h["out"] = _hbm(nc, "out", (T, D), F32, out=True)
    if DEBUG:
        h["dbg_xln"] = _hbm(nc, "dbg_xln", (128, 32768), BF16, out=True)
        h["dbg_x1"] = _hbm(nc, "dbg_x1", (128, 32768), BF16, out=True)
        h["dbg_x2"] = _hbm(nc, "dbg_x2", (128, 32768), BF16, out=True)
        h["dbg_wts"] = _hbm(nc, "dbg_wts", (128, 256), F32, out=True)
        h["dbg_q"] = _hbm(nc, "dbg_q", (128, 32768), BF16, out=True)
        h["dbg_vtm"] = _hbm(nc, "dbg_vtm", (128, 16384), BF16, out=True)
        h["dbg_bufc"] = _hbm(nc, "dbg_bufc", (128, CB_COLS), BF16, out=True)

    with ExitStack() as stack:
        sb = lambda name, shape, dt: stack.enter_context(
            nc.sbuf_tensor(name, shape, dt))
        bufs = dict(
            xres=sb("xres", [128, 32768], BF16),
            xln=sb("xln", [128, 32768], BF16),
            bufA=sb("bufA", [128, 16384], BF16),
            bufB=sb("bufB", [128, 8192], BF16),
            bufC=sb("bufC", [128, CB_COLS], BF16),
            fpD=sb("fpD", [128, 1088], F32),
            uni=sb("uni", [128, 1152], BF16),
            cb=sb("cb", [128, 1408], BF16),
            cf=sb("cf", [8, 8], F32),
            rW=sb("rW", [128, 64], BF16),
        )
        p = Prog(nc, stack)
        g = G(nc, p, h, **bufs)

        emit_init(g)
        p.flush_block("init")
        emit_ln(g, seqs=(0, 1), from_input=True)
        p.flush_block("ln1")
        if DEBUG:
            emit_debug_dump(g, g.xln, "dbg_xln")
            p.flush_block("dbgln")
        emit_router(g)
        p.flush_block("rt1")
        emit_ffn(g)
        p.flush_block("ffn1")
        if DEBUG:
            emit_debug_dump(g, g.xres, "dbg_x1")
            p.flush_block("dbg1")
        for s in (0, 1):
            emit_ln(g, seqs=(s,), from_input=False)
            p.flush_block(f"ln2s{s}")
            emit_attn(g, seq=s)
            p.flush_block(f"attns{s}")
            if DEBUG and s == 0:
                emit_debug_dump(g, g.xln, "dbg_q")
                emit_debug_dump(g, g.bufA, "dbg_vtm")
                emit_debug_dump(g, g.bufC, "dbg_bufc")
                p.flush_block("dbgattn")
        if DEBUG:
            emit_debug_dump(g, g.xres, "dbg_x2")
            p.flush_block("dbg2")
        emit_ln(g, seqs=(0, 1), from_input=False)
        p.flush_block("ln3")
        emit_load_biases(g)
        p.flush_block("bias2")
        emit_router(g)
        p.flush_block("rt3")
        emit_ffn(g)
        p.flush_block("ffn2")
        emit_store(g)
        p.flush_block("store")
    return nc


def _make_consts():
    cbf = np.zeros((128, 1408), dtype=ml_dtypes.bfloat16)
    cbf[:, CONST_ID:CONST_ID + 128] = np.eye(128, dtype=ml_dtypes.bfloat16)
    for e in range(8):
        cbf[e, CONST_SEL + e * 128:CONST_SEL + (e + 1) * 128] = 1.0
    cbf[0, CONST_ONES:CONST_ONES + 256] = 1.0
    cf32 = np.eye(8, dtype=np.float32)
    return cbf, cf32


_NC_CACHE = [None]


def kernel(L0_attn_x, L0_attn_Wq, L0_attn_Wk, L0_attn_Wv, L0_attn_Wo, L0_attn_bo,
           L0_attn_Aq, L0_attn_Bq, L0_attn_Ak, L0_attn_Bk, L0_attn_Av, L0_attn_Bv,
           L0_attn_Ao, L0_attn_Bo, L0_ffn_w_router, L0_ffn_w1, L0_ffn_b1,
           L0_ffn_w2, L0_ffn_b2):
    x = np.ascontiguousarray(np.asarray(L0_attn_x, dtype=np.float32))
    cbf, cf32 = _make_consts()
    f32c = lambda a: np.ascontiguousarray(np.asarray(a, dtype=np.float32))
    shared = {
        "Wq": f32c(L0_attn_Wq), "Wk": f32c(L0_attn_Wk),
        "Wv": f32c(L0_attn_Wv), "Wo": f32c(L0_attn_Wo),
        "bo": f32c(L0_attn_bo).reshape(1, D),
        "Aq": f32c(L0_attn_Aq), "Ak": f32c(L0_attn_Ak),
        "Av": f32c(L0_attn_Av), "Ao": f32c(L0_attn_Ao),
        "Bq": f32c(L0_attn_Bq), "Bk": f32c(L0_attn_Bk),
        "Bv": f32c(L0_attn_Bv), "Bo": f32c(L0_attn_Bo),
        "wr": f32c(L0_ffn_w_router),
        "w1": f32c(L0_ffn_w1), "b1": f32c(L0_ffn_b1),
        "w2": f32c(L0_ffn_w2), "b2": f32c(L0_ffn_b2),
        "cbf": cbf, "cf32": cf32,
    }
    if _NC_CACHE[0] is None:
        _NC_CACHE[0] = build_nc()
    nc = _NC_CACHE[0]
    in_maps = []
    for c in range(N_CORES):
        m = dict(shared)
        m["x"] = np.ascontiguousarray(x[2 * c:2 * c + 2].reshape(T, D))
        in_maps.append(m)
    res = run_bass_kernel_spmd(nc, in_maps, core_ids=list(range(N_CORES)))
    outs = [np.asarray(r["out"]).reshape(2, SEQ, D) for r in res.results]
    return np.concatenate(outs, axis=0).astype(np.float32)
